# version 05
# baseline (speedup 1.0000x reference)
import math
import numpy as np
import ml_dtypes
import concourse.bass as bass
import concourse.mybir as mybir
from concourse.bass_utils import run_bass_kernel_spmd

F32 = mybir.dt.float32
BF16 = mybir.dt.bfloat16
AF = mybir.ActivationFunctionType
ALU = mybir.AluOpType
AXX = mybir.AxisListType.X

D = 1024
S = 4096
NB = 4
H = 16
T = 512
NT = S // T
OWN = [[0, 3, 4, 7], [1, 2, 5, 6]]
PSPAN = [2, 4, 6, 8]
NUNIT = sum(PSPAN)
NE = 8
FC_FFN = 32
FC_MOE = 16
ENGINES = ("tensor", "vector", "scalar", "sync", "gpsimd")

DEBUG = False


class Buf:
    __slots__ = ("name", "w", "r")

    def __init__(self, name):
        self.name = name
        self.w = None
        self.r = {}


class Prog:
    def __init__(self, nc):
        self.nc = nc
        self.ops = {e: [] for e in ENGINES}
        self.cnt = {}
        self.lastw = {}
        self._stack = []
        self.handles = {}
        self.prev_dma = {}

    def sem(self, name):
        g = self.nc.semaphore(name)
        h = g.__enter__()
        self._stack.append(g)
        self.cnt[name] = 0
        self.handles[name] = h
        return name

    def _waits(self, eng, toks):
        best = {}
        for t in toks:
            if t is None:
                continue
            s, v = t
            if best.get(s, -1) < v:
                best[s] = v
        out = []
        for s, v in best.items():
            if s == eng:
                pass
            if self.lastw.get((eng, s), 0) < v:
                self.lastw[(eng, s)] = v
                out.append((s, v))
        return out

    def op(self, eng, build, reads=(), writes=(), dma_sem=None, pre_only=False):
        if dma_sem is not None:
            reads = tuple(reads) + tuple(
                b for b in self.prev_dma.get(dma_sem, ()) if b not in writes)
            self.prev_dma[dma_sem] = tuple(writes)
            sink = self.prev_dma.setdefault(("sink", dma_sem), Buf("sink" + dma_sem))
            writes = tuple(writes) + (sink,)
        toks = []
        for b in reads:
            toks.append(b.w)
        for b in writes:
            toks.append(b.w)
            toks.extend(b.r.items())
        wl = self._waits(eng, toks)
        tok = None
        if not pre_only:
            sem = dma_sem if dma_sem is not None else eng
            self.cnt[sem] += 16 if dma_sem is not None else 1
            tok = (sem, self.cnt[sem])
        need = set(s for s, _ in wl)
        if tok is not None:
            need.add(tok[0])
        hs = {n: self.handles[n] for n in need}

        def closure(e, wl=wl, build=build, tok=tok, hs=hs, dma=dma_sem is not None):
            for s, v in wl:
                e.wait_ge(hs[s], v)
            ins = build(e)
            if tok is not None:
                ins.then_inc(hs[tok[0]], 16 if dma else 1)

        self.ops[eng].append(closure)
        if not pre_only:
            for b in reads:
                if b.r.get(tok[0], -1) < tok[1]:
                    b.r[tok[0]] = tok[1]
            for b in writes:
                b.w = tok
                b.r = {}
        return tok

    def raw(self, eng, fn):
        self.ops[eng].append(fn)

    def close_sems(self):
        for g in reversed(self._stack):
            g.__exit__(None, None, None)


def build_nc(debug=False):
    nc = bass.Bass("TRN2", target_bir_lowering=False)

    dr = {}

    def din(name, shape):
        dr[name] = nc.dram_tensor(name, list(shape), F32, kind="ExternalInput")

    din("x_full", (D, S))
    din("x_slots", (D, 4 * T))
    din("ropeC", (128, S))
    din("ropeS", (128, S))
    din("ropeqC", (128, 4 * T))
    din("ropeqS", (128, 4 * T))
    din("wqT", (D, D))
    din("wkT", (D, D))
    din("wvT", (D, D))
    din("woT", (D, D))
    din("w13T", (D, FC_FFN * 256))
    din("w2T", (4096, D))
    din("wupC", (NE, D, FC_MOE * 256))
    din("wdn", (NE, 2048, D))
    din("wmod", (D, 6 * D))
    din("wgT", (D, 8))
    din("bg_row", (1, 8))
    din("c_fm", (128, 8))
    din("n1w", (128, 8))
    din("n2w", (128, 8))
    din("consts", (128, 8))
    din("cr1", (1, 8))
    din("onesf", (128, 128))
    din("identf", (128, 128))
    din("maskq", (4, 128, 8, T))
    yout = nc.dram_tensor("yout", [4, 128, 8, T], F32, kind="ExternalOutput")

    taps = {}
    if debug:
        def tap(name, shape, dt=F32):
            taps[name] = nc.dram_tensor(name, list(shape), dt, kind="ExternalOutput")
        tap("t_K", [128, 8, S], BF16)
        tap("t_VH", [128, 32, 1040], BF16)
        tap("t_h", [128, 8, T], BF16)
        tap("t_h1", [128, 8, T], BF16)
        tap("t_qh", [128, 8, T], BF16)
        tap("t_ctx", [128, 8, T], BF16)
        tap("t_y", [128, 8, T], F32)
        tap("t_x1", [128, 8, T], F32)
        tap("t_g", [128, 8, T], BF16)
        tap("t_comb", [128, 4, 8], F32)
        tap("t_mod", [128, 48], F32)

    sb = nc.alloc_sbuf_tensor
    K = sb("K", [128, 8, S], BF16)
    VH = sb("VH", [128, 32, 1040], BF16)
    A_Z = sb("A_Z", [128, 8, T], F32)
    Y_B = sb("Y_B", [128, 8, T], F32)
    B_H = sb("B_H", [128, 8, T], BF16)
    C_M = sb("C_M", [128, 8, T], BF16)
    DRW = sb("DRW", [128, 2, T], F32)
    EP = sb("EP", [128, 8, T], BF16)
    GP = sb("GP", [128, 2, T], F32)
    GAIN = sb("GAIN", [128, 6, 8], F32)
    N1W = sb("N1W", [128, 8], F32)
    N2W = sb("N2W", [128, 8], F32)
    CON = sb("CON", [128, 8], F32)
    CR1 = sb("CR1", [1, 8], F32)
    ONESB = sb("ONESB", [128, 128], BF16)
    ONESF = sb("ONESF", [128, 128], F32)
    IDENT = sb("IDENT", [128, 128], BF16)
    SH8 = sb("SH8", [8, 1024], BF16)
    RL = sb("RL", [65, T], F32)
    M4 = sb("M4", [128, 4], F32)
    M4B = sb("M4B", [128, 4], F32)
    RMX = sb("RMX", [128, 1], F32)
    MB4 = sb("MB4", [128, 1], BF16)
    MSC = sb("MSC", [1, 4], BF16)
    MBIAS = sb("MBIAS", [128, 16], F32)
    CMB = sb("CMB", [128, 4, 8], F32)
    CMBB = sb("CMBB", [128, 8], BF16)
    BGR = sb("BGR", [1, 8], BF16)
    CF = sb("CF", [128, 8], BF16)
    CF0 = sb("CF0", [128, 8], F32)
    MODS = sb("MODS", [128, 48], F32)
    RPC = sb("RPC", [128, T], F32)
    RPS = sb("RPS", [128, T], F32)
    WGT = sb("WGT", [128, 8, 8], BF16)

    ap_ = nc.alloc_psum_tensor
    PS0 = ap_("PS0", [128, 512], F32)
    PS1 = ap_("PS1", [128, 512], F32)
    PS2 = ap_("PS2", [128, 512], F32)
    PS3 = ap_("PS3", [128, 512], F32)
    PS45 = ap_("PS45", [128, 1024], F32)
    PS67 = ap_("PS67", [128, 1024], F32)

    p = Prog(nc)
    for e in ENGINES:
        p.sem(e)
    for n in ("DX", "DRP", "DRS", "DK1", "DK2", "DO0", "DO1",
              "DW0", "DW1", "DW2", "DW3", "DW4", "DW5",
              "DM0", "DM1", "DCN", "DCG"):
        p.sem(n)

    bK = Buf("K"); bVH = Buf("VH"); bA = Buf("A"); bY = Buf("Y")
    bD0a = Buf("D0a"); bD0b = Buf("D0b")
    bB = Buf("B"); bC = Buf("C"); bD1 = Buf("D1")
    bGPa = Buf("GPa"); bGPb = Buf("GPb")
    bE = [Buf(f"E{j}") for j in range(4)]
    bP = {n: Buf(n) for n in ("PS0", "PS1", "PS2", "PS3",
                              "PS45a", "PS45b", "PS67a", "PS67b")}
    bSH = Buf("SH8"); bRL = Buf("RL"); bM4 = [Buf("M4"), Buf("M4B")]; bRMX = Buf("RMX")
    bMB4 = Buf("MB4"); bCMB = Buf("CMB"); bCMBB = Buf("CMBB")
    bMSC = Buf("MSC"); bMBIAS = Buf("MBIAS")
    bMODS = Buf("MODS"); bGAIN = Buf("GAIN"); bCF = Buf("CF")
    bPT = [Buf("PT0"), Buf("PT1"), bD0a, bD0b]
    bRPC = Buf("RPC"); bRPS = Buf("RPS"); bWG = Buf("WGT")
    bconst = Buf("const")

    Yb16 = Y_B.ap().bitcast(BF16)
    GPb16 = GP.ap().bitcast(BF16)
    Bb16 = B_H
    PS45b16 = PS45.ap().bitcast(BF16)
    PS3b16 = PS3.ap().bitcast(BF16)
    qh3 = Yb16[:, :, 0:512]

    def ep_slot(j):
        return EP[:, 2 * j:2 * j + 2, :].rearrange("p a b -> p (a b)").rearrange(
            "p (c n) -> p c n", n=128)

    def ep_pair(j2):
        return EP[:, 4 * j2:4 * j2 + 4, :].rearrange("p a b -> p (a b)").rearrange(
            "p (c n) -> p c n", n=256)

    def b_half(j):
        return B_H[:, 4 * j:4 * j + 4, :].rearrange("p a b -> p (a b)").rearrange(
            "p (c n) -> p c n", n=128)

    CEPS = CON[:, 0:1]
    CK1024 = CON[:, 1:2]
    CQ8 = CON[:, 3:4]
    CONE = CON[:, 5:6]

    for nm, dst in (("consts", CON), ("cr1", CR1), ("n1w", N1W), ("n2w", N2W),
                    ("onesf", ONESF), ("c_fm", CF0)):
        p.op("sync", lambda e, nm=nm, dst=dst: e.dma_start(out=dst[:, :], in_=dr[nm].ap()),
             writes=(bconst,), dma_sem="DCN")
    for nm, dst in (("onesf", ONESB), ("identf", IDENT),
                    ("bg_row", BGR)):
        p.op("gpsimd", lambda e, nm=nm, dst=dst: e.dma_start(out=dst[:, :], in_=dr[nm].ap()),
             writes=(bconst,), dma_sem="DCG")
    p.op("gpsimd", lambda e: e.dma_start(
        out=WGT[:, :, :], in_=dr["wgT"].ap().rearrange("(c p) m -> p c m", p=128)),
        writes=(bWG,), dma_sem="DCG")

    vones = VH.ap().rearrange("p t (h x) -> p t h x", x=65)[:, :, :, 64:65]
    p.op("vector", lambda e: e.memset(vones, 1.0), writes=(bVH,))

    p.op("scalar", lambda e: e.activation(CF[:, :], CF0[:, :], AF.Silu),
         reads=(bconst,), writes=(bCF,))

    for j in range(48):
        sl = j % 2
        p.op("gpsimd", lambda e, j=j, sl=sl: e.dma_start(
            out=ep_slot(sl)[:, :, :], in_=dr["wmod"].ap().rearrange(
                "(c p) m -> p c m", p=128)[:, :, 128 * j:128 * (j + 1)]),
            writes=(bE[sl],), dma_sem=f"DW{sl}")
        for k in range(8):
            last = k == 7
            p.op("tensor", lambda e, last=last, j=j, k=k, sl=sl: e.matmul(
                PS2[:, j:j + 1], ep_slot(sl)[:, k, :], CF[:, k:k + 1],
                start=(k == 0), stop=last),
                reads=(bE[sl], bCF), writes=(bP["PS2"],), pre_only=not last)
    p.op("vector", lambda e: e.tensor_copy(MODS[:, :], PS2[:, 0:48]),
         reads=(bP["PS2"],), writes=(bMODS,))
    if debug:
        p.op("sync", lambda e: e.dma_start(out=taps["t_mod"].ap(), in_=MODS[:, :]),
             reads=(bMODS,), dma_sem="DO0")
    p.op("vector", lambda e: e.tensor_copy(GAIN[:, 0, :], MODS[:, 0:8]),
         reads=(bMODS,), writes=(bGAIN,))
    p.op("vector", lambda e: e.tensor_scalar_add(GAIN[:, 1, :], MODS[:, 8:16], 1.0),
         reads=(bMODS,), writes=(bGAIN,))
    p.op("vector", lambda e: e.tensor_copy(GAIN[:, 3, :], MODS[:, 24:32]),
         reads=(bMODS,), writes=(bGAIN,))
    p.op("vector", lambda e: e.tensor_scalar_add(GAIN[:, 4, :], MODS[:, 32:40], 1.0),
         reads=(bMODS,), writes=(bGAIN,))
    for src, dsti in ((16, 2), (40, 5)):
        p.op("scalar", lambda e, src=src: e.activation(
            GP[:, 0, 0:8], MODS[:, src:src + 8], AF.Abs),
            reads=(bMODS,), writes=(bGPa,))
        p.op("vector", lambda e: e.tensor_scalar_max(GP[:, 0, 0:8], GP[:, 0, 0:8], 0.25),
             reads=(bGPa,), writes=(bGPa,))
        p.op("scalar", lambda e, src=src: e.sign(GP[:, 1, 0:8], MODS[:, src:src + 8]),
             reads=(bMODS,), writes=(bGPb,))
        p.op("vector", lambda e, dsti=dsti: e.tensor_mul(
            GAIN[:, dsti, :], GP[:, 0, 0:8], GP[:, 1, 0:8]),
            reads=(bGPa, bGPb), writes=(bGAIN,))

    def ln_stats(srcbuf, src32, wantmean):
        if wantmean:
            for k in range(8):
                last = k == 7
                p.op("tensor", lambda e, last=last, k=k, src32=src32: e.matmul(
                    PS0[:, :], ONESF[:, :], src32[:, k, :], start=(k == 0), stop=last),
                    reads=(srcbuf, bconst), writes=(bP["PS0"],), pre_only=not last)
        p.op("scalar", lambda e, src32=src32: e.activation(C_M[:, :, :], src32, AF.Square),
             reads=(srcbuf,), writes=(bC,))
        for k in range(8):
            last = k == 7
            p.op("tensor", lambda e, last=last, k=k: e.matmul(
                PS1[:, :], ONESB[:, :], C_M[:, k, :], start=(k == 0), stop=last),
                reads=(bC, bconst), writes=(bP["PS1"],), pre_only=not last)
        if wantmean:
            pass
            p.op("vector", lambda e: e.tensor_scalar_mul(DRW[:, 1, :], PS0[:, :], 1.0 / 1024.0),
                 reads=(bP["PS0"],), writes=(bD1,))
            p.op("vector", lambda e: e.tensor_mul(GP[:, 0, :], DRW[:, 1, :], DRW[:, 1, :]),
                 reads=(bD1,), writes=(bGPa,))
            p.op("vector", lambda e: e.scalar_tensor_tensor(
                GP[:, 0, :], PS1[:, :], 1.0 / 1024.0, GP[:, 0, :],
                op0=ALU.mult, op1=ALU.subtract),
                reads=(bP["PS1"], bGPa), writes=(bGPa,))
            p.op("scalar", lambda e: e.activation(GP[:, 0, :], GP[:, 0, :], AF.Sqrt,
                                                  bias=CEPS, scale=CONE),
                 reads=(bGPa, bconst), writes=(bGPa,))
            p.op("vector", lambda e: e.reciprocal(DRW[:, 0, :], GP[:, 0, :]),
                 reads=(bGPa,), writes=(bD0a, bD0b))
            p.op("vector", lambda e: e.tensor_mul(DRW[:, 1, :], DRW[:, 1, :], DRW[:, 0, :]),
                 reads=(bD0a, bD0b, bD1), writes=(bD1,))
        else:
            p.op("scalar", lambda e: e.activation(GP[:, 0, :], PS1[:, :], AF.Sqrt,
                                                  bias=CEPS, scale=CK1024),
                 reads=(bP["PS1"], bconst), writes=(bGPa,))
            p.op("vector", lambda e: e.reciprocal(DRW[:, 0, :], GP[:, 0, :]),
                 reads=(bGPa,), writes=(bD0a, bD0b))

    def ln_apply(srcbuf, src32, dstbuf, dst16, grow, srow):
        for c in range(8):
            p.op("vector", lambda e, c=c, src32=src32: e.tensor_mul(
                GP[:, 0, :], src32[:, c, :], DRW[:, 0, :]),
                reads=(srcbuf, bD0a, bD0b), writes=(bGPa,))
            p.op("vector", lambda e: e.tensor_sub(GP[:, 0, :], GP[:, 0, :], DRW[:, 1, :]),
                 reads=(bGPa, bD1), writes=(bGPa,))
            p.op("scalar", lambda e, c=c, dst16=dst16, grow=grow, srow=srow: e.activation(
                dst16[:, c, :], GP[:, 0, :], AF.Identity,
                bias=GAIN[:, srow, c:c + 1], scale=GAIN[:, grow, c:c + 1]),
                reads=(bGPa, bGAIN), writes=(dstbuf,))

    def rms_apply(srcbuf, src, dstbuf, dst16, wten):
        for c in range(8):
            p.op("vector", lambda e, c=c, src=src: e.tensor_mul(
                GP[:, 0, :], src[:, c, :], DRW[:, 0, :]),
                reads=(srcbuf, bD0a, bD0b), writes=(bGPa,))
            p.op("scalar", lambda e, c=c, dst16=dst16, wten=wten: e.activation(
                dst16[:, c, :], GP[:, 0, :], AF.Identity, bias=0.0,
                scale=wten[:, c:c + 1]),
                reads=(bGPa, bconst), writes=(dstbuf,))

    def wstream(slot, dsemi, wname, col0, ncol):
        p.op("gpsimd", lambda e, slot=slot, wname=wname, col0=col0, ncol=ncol: e.dma_start(
            out=ep_slot(slot)[:, :, 0:ncol] if ncol <= 128 else ep_pair(slot // 2)[:, :, 0:ncol],
            in_=dr[wname].ap().rearrange("(c p) m -> p c m", p=128)[:, :, col0:col0 + ncol]),
            writes=(bE[slot],) if ncol <= 128 else (bE[slot], bE[slot + 1]),
            dma_sem=f"DW{dsemi}")

    def proj_group(ps_ap, psbuf, lhs3, rhsbuf, rhs3, extra=()):
        for k in range(8):
            last = k == 7
            p.op("tensor", lambda e, last=last, k=k, ps_ap=ps_ap, lhs3=lhs3, rhs3=rhs3: e.matmul(
                ps_ap, lhs3[:, k, :], rhs3[:, k, :], start=(k == 0), stop=last),
                reads=tuple(extra) + (rhsbuf,), writes=(psbuf,), pre_only=not last)

    def rope_pass(ps_ap, psbuf, dst_ap, dstbuf):
        p.op("vector", lambda e, ps_ap=ps_ap: e.tensor_mul(GP[:, 0, :], ps_ap, RPC[:, :]),
             reads=(psbuf, bRPC), writes=(bGPa,))
        for i, (o, s) in enumerate(((0, 32), (32, 0), (64, 96), (96, 64))):
            p.op("vector", lambda e, o=o, s=s, ps_ap=ps_ap: e.tensor_mul(
                GP[o:o + 32, 1, :], ps_ap[s:s + 32, :], RPS[o:o + 32, :]),
                reads=(psbuf, bRPS), writes=(bGPb,), pre_only=(i != 3))
        p.op("vector", lambda e, dst_ap=dst_ap: e.tensor_add(dst_ap, GP[:, 0, :], GP[:, 1, :]),
             reads=(bGPa, bGPb), writes=(dstbuf,))

    for j in range(NT):
        xsrc = dr["x_full"].ap().rearrange("(c p) s -> p c s", p=128)[:, :, T * j:T * (j + 1)]
        p.op("sync", lambda e, xsrc=xsrc: e.dma_start(out=Y_B[:, :, :], in_=xsrc),
             writes=(bY,), dma_sem="DX")
        p.op("sync", lambda e, j=j: e.dma_start(
            out=RPC[:, :], in_=dr["ropeC"].ap()[:, T * j:T * (j + 1)]),
            writes=(bRPC,), dma_sem="DRP")
        p.op("sync", lambda e, j=j: e.dma_start(
            out=RPS[:, :], in_=dr["ropeS"].ap()[:, T * j:T * (j + 1)]),
            writes=(bRPS,), dma_sem="DRS")
        ln_stats(bY, Y_B[:, :, :], wantmean=True)
        ln_apply(bY, Y_B[:, :, :], bB, B_H[:, :, :], grow=1, srow=0)
        ln_stats(bB, B_H[:, :, :], wantmean=False)
        rms_apply(bB, B_H[:, :, :], bC, C_M[:, :, :], N1W)
        for c2 in range(4):
            j2 = c2 % 2
            p.op("gpsimd", lambda e, c2=c2, j2=j2: e.dma_start(
                out=ep_pair(j2)[:, :, :],
                in_=dr["wkT"].ap().rearrange("(c p) m -> p c m", p=128)[
                    :, :, 256 * c2:256 * (c2 + 1)]),
                writes=(bE[2 * j2], bE[2 * j2 + 1]), dma_sem=f"DW{j2}")
            for ci in range(2):
                c = 2 * c2 + ci
                psn, psv = ("PS2", PS2[:, :]) if c % 2 == 0 else ("PS3", PS3[:, :])
                lhs = ep_pair(j2)[:, :, 128 * ci:128 * (ci + 1)]
                proj_group(psv, bP[psn], lhs, bC, C_M[:, :, :],
                           extra=(bE[2 * j2], bE[2 * j2 + 1]))
                rope_pass(psv, bP[psn], K[:, c, T * j:T * (j + 1)], bK)
        for half in range(2):
            p.op("gpsimd", lambda e, half=half: e.dma_start(
                out=EP[:, :, :],
                in_=dr["wvT"].ap().rearrange("(c p) m -> p c m", p=128)[
                    :, :, 512 * half:512 * (half + 1)]),
                writes=tuple(bE), dma_sem="DM0")
            for s4 in range(4):
                psn = "PS45a" if s4 % 2 == 0 else "PS67a"
                psv = (PS45 if s4 % 2 == 0 else PS67)[:, 0:512]
                for k in range(8):
                    last = k == 7
                    p.op("tensor", lambda e, last=last, k=k, s4=s4, psv=psv: e.matmul(
                        psv, C_M[:, k, 128 * s4:128 * (s4 + 1)], EP[:, k, :],
                        start=(k == 0), stop=last),
                        reads=(bC,) + tuple(bE), writes=(bP[psn],), pre_only=not last)
                vdst = VH[:, 4 * j + s4, :].rearrange("p (h x) -> p h x", x=65)[
                    :, 8 * half:8 * (half + 1), 0:64]
                p.op("vector", lambda e, psv=psv, vdst=vdst: e.tensor_copy(
                    vdst, psv.rearrange("p (h x) -> p h x", x=64)),
                     reads=(bP[psn],), writes=(bVH,))

    if debug:
        p.op("sync", lambda e: e.dma_start(out=taps["t_K"].ap(), in_=K[:, :, :]),
             reads=(bK,), dma_sem="DO0")
        p.op("sync", lambda e: e.dma_start(out=taps["t_VH"].ap(), in_=VH[:, :, :]),
             reads=(bVH,), dma_sem="DO0")

    unit_base = [sum(PSPAN[:u]) for u in range(4)]

    for u in range(4):
        span = PSPAN[u]
        nkc = 4 * span
        xsrc = dr["x_slots"].ap().rearrange("(c p) s -> p c s", p=128)[:, :, T * u:T * (u + 1)]
        p.op("sync", lambda e, xsrc=xsrc: e.dma_start(out=Y_B[:, :, :], in_=xsrc),
             writes=(bY,), dma_sem="DX")
        p.op("sync", lambda e, u=u: e.dma_start(
            out=RPC[:, :], in_=dr["ropeqC"].ap()[:, T * u:T * (u + 1)]),
            writes=(bRPC,), dma_sem="DRP")
        p.op("sync", lambda e, u=u: e.dma_start(
            out=RPS[:, :], in_=dr["ropeqS"].ap()[:, T * u:T * (u + 1)]),
            writes=(bRPS,), dma_sem="DRS")
        ln_stats(bY, Y_B[:, :, :], wantmean=True)
        ln_apply(bY, Y_B[:, :, :], bB, B_H[:, :, :], grow=1, srow=0)
        for c in range(8):
            p.op("vector", lambda e, c=c: e.scalar_tensor_tensor(
                A_Z[:, c, :], B_H[:, c, :], GAIN[:, 2, c:c + 1], Y_B[:, c, :],
                op0=ALU.mult, op1=ALU.add),
                reads=(bB, bGAIN, bY), writes=(bA,))
        if debug and u == 0:
            p.op("sync", lambda e: e.dma_start(out=taps["t_h"].ap(), in_=B_H[:, :, :]),
                 reads=(bB,), dma_sem="DO0")
        ln_stats(bB, B_H[:, :, :], wantmean=False)
        rms_apply(bB, B_H[:, :, :], bC, C_M[:, :, :], N1W)
        if debug and u == 0:
            p.op("sync", lambda e: e.dma_start(out=taps["t_h1"].ap(), in_=C_M[:, :, :]),
                 reads=(bC,), dma_sem="DO0")
        for c2 in range(4):
            j2 = c2 % 2
            p.op("gpsimd", lambda e, c2=c2, j2=j2: e.dma_start(
                out=ep_pair(j2)[:, :, :],
                in_=dr["wqT"].ap().rearrange("(c p) m -> p c m", p=128)[
                    :, :, 256 * c2:256 * (c2 + 1)]),
                writes=(bE[2 * j2], bE[2 * j2 + 1]), dma_sem=f"DW{j2}")
            for ci in range(2):
                c = 2 * c2 + ci
                psn, psv = ("PS2", PS2[:, :]) if c % 2 == 0 else ("PS3", PS3[:, :])
                lhs = ep_pair(j2)[:, :, 128 * ci:128 * (ci + 1)]
                proj_group(psv, bP[psn], lhs, bC, C_M[:, :, :],
                           extra=(bE[2 * j2], bE[2 * j2 + 1]))
                rope_pass(psv, bP[psn], qh3[:, c, :], bY)
        if debug and u == 0:
            p.op("sync", lambda e: e.dma_start(out=taps["t_qh"].ap(), in_=qh3[:, :, :]),
                 reads=(bY,), dma_sem="DO0")
        p.op("gpsimd", lambda e, u=u: e.dma_start(
            out=EP[:, 0:4, :], in_=dr["maskq"].ap()[u, :, 0:4, :]),
            writes=(bE[0], bE[1]), dma_sem="DM0")
        p.op("gpsimd", lambda e, u=u: e.dma_start(
            out=EP[:, 4:8, :], in_=dr["maskq"].ap()[u, :, 4:8, :]),
            writes=(bE[2], bE[3]), dma_sem="DM1")

        PS0b16 = PS0.ap().bitcast(BF16)
        M4s = [M4, M4B]

        def emit_bias_chain(h):
            p.op("vector", lambda e: e.reduce_max(RMX[:, :], M4s[h % 2][:, :], axis=AXX),
                 reads=(bM4[h % 2],), writes=(bRMX,))
            p.op("vector", lambda e: e.tensor_scalar(
                MB4[:, 0:1], RMX[:, 0:1], 0.0, 1.0, op0=ALU.max, op1=ALU.add),
                reads=(bRMX,), writes=(bMB4,))
            p.op("tensor", lambda e: e.transpose(
                PS0b16[0:1, 0:128], MB4[:, 0:1], IDENT[:, :]),
                reads=(bMB4, bconst), writes=(bP["PS0"],))
            p.op("vector", lambda e: e.reduce_max(MSC[0:1, 0:1], PS0b16[0:1, 0:128], axis=AXX),
                 reads=(bP["PS0"],), writes=(bMSC,))
            p.op("vector", lambda e: e.tensor_single_scalar(
                MSC[0:1, 1:2], MSC[0:1, 0:1], -0.125, op=ALU.mult),
                reads=(bMSC,), writes=(bMSC,))
            p.op("tensor", lambda e: e.matmul(
                PS1[:, 0:1], ONESB[0:1, 0:128], MSC[0:1, 1:2], start=True, stop=True),
                reads=(bMSC, bconst), writes=(bP["PS1"],))
            p.op("vector", lambda e, h=h: e.tensor_copy(MBIAS[:, h:h + 1], PS1[:, 0:1]),
                 reads=(bP["PS1"],), writes=(bMBIAS,))

        for h in range(H):
            hc = h // 2
            r0 = (h % 2) * 64
            for qs in range(4):
                for pr in range(span // 2):
                    pspair = PS45 if pr % 2 == 0 else PS67
                    pn = ("PS45a", "PS45b") if pr % 2 == 0 else ("PS67a", "PS67b")
                    for hf in range(2):
                        blk = 2 * pr + hf
                        last = hf == 1
                        p.op("tensor", lambda e, last=last, qs=qs, blk=blk, hf=hf,
                             hc=hc, r0=r0, pspair=pspair: e.matmul(
                            pspair[:, 512 * hf:512 * (hf + 1)],
                            qh3[r0:r0 + 64, hc, 128 * qs:128 * (qs + 1)],
                            K[r0:r0 + 64, hc, 512 * blk:512 * (blk + 1)],
                            start=True, stop=True, tile_position=(r0, 0)),
                            reads=(bY, bK), writes=(bP[pn[0]], bP[pn[1]]),
                            pre_only=not last)
                    p.op("vector", lambda e, pspair=pspair: e.reduce_max(
                        RMX[:, :], pspair[:, :], axis=AXX),
                         reads=(bP[pn[0]], bP[pn[1]]), writes=(bRMX,))
                    if pr == 0:
                        p.op("vector", lambda e, qs=qs, h=h: e.tensor_copy(
                            M4s[h % 2][:, qs:qs + 1], RMX[:, :]),
                             reads=(bRMX,), writes=(bM4[h % 2],))
                    else:
                        p.op("vector", lambda e, qs=qs, h=h: e.tensor_max(
                            M4s[h % 2][:, qs:qs + 1], M4s[h % 2][:, qs:qs + 1], RMX[:, :]),
                            reads=(bRMX, bM4[h % 2]), writes=(bM4[h % 2],))
            if h > 0:
                emit_bias_chain(h - 1)
        emit_bias_chain(H - 1)
        bring = [("PS67a", PS67[:, 0:512]), ("PS67b", PS67[:, 512:1024]),
                 ("PS45a", PS45[:, 0:512]), ("PS45b", PS45[:, 512:1024])]
        pring = [GPb16[:, 0, 0:512], GPb16[:, 0, 512:1024],
                 DRW.ap().bitcast(BF16)[:, 0, 0:512],
                 DRW.ap().bitcast(BF16)[:, 0, 512:1024]]

        def emit_B(h, kc):
            hc = h // 2
            r0 = (h % 2) * 64
            bn, bap = bring[kc % 4]
            p.op("tensor", lambda e, kc=kc, bap=bap, hc=hc, r0=r0: e.matmul(
                bap, K[r0:r0 + 64, hc, 128 * kc:128 * (kc + 1)],
                qh3[r0:r0 + 64, hc, :], start=True, stop=True,
                tile_position=(r0, 0)),
                reads=(bK, bY), writes=(bP[bn],))
            pslot = kc % 4
            pT = pring[pslot]
            pbuf = bPT[pslot]
            p.op("scalar", lambda e, bap=bap, pT=pT, h=h: e.activation(
                pT, bap, AF.Exp, bias=MBIAS[:, h:h + 1], scale=CQ8),
                reads=(bP[bn], bconst, bMBIAS), writes=(pbuf,))
            j8 = kc - (nkc - 8)
            if j8 >= 0:
                p.op("vector", lambda e, pT=pT, j8=j8: e.tensor_mul(
                    pT, pT, EP[:, j8, :]),
                    reads=tuple(bE) + (pbuf,), writes=(pbuf,))

        def emit_PV(h, kc):
            ctxn = "PS2" if h % 2 == 0 else "PS3"
            ctxps = PS2 if h % 2 == 0 else PS3
            pslot = kc % 4
            pT = pring[pslot]
            pbuf = bPT[pslot]
            last = kc == nkc - 1
            p.op("tensor", lambda e, kc=kc, h=h, pT=pT, ctxps=ctxps, nkc=nkc: e.matmul(
                ctxps[0:65, :], VH[:, kc, 65 * h:65 * (h + 1)], pT,
                start=(kc == 0), stop=(kc == nkc - 1)),
                reads=(bVH, pbuf), writes=(bP[ctxn],), pre_only=not last)

        def emit_evac(h):
            hc = h // 2
            ctxn = "PS2" if h % 2 == 0 else "PS3"
            ctxps = PS2 if h % 2 == 0 else PS3
            p.op("vector", lambda e, ctxps=ctxps: e.reciprocal(RL[64:65, :], ctxps[64:65, :]),
                 reads=(bP[ctxn],), writes=(bRL,))
            rbn = "PS0" if h % 2 == 0 else "PS1"
            rbp = PS0 if h % 2 == 0 else PS1
            p.op("tensor", lambda e, rbp=rbp: e.matmul(
                rbp[0:64, :], ONESF[64:65, 0:64], RL[64:65, :], start=True, stop=True,
                tile_position=(64, 0)),
                reads=(bRL, bconst), writes=(bP[rbn],))
            p.op("vector", lambda e, rbp=rbp: e.tensor_copy(DRW[0:64, 1, :], rbp[0:64, :]),
                 reads=(bP[rbn],), writes=(bD1,))
            if h % 2 == 0:
                p.op("vector", lambda e, ctxps=ctxps, hc=hc: e.tensor_mul(
                    C_M[0:64, hc, :], ctxps[0:64, :], DRW[0:64, 1, :]),
                    reads=(bP[ctxn], bD1), writes=(bC,))
            else:
                p.op("vector", lambda e, ctxps=ctxps: e.tensor_mul(
                    GPb16[0:64, 1, 0:512], ctxps[0:64, :], DRW[0:64, 1, :]),
                    reads=(bP[ctxn], bD1), writes=(bGPb,))
                p.op("sync", lambda e, hc=hc: e.dma_start(
                    out=C_M[64:128, hc, :], in_=GPb16[0:64, 1, 0:512]),
                    reads=(bGPb,), writes=(bC,), dma_sem="DK1")

        for h in range(H):
            emit_B(h, 0)
            if nkc > 1:
                emit_B(h, 1)
            for kc in range(nkc):
                if kc + 2 < nkc:
                    emit_B(h, kc + 2)
                emit_PV(h, kc)
                if kc == 0 and h > 0:
                    emit_evac(h - 1)
        emit_evac(H - 1)
        if debug and u == 0:
            p.op("sync", lambda e: e.dma_start(out=taps["t_ctx"].ap(), in_=C_M[:, :, :]),
                 reads=(bC,), dma_sem="DO0")

        for c2 in range(4):
            j2 = c2 % 2
            p.op("gpsimd", lambda e, c2=c2, j2=j2: e.dma_start(
                out=ep_pair(j2)[:, :, :],
                in_=dr["woT"].ap().rearrange("(c p) m -> p c m", p=128)[
                    :, :, 256 * c2:256 * (c2 + 1)]),
                writes=(bE[2 * j2], bE[2 * j2 + 1]), dma_sem=f"DW{j2}")
            for ci in range(2):
                c = 2 * c2 + ci
                psn = "PS45a" if c % 2 == 0 else "PS45b"
                psv = PS45[:, 0:512] if c % 2 == 0 else PS45[:, 512:1024]
                lhs = ep_pair(j2)[:, :, 128 * ci:128 * (ci + 1)]
                proj_group(psv, bP[psn], lhs, bC, C_M[:, :, :],
                           extra=(bE[2 * j2], bE[2 * j2 + 1]))
                p.op("vector", lambda e, c=c, psv=psv: e.tensor_add(
                    Y_B[:, c, :], psv, B_H[:, c, :]),
                    reads=(bP[psn], bB), writes=(bY,))
                p.op("vector", lambda e, c=c, psv=psv: e.scalar_tensor_tensor(
                    A_Z[:, c, :], psv, GAIN[:, 2, c:c + 1], A_Z[:, c, :],
                    op0=ALU.mult, op1=ALU.add),
                    reads=(bP[psn], bGAIN, bA), writes=(bA,))
        if debug and u == 0:
            p.op("sync", lambda e: e.dma_start(out=taps["t_y"].ap(), in_=Y_B[:, :, :]),
                 reads=(bY,), dma_sem="DO0")

        ln_stats(bY, Y_B[:, :, :], wantmean=False)
        rms_apply(bY, Y_B[:, :, :], bC, C_M[:, :, :], N2W)
        for halfF in range(2):
            for fc16 in range(FC_MOE):
                fc = 16 * halfF + fc16
                j2 = fc % 2
                if fc16 % 2 == 0:
                    fb = (fc // 2) % 2
                    dst = EP[:, :, :] if fb == 0 else B_H[:, :, :]
                    wbufs = tuple(bE) if fb == 0 else (bB,)
                    p.op("gpsimd", lambda e, fc=fc, dst=dst: e.dma_start(
                        out=dst,
                        in_=dr["w13T"].ap().rearrange("(c p) m -> p c m", p=128)[
                            :, :, 256 * fc:256 * (fc + 2)]),
                        writes=wbufs, dma_sem=f"DW{fb}")
                fb = (fc // 2) % 2
                wt = (EP if fb == 0 else B_H).ap().rearrange(
                    "p a b -> p (a b)").rearrange("p (c n) -> p c n", n=512)
                w13 = wt[:, :, 256 * (fc % 2):256 * (fc % 2) + 256]
                wbufs13 = tuple(bE) if fb == 0 else (bB,)
                pu = "PS45a" if fc % 2 == 0 else "PS67a"
                pg = "PS45b" if fc % 2 == 0 else "PS67b"
                pub = (PS45 if fc % 2 == 0 else PS67)[:, 0:512]
                pgb = (PS45 if fc % 2 == 0 else PS67)[:, 512:1024]
                for k in range(8):
                    last = k == 7
                    p.op("tensor", lambda e, last=last, k=k, w13=w13, pub=pub: e.matmul(
                        pub, w13[:, k, 0:128], C_M[:, k, :], start=(k == 0), stop=last),
                        reads=wbufs13 + (bC,), writes=(bP[pu],),
                        pre_only=not last)
                for k in range(8):
                    last = k == 7
                    p.op("tensor", lambda e, last=last, k=k, w13=w13, pgb=pgb: e.matmul(
                        pgb, w13[:, k, 128:256], C_M[:, k, :], start=(k == 0), stop=last),
                        reads=wbufs13 + (bC,), writes=(bP[pg],),
                        pre_only=not last)
                slb = GPb16[:, 0, 0:512]
                p.op("scalar", lambda e, pub=pub, slb=slb: e.activation(slb, pub, AF.Silu),
                     reads=(bP[pu],), writes=(bGPa,))
                ffdst = Yb16[:, fc16 % 8, 512 * (fc16 // 8):512 * (fc16 // 8 + 1)]
                p.op("vector", lambda e, ffdst=ffdst, slb=slb, pgb=pgb: e.tensor_mul(
                    ffdst, slb, pgb),
                    reads=(bGPa, bP[pg]), writes=(bY,))
            for c in range(8):
                if c % 2 == 0:
                    cb2 = (c // 2) % 2
                    dstw = (EP if cb2 == 0 else B_H).ap().rearrange(
                        "p a b -> p (a b)").rearrange("p (f n) -> p f n", n=256)[:, 0:16, :]
                    wbuf2 = tuple(bE) if cb2 == 0 else (bB,)
                    p.op("gpsimd", lambda e, c=c, halfF=halfF, dstw=dstw: e.dma_start(
                        out=dstw,
                        in_=dr["w2T"].ap().rearrange("(f p) m -> p f m", p=128)[
                            :, 16 * halfF:16 * (halfF + 1), 128 * c:128 * (c + 2)]),
                        writes=wbuf2, dma_sem=f"DW{cb2}")
                cb2 = (c // 2) % 2
                wv = (EP if cb2 == 0 else B_H).ap().rearrange(
                    "p a b -> p (a b)").rearrange("p (f n) -> p f n", n=256)[
                    :, 0:16, 128 * (c % 2):128 * (c % 2) + 128]
                wrd = tuple(bE) if cb2 == 0 else (bB,)
                psn = "PS2" if c % 2 == 0 else "PS3"
                psv = (PS2 if c % 2 == 0 else PS3)[:, :]
                for fc16 in range(FC_MOE):
                    last = fc16 == FC_MOE - 1
                    ffsrc = Yb16[:, fc16 % 8, 512 * (fc16 // 8):512 * (fc16 // 8 + 1)]
                    p.op("tensor", lambda e, last=last, fc16=fc16, wv=wv, psv=psv, ffsrc=ffsrc: e.matmul(
                        psv, wv[:, fc16, :], ffsrc, start=(fc16 == 0), stop=last),
                        reads=wrd + (bY,), writes=(bP[psn],), pre_only=not last)
                p.op("vector", lambda e, c=c, psv=psv: e.scalar_tensor_tensor(
                    A_Z[:, c, :], psv, GAIN[:, 2, c:c + 1], A_Z[:, c, :],
                    op0=ALU.mult, op1=ALU.add),
                    reads=(bP[psn], bGAIN, bA), writes=(bA,))
        if debug and u == 0:
            p.op("sync", lambda e: e.dma_start(out=taps["t_x1"].ap(), in_=A_Z[:, :, :]),
                 reads=(bA,), dma_sem="DO0")

        ln_stats(bA, A_Z[:, :, :], wantmean=True)
        ln_apply(bA, A_Z[:, :, :], bB, B_H[:, :, :], grow=4, srow=3)
        if debug and u == 0:
            p.op("sync", lambda e: e.dma_start(out=taps["t_g"].ap(), in_=B_H[:, :, :]),
                 reads=(bB,), dma_sem="DO0")
        for qs in range(4):
            gout = PS2[:, 8 * qs:8 * (qs + 1)]
            for k in range(8):
                p.op("tensor", lambda e, k=k, qs=qs, gout=gout: e.matmul(
                    gout, B_H[:, k, 128 * qs:128 * (qs + 1)], WGT[:, k, :],
                    start=(k == 0), stop=False),
                    reads=(bB, bWG), writes=(bP["PS2"],), pre_only=True)
            p.op("tensor", lambda e, qs=qs, gout=gout: e.matmul(
                gout, ONESB[0:1, 0:128], BGR[0:1, :], start=False, stop=True),
                reads=(bB, bWG, bconst), writes=(bP["PS2"],))
            p.op("vector", lambda e, gout=gout: e.reduce_max(
                RMX[:, :], gout, axis=AXX, negate=True),
                reads=(bP["PS2"],), writes=(bRMX,))
            p.op("vector", lambda e, gout=gout: e.tensor_scalar(
                GP[:, 0, 0:8], gout, RMX[:, 0:1], 0.0, op0=ALU.add, op1=ALU.is_ge),
                reads=(bP["PS2"], bRMX), writes=(bGPa,))
            p.op("vector", lambda e, gout=gout: e.scalar_tensor_tensor(
                GP[:, 1, 0:8], GP[:, 0, 0:8], -1e30, gout,
                op0=ALU.mult, op1=ALU.add),
                reads=(bGPa, bP["PS2"]), writes=(bGPb,))
            p.op("vector", lambda e: e.reduce_max(
                M4[:, 0:1], GP[:, 1, 0:8], axis=AXX, negate=True),
                reads=(bGPb,), writes=(bM4[0],))
            p.op("scalar", lambda e, gout=gout: e.activation(
                GP[:, 0, 0:8], gout, AF.Exp, bias=RMX[:, 0:1], scale=1.0),
                reads=(bP["PS2"], bRMX), writes=(bGPa,))
            p.op("vector", lambda e: e.tensor_sub(M4[:, 1:2], M4[:, 0:1], RMX[:, 0:1]),
                 reads=(bM4[0], bRMX), writes=(bM4[0],))
            p.op("scalar", lambda e: e.activation(
                M4[:, 2:3], M4[:, 1:2], AF.Exp, bias=0.0, scale=CON[:, 2:3]),
                reads=(bM4[0], bconst), writes=(bM4[0],))
            p.op("vector", lambda e: e.tensor_scalar_add(M4[:, 2:3], M4[:, 2:3], 1.0),
                 reads=(bM4[0],), writes=(bM4[0],))
            p.op("vector", lambda e: e.reciprocal(M4[:, 3:4], M4[:, 2:3]),
                 reads=(bM4[0],), writes=(bM4[0],))
            p.op("vector", lambda e, gout=gout: e.tensor_scalar(
                GP[:, 1, 0:8], gout, M4[:, 0:1], 0.0, op0=ALU.add, op1=ALU.is_ge),
                reads=(bP["PS2"], bM4[0]), writes=(bGPb,))
            p.op("vector", lambda e, qs=qs: e.scalar_tensor_tensor(
                CMB[:, qs, :], GP[:, 0, 0:8], M4[:, 3:4], GP[:, 1, 0:8],
                op0=ALU.mult, op1=ALU.mult),
                reads=(bGPa, bGPb, bM4[0]), writes=(bCMB,))
        if debug and u == 0:
            p.op("sync", lambda e: e.dma_start(out=taps["t_comb"].ap(), in_=CMB[:, :, :]),
                 reads=(bCMB,), dma_sem="DO0")
        for qs in range(4):
            p.op("vector", lambda e, qs=qs: e.tensor_copy(CMBB[:, :], CMB[:, qs, :]),
                 reads=(bCMB,), writes=(bCMBB,))
            p.op("tensor", lambda e, qs=qs: e.transpose(
                PS3b16[0:8, 128 * qs:128 * (qs + 1)], CMBB[:, :], IDENT[:, :]),
                reads=(bCMBB, bconst), writes=(bP["PS3"],))
        p.op("vector", lambda e: e.tensor_copy(SH8[0:8, 0:512], PS3b16[0:8, 0:512]),
             reads=(bP["PS3"],), writes=(bSH,))

        for ex in range(NE):
            p.op("sync", lambda e, ex=ex: e.dma_start(
                out=SH8[0:1, 512:1024], in_=SH8[ex:ex + 1, 0:512]),
                reads=(bSH,), writes=(bSH,), dma_sem="DK2")
            cbn = "PS0" if ex % 2 == 0 else "PS1"
            cbp = PS0 if ex % 2 == 0 else PS1
            p.op("tensor", lambda e, cbp=cbp: e.matmul(
                cbp[:, :], ONESB[0:1, :], SH8[0:1, 512:1024], start=True, stop=True),
                reads=(bSH, bconst), writes=(bP[cbn],))
            for fc in range(FC_MOE):
                if fc % 2 == 0:
                    fb = (fc // 2) % 2
                    dstw = EP[:, :, :] if fb == 0 else C_M[:, :, :]
                    wbufs = tuple(bE) if fb == 0 else (bC,)
                    p.op("gpsimd", lambda e, ex=ex, fc=fc, dstw=dstw: e.dma_start(
                        out=dstw,
                        in_=dr["wupC"].ap()[ex].rearrange("(c p) m -> p c m", p=128)[
                            :, :, 256 * fc:256 * (fc + 2)]),
                        writes=wbufs, dma_sem=f"DW{fb}")
                fb = (fc // 2) % 2
                wtm = (EP if fb == 0 else C_M).ap().rearrange(
                    "p a b -> p (a b)").rearrange("p (c n) -> p c n", n=512)
                wup = wtm[:, :, 256 * (fc % 2):256 * (fc % 2) + 256]
                wbufsu = tuple(bE) if fb == 0 else (bC,)
                pu = "PS45a" if fc % 2 == 0 else "PS67a"
                pg = "PS45b" if fc % 2 == 0 else "PS67b"
                pub = (PS45 if fc % 2 == 0 else PS67)[:, 0:512]
                pgb = (PS45 if fc % 2 == 0 else PS67)[:, 512:1024]
                for k in range(8):
                    last = k == 7
                    p.op("tensor", lambda e, last=last, k=k, wup=wup, pub=pub: e.matmul(
                        pub, wup[:, k, 0:128], B_H[:, k, :], start=(k == 0), stop=last),
                        reads=wbufsu + (bB,), writes=(bP[pu],),
                        pre_only=not last)
                for k in range(8):
                    last = k == 7
                    p.op("tensor", lambda e, last=last, k=k, wup=wup, pgb=pgb: e.matmul(
                        pgb, wup[:, k, 128:256], B_H[:, k, :], start=(k == 0), stop=last),
                        reads=wbufsu + (bB,), writes=(bP[pg],),
                        pre_only=not last)
                slb = GPb16[:, 0, 0:512]
                p.op("scalar", lambda e, pgb=pgb, slb=slb: e.activation(slb, pgb, AF.Silu),
                     reads=(bP[pg],), writes=(bGPa,))
                p.op("vector", lambda e, slb=slb, pub=pub: e.tensor_mul(
                    GP[:, 1, :], slb, pub),
                    reads=(bGPa, bP[pu]), writes=(bGPb,))
                hhdst = Yb16[:, fc % 8, 512 * (fc // 8):512 * (fc // 8 + 1)]
                p.op("vector", lambda e, hhdst=hhdst, cbp=cbp: e.tensor_mul(
                    hhdst, GP[:, 1, :], cbp[:, :]),
                    reads=(bGPb, bP[cbn]), writes=(bY,))
            for c in range(8):
                if c % 2 == 0:
                    cb2 = (c // 2) % 2
                    dstw = (EP if cb2 == 0 else C_M).ap().rearrange(
                        "p a b -> p (a b)").rearrange("p (f n) -> p f n", n=256)
                    wbufsd = tuple(bE) if cb2 == 0 else (bC,)
                    p.op("gpsimd", lambda e, ex=ex, c=c, dstw=dstw: e.dma_start(
                        out=dstw,
                        in_=dr["wdn"].ap()[ex].rearrange("(f p) m -> p f m", p=128)[
                            :, :, 128 * c:128 * (c + 2)]),
                        writes=wbufsd, dma_sem=f"DM{cb2}")
                cb2 = (c // 2) % 2
                wv = (EP if cb2 == 0 else C_M).ap().rearrange(
                    "p a b -> p (a b)").rearrange("p (f n) -> p f n", n=256)[
                    :, :, 128 * (c % 2):128 * (c % 2) + 128]
                wrd = tuple(bE) if cb2 == 0 else (bC,)
                psn = "PS2" if c % 2 == 0 else "PS3"
                psv = (PS2 if c % 2 == 0 else PS3)[:, :]
                for fc in range(FC_MOE):
                    last = fc == FC_MOE - 1
                    hhsrc = Yb16[:, fc % 8, 512 * (fc // 8):512 * (fc // 8 + 1)]
                    p.op("tensor", lambda e, last=last, fc=fc, wv=wv, psv=psv, hhsrc=hhsrc: e.matmul(
                        psv, wv[:, fc, :], hhsrc, start=(fc == 0), stop=last),
                        reads=wrd + (bY,), writes=(bP[psn],), pre_only=not last)
                p.op("vector", lambda e, c=c, psv=psv: e.scalar_tensor_tensor(
                    A_Z[:, c, :], psv, GAIN[:, 5, c:c + 1], A_Z[:, c, :],
                    op0=ALU.mult, op1=ALU.add),
                    reads=(bP[psn], bGAIN, bA), writes=(bA,))

        dsem = "DO0" if u % 2 == 0 else "DO1"
        p.op("sync", lambda e, u=u: e.dma_start(out=yout.ap()[u], in_=A_Z[:, :, :]),
             reads=(bA,), dma_sem=dsem)

    totals = [(n, p.cnt[n]) for n in ("DO0", "DO1") if p.cnt[n] > 0]
    def tail(e, totals=totals, hs={n: p.handles[n] for n, _ in totals}):
        for n, v in totals:
            e.wait_ge(hs[n], v)
    p.raw("sync", tail)

    with nc.Block() as block:
        for eng in ENGINES:
            def runner(e, lst=p.ops[eng]):
                for f in lst:
                    f(e)
            getattr(block, eng)(runner)
    p.close_sems()
    return nc


def _rope_tables():
    pos = np.arange(S, dtype=np.float64)
    kk = np.arange(64) // 2
    ang = pos[None, :] / np.power(10000.0, 2.0 * kk.astype(np.float64) / 64.0)[:, None]
    cos = np.cos(ang)
    sinr = np.sin(ang)
    CT = np.empty((64, S))
    ST = np.empty((64, S))
    for n in range(64):
        m = n % 32
        CT[n] = cos[2 * m]
        ST[n] = sinr[2 * m] * (-1.0 if n < 32 else 1.0)
    CT2 = np.concatenate([CT, CT], axis=0).astype(np.float32)
    ST2 = np.concatenate([ST, ST], axis=0).astype(np.float32)
    return np.ascontiguousarray(CT2), np.ascontiguousarray(ST2)


def _deinterleave_perm():
    perm = np.empty(D, dtype=np.int64)
    for h in range(H):
        base = 64 * h
        perm[base:base + 32] = base + 2 * np.arange(32)
        perm[base + 32:base + 64] = base + 2 * np.arange(32) + 1
    return perm


def _fm(v):
    return np.ascontiguousarray(v.reshape(8, 128).T.astype(np.float32))


def stage_inputs(inputs):
    x = np.asarray(inputs["L0_attn_x"], np.float32)
    perm = _deinterleave_perm()
    CT2, ST2 = _rope_tables()
    wq = np.asarray(inputs["L0_attn_wq"], np.float32)[perm, :]
    wk = np.asarray(inputs["L0_attn_wk"], np.float32)[perm, :]
    wqT = np.ascontiguousarray(wq.T)
    wkT = np.ascontiguousarray(wk.T)
    wvT = np.ascontiguousarray(np.asarray(inputs["L0_attn_wv"], np.float32).T)
    woT = np.ascontiguousarray(np.asarray(inputs["L0_attn_wo"], np.float32).T)
    w1T = np.asarray(inputs["L0_attn_w1"], np.float32).T
    w3T = np.asarray(inputs["L0_attn_w3"], np.float32).T
    w13 = np.empty((D, FC_FFN * 256), np.float32)
    for fc in range(FC_FFN):
        w13[:, 256 * fc:256 * fc + 128] = w1T[:, 128 * fc:128 * (fc + 1)]
        w13[:, 256 * fc + 128:256 * (fc + 1)] = w3T[:, 128 * fc:128 * (fc + 1)]
    w2T = np.ascontiguousarray(np.asarray(inputs["L0_attn_w2"], np.float32).T)
    wup = np.asarray(inputs["L0_ffn_Wup"], np.float32)
    wupC = np.empty((NE, D, FC_MOE * 256), np.float32)
    for fc in range(FC_MOE):
        wupC[:, :, 256 * fc:256 * fc + 128] = wup[:, :, 128 * fc:128 * (fc + 1)]
        wupC[:, :, 256 * fc + 128:256 * (fc + 1)] = wup[:, :, 2048 + 128 * fc:2048 + 128 * (fc + 1)]
    wdn = np.ascontiguousarray(np.asarray(inputs["L0_ffn_Wdown"], np.float32))
    wmod = np.ascontiguousarray(np.asarray(inputs["L0_wmod"], np.float32))
    wgT = np.ascontiguousarray(np.asarray(inputs["L0_ffn_Wg"], np.float32).T)
    bg_row = np.asarray(inputs["L0_ffn_bg"], np.float32).reshape(1, 8)
    n1w = _fm(np.asarray(inputs["L0_attn_norm1_w"], np.float32))
    n2w = _fm(np.asarray(inputs["L0_attn_norm2_w"], np.float32))
    consts = np.broadcast_to(np.array(
        [1e-5, 1.0 / 1024.0, -1.0, 0.125, 0.25, 1.0, -1e30, 0.0], np.float32), (128, 8)).copy()
    cr1 = np.array([[-1.0, 0, 0, 0, 0, 0, 0, 0]], np.float32)
    onesf = np.ones((128, 128), np.float32)
    identf = np.eye(128, dtype=np.float32)

    def maskq_for(par):
        m = np.ones((4, 128, 8, T), np.float32)
        for u in range(4):
            span = PSPAN[u]
            tspan = OWN[par][u] + 1
            nkc = 4 * span
            for j8 in range(8):
                kc = nkc - 8 + j8
                blk = kc // 4
                if blk > tspan - 1:
                    m[u, :, j8, :] = 0.0
                    continue
                if blk < tspan - 1:
                    continue
                off = (kc % 4) * 128
                kpos = off + np.arange(128)[:, None]
                qpos = np.arange(T)[None, :]
                m[u, :, j8, :] = (kpos <= qpos).astype(np.float32)
        return m

    maskq = [maskq_for(0), maskq_for(1)]

    in_maps = []
    for core in range(8):
        b = core // 2
        par = core % 2
        xT = np.ascontiguousarray(x[b].T)
        xs = np.concatenate([xT[:, T * t:T * (t + 1)] for t in OWN[par]], axis=1)
        rqC = np.concatenate([CT2[:, T * t:T * (t + 1)] for t in OWN[par]], axis=1)
        rqS = np.concatenate([ST2[:, T * t:T * (t + 1)] for t in OWN[par]], axis=1)
        in_maps.append({
            "x_full": xT, "x_slots": np.ascontiguousarray(xs),
            "ropeC": CT2, "ropeS": ST2,
            "ropeqC": np.ascontiguousarray(rqC), "ropeqS": np.ascontiguousarray(rqS),
            "wqT": wqT, "wkT": wkT, "wvT": wvT, "woT": woT,
            "w13T": w13, "w2T": w2T, "wupC": wupC, "wdn": wdn,
            "wmod": wmod, "wgT": wgT, "bg_row": bg_row,
            "c_fm": _fm(np.asarray(inputs["L0_c"], np.float32)[b]),
            "n1w": n1w, "n2w": n2w, "consts": consts, "cr1": cr1,
            "onesf": onesf, "identf": identf, "maskq": maskq[par],
        })
    return in_maps


def gather_output(results):
    out = np.empty((NB, S, D), np.float32)
    for core in range(8):
        b = core // 2
        par = core % 2
        y = results[core]["yout"]
        for u, t in enumerate(OWN[par]):
            out[b, T * t:T * (t + 1), :] = y[u].transpose(2, 1, 0).reshape(T, D)
    return out


def kernel(**inputs):
    in_maps = stage_inputs(inputs)
    nc = build_nc(debug=DEBUG)
    res = run_bass_kernel_spmd(nc, in_maps, core_ids=list(range(8)))
    return gather_output(res.results)


# version 06
# speedup vs baseline: 1.0718x; 1.0718x over previous
import math
import numpy as np
import ml_dtypes
import concourse.bass as bass
import concourse.mybir as mybir
from concourse.bass_utils import run_bass_kernel_spmd

F32 = mybir.dt.float32
BF16 = mybir.dt.bfloat16
AF = mybir.ActivationFunctionType
ALU = mybir.AluOpType
AXX = mybir.AxisListType.X

D = 1024
S = 4096
NB = 4
H = 16
T = 512
NT = S // T
OWN = [[0, 3, 4, 7], [1, 2, 5, 6]]
PSPAN = [2, 4, 6, 8]
NUNIT = sum(PSPAN)
NE = 8
FC_FFN = 32
FC_MOE = 16
ENGINES = ("tensor", "vector", "scalar", "sync", "gpsimd")

DEBUG = False


class Buf:
    __slots__ = ("name", "w", "r")

    def __init__(self, name):
        self.name = name
        self.w = None
        self.r = {}


class Prog:
    def __init__(self, nc):
        self.nc = nc
        self.ops = {e: [] for e in ENGINES}
        self.cnt = {}
        self.lastw = {}
        self._stack = []
        self.handles = {}
        self.prev_dma = {}

    def sem(self, name):
        g = self.nc.semaphore(name)
        h = g.__enter__()
        self._stack.append(g)
        self.cnt[name] = 0
        self.handles[name] = h
        return name

    def _waits(self, eng, toks):
        best = {}
        for t in toks:
            if t is None:
                continue
            s, v = t
            if best.get(s, -1) < v:
                best[s] = v
        out = []
        for s, v in best.items():
            if s == eng:
                pass
            if self.lastw.get((eng, s), 0) < v:
                self.lastw[(eng, s)] = v
                out.append((s, v))
        return out

    def op(self, eng, build, reads=(), writes=(), dma_sem=None, pre_only=False):
        if dma_sem is not None:
            reads = tuple(reads) + tuple(
                b for b in self.prev_dma.get(dma_sem, ()) if b not in writes)
            self.prev_dma[dma_sem] = tuple(writes)
            sink = self.prev_dma.setdefault(("sink", dma_sem), Buf("sink" + dma_sem))
            writes = tuple(writes) + (sink,)
        toks = []
        for b in reads:
            toks.append(b.w)
        for b in writes:
            toks.append(b.w)
            toks.extend(b.r.items())
        wl = self._waits(eng, toks)
        tok = None
        if not pre_only:
            sem = dma_sem if dma_sem is not None else eng
            self.cnt[sem] += 16 if dma_sem is not None else 1
            tok = (sem, self.cnt[sem])
        need = set(s for s, _ in wl)
        if tok is not None:
            need.add(tok[0])
        hs = {n: self.handles[n] for n in need}

        def closure(e, wl=wl, build=build, tok=tok, hs=hs, dma=dma_sem is not None):
            for s, v in wl:
                e.wait_ge(hs[s], v)
            ins = build(e)
            if tok is not None:
                ins.then_inc(hs[tok[0]], 16 if dma else 1)

        self.ops[eng].append(closure)
        if not pre_only:
            for b in reads:
                if b.r.get(tok[0], -1) < tok[1]:
                    b.r[tok[0]] = tok[1]
            for b in writes:
                b.w = tok
                b.r = {}
        return tok

    def raw(self, eng, fn):
        self.ops[eng].append(fn)

    def close_sems(self):
        for g in reversed(self._stack):
            g.__exit__(None, None, None)


def build_nc(debug=False):
    nc = bass.Bass("TRN2", target_bir_lowering=False)

    dr = {}

    def din(name, shape):
        dr[name] = nc.dram_tensor(name, list(shape), F32, kind="ExternalInput")

    din("x_full", (D, S))
    din("x_slots", (D, 4 * T))
    din("ropeC", (128, S))
    din("ropeS", (128, S))
    din("ropeqC", (128, 4 * T))
    din("ropeqS", (128, 4 * T))
    din("wqT", (D, D))
    din("wkT", (D, D))
    din("wvT", (D, D))
    din("woT", (D, D))
    din("w13T", (D, FC_FFN * 256))
    din("w2T", (4096, D))
    din("wupC", (NE, D, FC_MOE * 256))
    din("wdn", (NE, 2048, D))
    din("wmod", (D, 6 * D))
    din("wgT", (D, 8))
    din("bg_row", (1, 8))
    din("c_fm", (128, 8))
    din("n1w", (128, 8))
    din("n2w", (128, 8))
    din("consts", (128, 8))
    din("cr1", (1, 8))
    din("onesf", (128, 128))
    din("identf", (128, 128))
    din("maskq", (4, 128, 8, T))
    yout = nc.dram_tensor("yout", [4, 128, 8, T], F32, kind="ExternalOutput")

    taps = {}
    if debug:
        def tap(name, shape, dt=F32):
            taps[name] = nc.dram_tensor(name, list(shape), dt, kind="ExternalOutput")
        tap("t_K", [128, 8, S], BF16)
        tap("t_VH", [128, 32, 1040], BF16)
        tap("t_h", [128, 8, T], BF16)
        tap("t_h1", [128, 8, T], BF16)
        tap("t_qh", [128, 8, T], BF16)
        tap("t_ctx", [128, 8, T], BF16)
        tap("t_y", [128, 8, T], F32)
        tap("t_x1", [128, 8, T], F32)
        tap("t_g", [128, 8, T], BF16)
        tap("t_comb", [128, 4, 8], F32)
        tap("t_mod", [128, 48], F32)

    sb = nc.alloc_sbuf_tensor
    K = sb("K", [128, 8, S], BF16)
    VH = sb("VH", [128, 32, 1040], BF16)
    A_Z = sb("A_Z", [128, 8, T], F32)
    Y_B = sb("Y_B", [128, 8, T], F32)
    B_H = sb("B_H", [128, 8, T], BF16)
    C_M = sb("C_M", [128, 8, T], BF16)
    DRW = sb("DRW", [128, 2, T], F32)
    EP = sb("EP", [128, 8, T], BF16)
    GP = sb("GP", [128, 2, T], F32)
    GAIN = sb("GAIN", [128, 6, 8], F32)
    N1W = sb("N1W", [128, 8], F32)
    N2W = sb("N2W", [128, 8], F32)
    CON = sb("CON", [128, 8], F32)
    CR1 = sb("CR1", [1, 8], F32)
    ONESB = sb("ONESB", [128, 128], BF16)
    ONESF = sb("ONESF", [128, 128], F32)
    IDENT = sb("IDENT", [128, 128], BF16)
    SH8 = sb("SH8", [8, 1024], BF16)
    RL = sb("RL", [65, T], F32)
    M4 = sb("M4", [128, 4], F32)
    M4B = sb("M4B", [128, 4], F32)
    RMX = sb("RMX", [128, 1], F32)
    MB4 = sb("MB4", [128, 1], BF16)
    MSC = sb("MSC", [1, 4], BF16)
    MBIAS = sb("MBIAS", [128, 16], F32)
    CMB = sb("CMB", [128, 4, 8], F32)
    CMBB = sb("CMBB", [128, 8], BF16)
    BGR = sb("BGR", [1, 8], BF16)
    CF = sb("CF", [128, 8], BF16)
    CF0 = sb("CF0", [128, 8], F32)
    MODS = sb("MODS", [128, 48], F32)
    RPC = sb("RPC", [128, T], F32)
    RPS = sb("RPS", [128, T], F32)
    WGT = sb("WGT", [128, 8, 8], BF16)

    ap_ = nc.alloc_psum_tensor
    PS0 = ap_("PS0", [128, 512], F32)
    PS1 = ap_("PS1", [128, 512], F32)
    PS2 = ap_("PS2", [128, 512], F32)
    PS3 = ap_("PS3", [128, 512], F32)
    PS45 = ap_("PS45", [128, 1024], F32)
    PS67 = ap_("PS67", [128, 1024], F32)

    p = Prog(nc)
    for e in ENGINES:
        p.sem(e)
    for n in ("DX", "DRP", "DRS", "DK1", "DK2", "DO0", "DO1",
              "DW0", "DW1", "DW2", "DW3", "DW4", "DW5",
              "DM0", "DM1", "DCN", "DCG"):
        p.sem(n)

    bK = Buf("K"); bVH = Buf("VH"); bA = Buf("A"); bY = Buf("Y")
    bD0a = Buf("D0a"); bD0b = Buf("D0b")
    bB = Buf("B"); bC = Buf("C"); bD1 = Buf("D1")
    bGPa = Buf("GPa"); bGPb = Buf("GPb")
    bE = [Buf(f"E{j}") for j in range(4)]
    bP = {n: Buf(n) for n in ("PS0", "PS1", "PS2", "PS3",
                              "PS45a", "PS45b", "PS67a", "PS67b")}
    bSH = Buf("SH8"); bRL = Buf("RL"); bM4 = [Buf("M4"), Buf("M4B")]; bRMX = Buf("RMX")
    bMB4 = Buf("MB4"); bCMB = Buf("CMB"); bCMBB = Buf("CMBB")
    bMSC = Buf("MSC"); bMBIAS = Buf("MBIAS")
    bMODS = Buf("MODS"); bGAIN = Buf("GAIN"); bCF = Buf("CF")
    bPT = [Buf("PT0"), Buf("PT1"), bD0a, bD0b]
    bRPC = Buf("RPC"); bRPS = Buf("RPS"); bWG = Buf("WGT")
    bconst = Buf("const")

    Yb16 = Y_B.ap().bitcast(BF16)
    GPb16 = GP.ap().bitcast(BF16)
    Bb16 = B_H
    PS45b16 = PS45.ap().bitcast(BF16)
    PS3b16 = PS3.ap().bitcast(BF16)
    qh3 = Yb16[:, :, 0:512]

    def ep_slot(j):
        return EP[:, 2 * j:2 * j + 2, :].rearrange("p a b -> p (a b)").rearrange(
            "p (c n) -> p c n", n=128)

    def ep_pair(j2):
        return EP[:, 4 * j2:4 * j2 + 4, :].rearrange("p a b -> p (a b)").rearrange(
            "p (c n) -> p c n", n=256)

    def b_half(j):
        return B_H[:, 4 * j:4 * j + 4, :].rearrange("p a b -> p (a b)").rearrange(
            "p (c n) -> p c n", n=128)

    CEPS = CON[:, 0:1]
    CK1024 = CON[:, 1:2]
    CQ8 = CON[:, 3:4]
    CONE = CON[:, 5:6]

    for nm, dst in (("consts", CON), ("cr1", CR1), ("n1w", N1W), ("n2w", N2W),
                    ("onesf", ONESF), ("c_fm", CF0)):
        p.op("sync", lambda e, nm=nm, dst=dst: e.dma_start(out=dst[:, :], in_=dr[nm].ap()),
             writes=(bconst,), dma_sem="DCN")
    for nm, dst in (("onesf", ONESB), ("identf", IDENT),
                    ("bg_row", BGR)):
        p.op("gpsimd", lambda e, nm=nm, dst=dst: e.dma_start(out=dst[:, :], in_=dr[nm].ap()),
             writes=(bconst,), dma_sem="DCG")
    p.op("gpsimd", lambda e: e.dma_start(
        out=WGT[:, :, :], in_=dr["wgT"].ap().rearrange("(c p) m -> p c m", p=128)),
        writes=(bWG,), dma_sem="DCG")

    vones = VH.ap().rearrange("p t (h x) -> p t h x", x=65)[:, :, :, 64:65]
    p.op("vector", lambda e: e.memset(vones, 1.0), writes=(bVH,))

    p.op("scalar", lambda e: e.activation(CF[:, :], CF0[:, :], AF.Silu),
         reads=(bconst,), writes=(bCF,))

    for j in range(48):
        sl = j % 2
        p.op("gpsimd", lambda e, j=j, sl=sl: e.dma_start(
            out=ep_slot(sl)[:, :, :], in_=dr["wmod"].ap().rearrange(
                "(c p) m -> p c m", p=128)[:, :, 128 * j:128 * (j + 1)]),
            writes=(bE[sl],), dma_sem=f"DW{sl}")
        for k in range(8):
            last = k == 7
            p.op("tensor", lambda e, last=last, j=j, k=k, sl=sl: e.matmul(
                PS2[:, j:j + 1], ep_slot(sl)[:, k, :], CF[:, k:k + 1],
                start=(k == 0), stop=last),
                reads=(bE[sl], bCF), writes=(bP["PS2"],), pre_only=not last)
    p.op("vector", lambda e: e.tensor_copy(MODS[:, :], PS2[:, 0:48]),
         reads=(bP["PS2"],), writes=(bMODS,))
    if debug:
        p.op("sync", lambda e: e.dma_start(out=taps["t_mod"].ap(), in_=MODS[:, :]),
             reads=(bMODS,), dma_sem="DO0")
    p.op("vector", lambda e: e.tensor_copy(GAIN[:, 0, :], MODS[:, 0:8]),
         reads=(bMODS,), writes=(bGAIN,))
    p.op("vector", lambda e: e.tensor_scalar_add(GAIN[:, 1, :], MODS[:, 8:16], 1.0),
         reads=(bMODS,), writes=(bGAIN,))
    p.op("vector", lambda e: e.tensor_copy(GAIN[:, 3, :], MODS[:, 24:32]),
         reads=(bMODS,), writes=(bGAIN,))
    p.op("vector", lambda e: e.tensor_scalar_add(GAIN[:, 4, :], MODS[:, 32:40], 1.0),
         reads=(bMODS,), writes=(bGAIN,))
    for src, dsti in ((16, 2), (40, 5)):
        p.op("scalar", lambda e, src=src: e.activation(
            GP[:, 0, 0:8], MODS[:, src:src + 8], AF.Abs),
            reads=(bMODS,), writes=(bGPa,))
        p.op("vector", lambda e: e.tensor_scalar_max(GP[:, 0, 0:8], GP[:, 0, 0:8], 0.25),
             reads=(bGPa,), writes=(bGPa,))
        p.op("scalar", lambda e, src=src: e.sign(GP[:, 1, 0:8], MODS[:, src:src + 8]),
             reads=(bMODS,), writes=(bGPb,))
        p.op("vector", lambda e, dsti=dsti: e.tensor_mul(
            GAIN[:, dsti, :], GP[:, 0, 0:8], GP[:, 1, 0:8]),
            reads=(bGPa, bGPb), writes=(bGAIN,))

    def ln_stats(srcbuf, src32, wantmean):
        if wantmean:
            for k in range(8):
                last = k == 7
                p.op("tensor", lambda e, last=last, k=k, src32=src32: e.matmul(
                    PS0[:, :], ONESF[:, :], src32[:, k, :], start=(k == 0), stop=last),
                    reads=(srcbuf, bconst), writes=(bP["PS0"],), pre_only=not last)
        p.op("scalar", lambda e, src32=src32: e.activation(C_M[:, :, :], src32, AF.Square),
             reads=(srcbuf,), writes=(bC,))
        for k in range(8):
            last = k == 7
            p.op("tensor", lambda e, last=last, k=k: e.matmul(
                PS1[:, :], ONESB[:, :], C_M[:, k, :], start=(k == 0), stop=last),
                reads=(bC, bconst), writes=(bP["PS1"],), pre_only=not last)
        if wantmean:
            pass
            p.op("vector", lambda e: e.tensor_scalar_mul(DRW[:, 1, :], PS0[:, :], 1.0 / 1024.0),
                 reads=(bP["PS0"],), writes=(bD1,))
            p.op("vector", lambda e: e.tensor_mul(GP[:, 0, :], DRW[:, 1, :], DRW[:, 1, :]),
                 reads=(bD1,), writes=(bGPa,))
            p.op("vector", lambda e: e.scalar_tensor_tensor(
                GP[:, 0, :], PS1[:, :], 1.0 / 1024.0, GP[:, 0, :],
                op0=ALU.mult, op1=ALU.subtract),
                reads=(bP["PS1"], bGPa), writes=(bGPa,))
            p.op("scalar", lambda e: e.activation(GP[:, 0, :], GP[:, 0, :], AF.Sqrt,
                                                  bias=CEPS, scale=CONE),
                 reads=(bGPa, bconst), writes=(bGPa,))
            p.op("vector", lambda e: e.reciprocal(DRW[:, 0, :], GP[:, 0, :]),
                 reads=(bGPa,), writes=(bD0a, bD0b))
            p.op("vector", lambda e: e.tensor_mul(DRW[:, 1, :], DRW[:, 1, :], DRW[:, 0, :]),
                 reads=(bD0a, bD0b, bD1), writes=(bD1,))
        else:
            p.op("scalar", lambda e: e.activation(GP[:, 0, :], PS1[:, :], AF.Sqrt,
                                                  bias=CEPS, scale=CK1024),
                 reads=(bP["PS1"], bconst), writes=(bGPa,))
            p.op("vector", lambda e: e.reciprocal(DRW[:, 0, :], GP[:, 0, :]),
                 reads=(bGPa,), writes=(bD0a, bD0b))

    def ln_apply(srcbuf, src32, dstbuf, dst16, grow, srow):
        for c in range(8):
            p.op("vector", lambda e, c=c, src32=src32: e.tensor_mul(
                GP[:, 0, :], src32[:, c, :], DRW[:, 0, :]),
                reads=(srcbuf, bD0a, bD0b), writes=(bGPa,))
            p.op("vector", lambda e: e.tensor_sub(GP[:, 0, :], GP[:, 0, :], DRW[:, 1, :]),
                 reads=(bGPa, bD1), writes=(bGPa,))
            p.op("scalar", lambda e, c=c, dst16=dst16, grow=grow, srow=srow: e.activation(
                dst16[:, c, :], GP[:, 0, :], AF.Identity,
                bias=GAIN[:, srow, c:c + 1], scale=GAIN[:, grow, c:c + 1]),
                reads=(bGPa, bGAIN), writes=(dstbuf,))

    def rms_apply(srcbuf, src, dstbuf, dst16, wten):
        for c in range(8):
            p.op("vector", lambda e, c=c, src=src: e.tensor_mul(
                GP[:, 0, :], src[:, c, :], DRW[:, 0, :]),
                reads=(srcbuf, bD0a, bD0b), writes=(bGPa,))
            p.op("scalar", lambda e, c=c, dst16=dst16, wten=wten: e.activation(
                dst16[:, c, :], GP[:, 0, :], AF.Identity, bias=0.0,
                scale=wten[:, c:c + 1]),
                reads=(bGPa, bconst), writes=(dstbuf,))

    def wstream(slot, dsemi, wname, col0, ncol):
        p.op("gpsimd", lambda e, slot=slot, wname=wname, col0=col0, ncol=ncol: e.dma_start(
            out=ep_slot(slot)[:, :, 0:ncol] if ncol <= 128 else ep_pair(slot // 2)[:, :, 0:ncol],
            in_=dr[wname].ap().rearrange("(c p) m -> p c m", p=128)[:, :, col0:col0 + ncol]),
            writes=(bE[slot],) if ncol <= 128 else (bE[slot], bE[slot + 1]),
            dma_sem=f"DW{dsemi}")

    def proj_group(ps_ap, psbuf, lhs3, rhsbuf, rhs3, extra=()):
        for k in range(8):
            last = k == 7
            p.op("tensor", lambda e, last=last, k=k, ps_ap=ps_ap, lhs3=lhs3, rhs3=rhs3: e.matmul(
                ps_ap, lhs3[:, k, :], rhs3[:, k, :], start=(k == 0), stop=last),
                reads=tuple(extra) + (rhsbuf,), writes=(psbuf,), pre_only=not last)

    def rope_pass(ps_ap, psbuf, dst_ap, dstbuf):
        p.op("vector", lambda e, ps_ap=ps_ap: e.tensor_mul(GP[:, 0, :], ps_ap, RPC[:, :]),
             reads=(psbuf, bRPC), writes=(bGPa,))
        for i, (o, s) in enumerate(((0, 32), (32, 0), (64, 96), (96, 64))):
            p.op("vector", lambda e, o=o, s=s, ps_ap=ps_ap: e.tensor_mul(
                GP[o:o + 32, 1, :], ps_ap[s:s + 32, :], RPS[o:o + 32, :]),
                reads=(psbuf, bRPS), writes=(bGPb,), pre_only=(i != 3))
        p.op("vector", lambda e, dst_ap=dst_ap: e.tensor_add(dst_ap, GP[:, 0, :], GP[:, 1, :]),
             reads=(bGPa, bGPb), writes=(dstbuf,))

    for j in range(NT):
        xsrc = dr["x_full"].ap().rearrange("(c p) s -> p c s", p=128)[:, :, T * j:T * (j + 1)]
        p.op("sync", lambda e, xsrc=xsrc: e.dma_start(out=Y_B[:, :, :], in_=xsrc),
             writes=(bY,), dma_sem="DX")
        p.op("sync", lambda e, j=j: e.dma_start(
            out=RPC[:, :], in_=dr["ropeC"].ap()[:, T * j:T * (j + 1)]),
            writes=(bRPC,), dma_sem="DRP")
        p.op("sync", lambda e, j=j: e.dma_start(
            out=RPS[:, :], in_=dr["ropeS"].ap()[:, T * j:T * (j + 1)]),
            writes=(bRPS,), dma_sem="DRS")
        ln_stats(bY, Y_B[:, :, :], wantmean=True)
        ln_apply(bY, Y_B[:, :, :], bB, B_H[:, :, :], grow=1, srow=0)
        ln_stats(bB, B_H[:, :, :], wantmean=False)
        rms_apply(bB, B_H[:, :, :], bC, C_M[:, :, :], N1W)
        for c2 in range(4):
            j2 = c2 % 2
            p.op("gpsimd", lambda e, c2=c2, j2=j2: e.dma_start(
                out=ep_pair(j2)[:, :, :],
                in_=dr["wkT"].ap().rearrange("(c p) m -> p c m", p=128)[
                    :, :, 256 * c2:256 * (c2 + 1)]),
                writes=(bE[2 * j2], bE[2 * j2 + 1]), dma_sem=f"DW{j2}")
            for ci in range(2):
                c = 2 * c2 + ci
                psn, psv = ("PS2", PS2[:, :]) if c % 2 == 0 else ("PS3", PS3[:, :])
                lhs = ep_pair(j2)[:, :, 128 * ci:128 * (ci + 1)]
                proj_group(psv, bP[psn], lhs, bC, C_M[:, :, :],
                           extra=(bE[2 * j2], bE[2 * j2 + 1]))
                rope_pass(psv, bP[psn], K[:, c, T * j:T * (j + 1)], bK)
        for half in range(2):
            p.op("gpsimd", lambda e, half=half: e.dma_start(
                out=EP[:, :, :],
                in_=dr["wvT"].ap().rearrange("(c p) m -> p c m", p=128)[
                    :, :, 512 * half:512 * (half + 1)]),
                writes=tuple(bE), dma_sem="DM0")
            for s4 in range(4):
                psn = "PS45a" if s4 % 2 == 0 else "PS67a"
                psv = (PS45 if s4 % 2 == 0 else PS67)[:, 0:512]
                for k in range(8):
                    last = k == 7
                    p.op("tensor", lambda e, last=last, k=k, s4=s4, psv=psv: e.matmul(
                        psv, C_M[:, k, 128 * s4:128 * (s4 + 1)], EP[:, k, :],
                        start=(k == 0), stop=last),
                        reads=(bC,) + tuple(bE), writes=(bP[psn],), pre_only=not last)
                vdst = VH[:, 4 * j + s4, :].rearrange("p (h x) -> p h x", x=65)[
                    :, 8 * half:8 * (half + 1), 0:64]
                p.op("vector", lambda e, psv=psv, vdst=vdst: e.tensor_copy(
                    vdst, psv.rearrange("p (h x) -> p h x", x=64)),
                     reads=(bP[psn],), writes=(bVH,))

    if debug:
        p.op("sync", lambda e: e.dma_start(out=taps["t_K"].ap(), in_=K[:, :, :]),
             reads=(bK,), dma_sem="DO0")
        p.op("sync", lambda e: e.dma_start(out=taps["t_VH"].ap(), in_=VH[:, :, :]),
             reads=(bVH,), dma_sem="DO0")

    unit_base = [sum(PSPAN[:u]) for u in range(4)]

    for u in range(4):
        span = PSPAN[u]
        nkc = 4 * span
        xsrc = dr["x_slots"].ap().rearrange("(c p) s -> p c s", p=128)[:, :, T * u:T * (u + 1)]
        p.op("sync", lambda e, xsrc=xsrc: e.dma_start(out=Y_B[:, :, :], in_=xsrc),
             writes=(bY,), dma_sem="DX")
        p.op("sync", lambda e, u=u: e.dma_start(
            out=RPC[:, :], in_=dr["ropeqC"].ap()[:, T * u:T * (u + 1)]),
            writes=(bRPC,), dma_sem="DRP")
        p.op("sync", lambda e, u=u: e.dma_start(
            out=RPS[:, :], in_=dr["ropeqS"].ap()[:, T * u:T * (u + 1)]),
            writes=(bRPS,), dma_sem="DRS")
        ln_stats(bY, Y_B[:, :, :], wantmean=True)
        ln_apply(bY, Y_B[:, :, :], bB, B_H[:, :, :], grow=1, srow=0)
        for c in range(8):
            p.op("vector", lambda e, c=c: e.scalar_tensor_tensor(
                A_Z[:, c, :], B_H[:, c, :], GAIN[:, 2, c:c + 1], Y_B[:, c, :],
                op0=ALU.mult, op1=ALU.add),
                reads=(bB, bGAIN, bY), writes=(bA,))
        if debug and u == 0:
            p.op("sync", lambda e: e.dma_start(out=taps["t_h"].ap(), in_=B_H[:, :, :]),
                 reads=(bB,), dma_sem="DO0")
        ln_stats(bB, B_H[:, :, :], wantmean=False)
        rms_apply(bB, B_H[:, :, :], bC, C_M[:, :, :], N1W)
        if debug and u == 0:
            p.op("sync", lambda e: e.dma_start(out=taps["t_h1"].ap(), in_=C_M[:, :, :]),
                 reads=(bC,), dma_sem="DO0")
        for c2 in range(4):
            j2 = c2 % 2
            p.op("gpsimd", lambda e, c2=c2, j2=j2: e.dma_start(
                out=ep_pair(j2)[:, :, :],
                in_=dr["wqT"].ap().rearrange("(c p) m -> p c m", p=128)[
                    :, :, 256 * c2:256 * (c2 + 1)]),
                writes=(bE[2 * j2], bE[2 * j2 + 1]), dma_sem=f"DW{j2}")
            for ci in range(2):
                c = 2 * c2 + ci
                psn, psv = ("PS2", PS2[:, :]) if c % 2 == 0 else ("PS3", PS3[:, :])
                lhs = ep_pair(j2)[:, :, 128 * ci:128 * (ci + 1)]
                proj_group(psv, bP[psn], lhs, bC, C_M[:, :, :],
                           extra=(bE[2 * j2], bE[2 * j2 + 1]))
                rope_pass(psv, bP[psn], qh3[:, c, :], bY)
        if debug and u == 0:
            p.op("sync", lambda e: e.dma_start(out=taps["t_qh"].ap(), in_=qh3[:, :, :]),
                 reads=(bY,), dma_sem="DO0")
        p.op("gpsimd", lambda e, u=u: e.dma_start(
            out=EP[:, 0:4, :], in_=dr["maskq"].ap()[u, :, 0:4, :]),
            writes=(bE[0], bE[1]), dma_sem="DM0")
        p.op("gpsimd", lambda e, u=u: e.dma_start(
            out=EP[:, 4:8, :], in_=dr["maskq"].ap()[u, :, 4:8, :]),
            writes=(bE[2], bE[3]), dma_sem="DM1")

        PS0b16 = PS0.ap().bitcast(BF16)
        M4s = [M4, M4B]

        def emit_bias_chain(h):
            p.op("vector", lambda e: e.reduce_max(RMX[:, :], M4[:, :], axis=AXX),
                 reads=(bM4[0],), writes=(bRMX,))
            p.op("vector", lambda e: e.tensor_scalar(
                MB4[:, 0:1], RMX[:, 0:1], 0.0, 1.0, op0=ALU.max, op1=ALU.add),
                reads=(bRMX,), writes=(bMB4,))
            p.op("tensor", lambda e: e.transpose(
                PS0b16[0:1, 0:128], MB4[:, 0:1], IDENT[:, :]),
                reads=(bMB4, bconst), writes=(bP["PS0"],))
            p.op("vector", lambda e: e.reduce_max(MSC[0:1, 0:1], PS0b16[0:1, 0:128], axis=AXX),
                 reads=(bP["PS0"],), writes=(bMSC,))
            p.op("vector", lambda e: e.tensor_single_scalar(
                MSC[0:1, 1:2], MSC[0:1, 0:1], -0.125, op=ALU.mult),
                reads=(bMSC,), writes=(bMSC,))
            p.op("tensor", lambda e: e.matmul(
                PS1[:, 0:1], ONESB[0:1, 0:128], MSC[0:1, 1:2], start=True, stop=True),
                reads=(bMSC, bconst), writes=(bP["PS1"],))
            p.op("vector", lambda e: e.tensor_copy(MBIAS[:, 0:1], PS1[:, 0:1]),
                 reads=(bP["PS1"],), writes=(bMBIAS,))

        aunit = [0]
        for hp in range(H // 2):
            for qs in range(4):
                for pr in range(span // 2):
                    for par in range(2):
                        h = 2 * hp + par
                        hc = hp
                        r0 = par * 64
                        pspair = PS45 if aunit[0] % 2 == 0 else PS67
                        pn = ("PS45a", "PS45b") if aunit[0] % 2 == 0 else ("PS67a", "PS67b")
                        aunit[0] += 1
                        for hf in range(2):
                            blk = 2 * pr + hf
                            last = hf == 1
                            p.op("tensor", lambda e, last=last, qs=qs, blk=blk, hf=hf,
                                 hc=hc, r0=r0, pspair=pspair: e.matmul(
                                pspair[:, 512 * hf:512 * (hf + 1)],
                                qh3[r0:r0 + 64, hc, 128 * qs:128 * (qs + 1)],
                                K[r0:r0 + 64, hc, 512 * blk:512 * (blk + 1)],
                                start=True, stop=True, tile_position=(r0, 0)),
                                reads=(bY, bK), writes=(bP[pn[0]], bP[pn[1]]),
                                pre_only=not last)
                        p.op("vector", lambda e, pspair=pspair: e.reduce_max(
                            RMX[:, :], pspair[:, :], axis=AXX),
                             reads=(bP[pn[0]], bP[pn[1]]), writes=(bRMX,))
                        first = hp == 0 and pr == 0 and par == 0
                        if first:
                            p.op("vector", lambda e, qs=qs: e.tensor_copy(
                                M4[:, qs:qs + 1], RMX[:, :]),
                                 reads=(bRMX,), writes=(bM4[0],))
                        else:
                            p.op("vector", lambda e, qs=qs: e.tensor_max(
                                M4[:, qs:qs + 1], M4[:, qs:qs + 1], RMX[:, :]),
                                reads=(bRMX, bM4[0]), writes=(bM4[0],))
        emit_bias_chain(0)
        bring = [("PS67a", PS67[:, 0:512]), ("PS67b", PS67[:, 512:1024]),
                 ("PS45a", PS45[:, 0:512]), ("PS45b", PS45[:, 512:1024])]
        pring = [GPb16[:, 0, 0:512], GPb16[:, 0, 512:1024],
                 DRW.ap().bitcast(BF16)[:, 0, 0:512],
                 DRW.ap().bitcast(BF16)[:, 0, 512:1024]]

        def emit_B(h, kc):
            hc = h // 2
            r0 = (h % 2) * 64
            bn, bap = bring[(2 * kc + (h % 2)) % 4]
            p.op("tensor", lambda e, kc=kc, bap=bap, hc=hc, r0=r0: e.matmul(
                bap, K[r0:r0 + 64, hc, 128 * kc:128 * (kc + 1)],
                qh3[r0:r0 + 64, hc, :], start=True, stop=True,
                tile_position=(r0, 0)),
                reads=(bK, bY), writes=(bP[bn],))
            pslot = (2 * kc + (h % 2)) % 4
            pT = pring[pslot]
            pbuf = bPT[pslot]
            p.op("scalar", lambda e, bap=bap, pT=pT, h=h: e.activation(
                pT, bap, AF.Exp, bias=MBIAS[:, 0:1], scale=CQ8),
                reads=(bP[bn], bconst, bMBIAS), writes=(pbuf,))
            j8 = kc - (nkc - 8)
            if j8 >= 0:
                p.op("vector", lambda e, pT=pT, j8=j8: e.tensor_mul(
                    pT, pT, EP[:, j8, :]),
                    reads=tuple(bE) + (pbuf,), writes=(pbuf,))

        def emit_PV(h, kc):
            ctxn = "PS2" if h % 2 == 0 else "PS3"
            ctxps = PS2 if h % 2 == 0 else PS3
            pslot = (2 * kc + (h % 2)) % 4
            pT = pring[pslot]
            pbuf = bPT[pslot]
            last = kc == nkc - 1
            p.op("tensor", lambda e, kc=kc, h=h, pT=pT, ctxps=ctxps, nkc=nkc: e.matmul(
                ctxps[0:65, :], VH[:, kc, 65 * h:65 * (h + 1)], pT,
                start=(kc == 0), stop=(kc == nkc - 1)),
                reads=(bVH, pbuf), writes=(bP[ctxn],), pre_only=not last)

        def emit_evac(h):
            hc = h // 2
            ctxn = "PS2" if h % 2 == 0 else "PS3"
            ctxps = PS2 if h % 2 == 0 else PS3
            p.op("scalar", lambda e, ctxps=ctxps: e.copy(RL[64:65, :], ctxps[64:65, :]),
                 reads=(bP[ctxn],), writes=(bRL,))
            rbn = "PS0" if h % 2 == 0 else "PS1"
            rbp = PS0 if h % 2 == 0 else PS1
            p.op("tensor", lambda e, rbp=rbp: e.matmul(
                rbp[0:64, :], ONESF[64:65, 0:64], RL[64:65, :], start=True, stop=True,
                tile_position=(64, 0)),
                reads=(bRL, bconst), writes=(bP[rbn],))
            p.op("vector", lambda e, rbp=rbp: e.reciprocal(DRW[0:64, 1, :], rbp[0:64, :]),
                 reads=(bP[rbn],), writes=(bD1,))
            if h % 2 == 0:
                p.op("vector", lambda e, ctxps=ctxps, hc=hc: e.tensor_mul(
                    C_M[0:64, hc, :], ctxps[0:64, :], DRW[0:64, 1, :]),
                    reads=(bP[ctxn], bD1), writes=(bC,))
            else:
                p.op("vector", lambda e, ctxps=ctxps: e.tensor_mul(
                    GPb16[0:64, 1, 0:512], ctxps[0:64, :], DRW[0:64, 1, :]),
                    reads=(bP[ctxn], bD1), writes=(bGPb,))
                p.op("sync", lambda e, hc=hc: e.dma_start(
                    out=C_M[64:128, hc, :], in_=GPb16[0:64, 1, 0:512]),
                    reads=(bGPb,), writes=(bC,), dma_sem="DK1")

        for hp in range(H // 2):
            h0, h1 = 2 * hp, 2 * hp + 1
            emit_B(h0, 0)
            emit_B(h1, 0)
            for kc in range(nkc):
                if kc == 0 and hp > 0:
                    emit_evac(2 * hp - 2)
                    emit_evac(2 * hp - 1)
                if kc + 1 < nkc:
                    emit_B(h0, kc + 1)
                    emit_B(h1, kc + 1)
                emit_PV(h0, kc)
                emit_PV(h1, kc)
        emit_evac(H - 2)
        emit_evac(H - 1)
        if debug and u == 0:
            p.op("sync", lambda e: e.dma_start(out=taps["t_ctx"].ap(), in_=C_M[:, :, :]),
                 reads=(bC,), dma_sem="DO0")

        for c2 in range(4):
            j2 = c2 % 2
            p.op("gpsimd", lambda e, c2=c2, j2=j2: e.dma_start(
                out=ep_pair(j2)[:, :, :],
                in_=dr["woT"].ap().rearrange("(c p) m -> p c m", p=128)[
                    :, :, 256 * c2:256 * (c2 + 1)]),
                writes=(bE[2 * j2], bE[2 * j2 + 1]), dma_sem=f"DW{j2}")
            for ci in range(2):
                c = 2 * c2 + ci
                psn = "PS45a" if c % 2 == 0 else "PS45b"
                psv = PS45[:, 0:512] if c % 2 == 0 else PS45[:, 512:1024]
                lhs = ep_pair(j2)[:, :, 128 * ci:128 * (ci + 1)]
                proj_group(psv, bP[psn], lhs, bC, C_M[:, :, :],
                           extra=(bE[2 * j2], bE[2 * j2 + 1]))
                p.op("vector", lambda e, c=c, psv=psv: e.tensor_add(
                    Y_B[:, c, :], psv, B_H[:, c, :]),
                    reads=(bP[psn], bB), writes=(bY,))
                p.op("vector", lambda e, c=c, psv=psv: e.scalar_tensor_tensor(
                    A_Z[:, c, :], psv, GAIN[:, 2, c:c + 1], A_Z[:, c, :],
                    op0=ALU.mult, op1=ALU.add),
                    reads=(bP[psn], bGAIN, bA), writes=(bA,))
        if debug and u == 0:
            p.op("sync", lambda e: e.dma_start(out=taps["t_y"].ap(), in_=Y_B[:, :, :]),
                 reads=(bY,), dma_sem="DO0")

        ln_stats(bY, Y_B[:, :, :], wantmean=False)
        rms_apply(bY, Y_B[:, :, :], bC, C_M[:, :, :], N2W)
        for halfF in range(2):
            for fc16 in range(FC_MOE):
                fc = 16 * halfF + fc16
                j2 = fc % 2
                if fc16 % 2 == 0:
                    fb = (fc // 2) % 2
                    dst = EP[:, :, :] if fb == 0 else B_H[:, :, :]
                    wbufs = tuple(bE) if fb == 0 else (bB,)
                    p.op("gpsimd", lambda e, fc=fc, dst=dst: e.dma_start(
                        out=dst,
                        in_=dr["w13T"].ap().rearrange("(c p) m -> p c m", p=128)[
                            :, :, 256 * fc:256 * (fc + 2)]),
                        writes=wbufs, dma_sem=f"DW{fb}")
                fb = (fc // 2) % 2
                wt = (EP if fb == 0 else B_H).ap().rearrange(
                    "p a b -> p (a b)").rearrange("p (c n) -> p c n", n=512)
                w13 = wt[:, :, 256 * (fc % 2):256 * (fc % 2) + 256]
                wbufs13 = tuple(bE) if fb == 0 else (bB,)
                pu = "PS45a" if fc % 2 == 0 else "PS67a"
                pg = "PS45b" if fc % 2 == 0 else "PS67b"
                pub = (PS45 if fc % 2 == 0 else PS67)[:, 0:512]
                pgb = (PS45 if fc % 2 == 0 else PS67)[:, 512:1024]
                for k in range(8):
                    last = k == 7
                    p.op("tensor", lambda e, last=last, k=k, w13=w13, pub=pub: e.matmul(
                        pub, w13[:, k, 0:128], C_M[:, k, :], start=(k == 0), stop=last),
                        reads=wbufs13 + (bC,), writes=(bP[pu],),
                        pre_only=not last)
                for k in range(8):
                    last = k == 7
                    p.op("tensor", lambda e, last=last, k=k, w13=w13, pgb=pgb: e.matmul(
                        pgb, w13[:, k, 128:256], C_M[:, k, :], start=(k == 0), stop=last),
                        reads=wbufs13 + (bC,), writes=(bP[pg],),
                        pre_only=not last)
                slb = GPb16[:, 0, 0:512]
                p.op("scalar", lambda e, pub=pub, slb=slb: e.activation(slb, pub, AF.Silu),
                     reads=(bP[pu],), writes=(bGPa,))
                ffdst = Yb16[:, fc16 % 8, 512 * (fc16 // 8):512 * (fc16 // 8 + 1)]
                p.op("vector", lambda e, ffdst=ffdst, slb=slb, pgb=pgb: e.tensor_mul(
                    ffdst, slb, pgb),
                    reads=(bGPa, bP[pg]), writes=(bY,))
            for c in range(8):
                if c % 2 == 0:
                    cb2 = (c // 2) % 2
                    dstw = (EP if cb2 == 0 else B_H).ap().rearrange(
                        "p a b -> p (a b)").rearrange("p (f n) -> p f n", n=256)[:, 0:16, :]
                    wbuf2 = tuple(bE) if cb2 == 0 else (bB,)
                    p.op("gpsimd", lambda e, c=c, halfF=halfF, dstw=dstw: e.dma_start(
                        out=dstw,
                        in_=dr["w2T"].ap().rearrange("(f p) m -> p f m", p=128)[
                            :, 16 * halfF:16 * (halfF + 1), 128 * c:128 * (c + 2)]),
                        writes=wbuf2, dma_sem=f"DW{cb2}")
                cb2 = (c // 2) % 2
                wv = (EP if cb2 == 0 else B_H).ap().rearrange(
                    "p a b -> p (a b)").rearrange("p (f n) -> p f n", n=256)[
                    :, 0:16, 128 * (c % 2):128 * (c % 2) + 128]
                wrd = tuple(bE) if cb2 == 0 else (bB,)
                psn = "PS2" if c % 2 == 0 else "PS3"
                psv = (PS2 if c % 2 == 0 else PS3)[:, :]
                for fc16 in range(FC_MOE):
                    last = fc16 == FC_MOE - 1
                    ffsrc = Yb16[:, fc16 % 8, 512 * (fc16 // 8):512 * (fc16 // 8 + 1)]
                    p.op("tensor", lambda e, last=last, fc16=fc16, wv=wv, psv=psv, ffsrc=ffsrc: e.matmul(
                        psv, wv[:, fc16, :], ffsrc, start=(fc16 == 0), stop=last),
                        reads=wrd + (bY,), writes=(bP[psn],), pre_only=not last)
                p.op("vector", lambda e, c=c, psv=psv: e.scalar_tensor_tensor(
                    A_Z[:, c, :], psv, GAIN[:, 2, c:c + 1], A_Z[:, c, :],
                    op0=ALU.mult, op1=ALU.add),
                    reads=(bP[psn], bGAIN, bA), writes=(bA,))
        if debug and u == 0:
            p.op("sync", lambda e: e.dma_start(out=taps["t_x1"].ap(), in_=A_Z[:, :, :]),
                 reads=(bA,), dma_sem="DO0")

        ln_stats(bA, A_Z[:, :, :], wantmean=True)
        ln_apply(bA, A_Z[:, :, :], bB, B_H[:, :, :], grow=4, srow=3)
        if debug and u == 0:
            p.op("sync", lambda e: e.dma_start(out=taps["t_g"].ap(), in_=B_H[:, :, :]),
                 reads=(bB,), dma_sem="DO0")
        for qs in range(4):
            gout = PS2[:, 8 * qs:8 * (qs + 1)]
            for k in range(8):
                p.op("tensor", lambda e, k=k, qs=qs, gout=gout: e.matmul(
                    gout, B_H[:, k, 128 * qs:128 * (qs + 1)], WGT[:, k, :],
                    start=(k == 0), stop=False),
                    reads=(bB, bWG), writes=(bP["PS2"],), pre_only=True)
            p.op("tensor", lambda e, qs=qs, gout=gout: e.matmul(
                gout, ONESB[0:1, 0:128], BGR[0:1, :], start=False, stop=True),
                reads=(bB, bWG, bconst), writes=(bP["PS2"],))
            p.op("vector", lambda e, gout=gout: e.reduce_max(
                RMX[:, :], gout, axis=AXX, negate=True),
                reads=(bP["PS2"],), writes=(bRMX,))
            p.op("vector", lambda e, gout=gout: e.tensor_scalar(
                GP[:, 0, 0:8], gout, RMX[:, 0:1], 0.0, op0=ALU.add, op1=ALU.is_ge),
                reads=(bP["PS2"], bRMX), writes=(bGPa,))
            p.op("vector", lambda e, gout=gout: e.scalar_tensor_tensor(
                GP[:, 1, 0:8], GP[:, 0, 0:8], -1e30, gout,
                op0=ALU.mult, op1=ALU.add),
                reads=(bGPa, bP["PS2"]), writes=(bGPb,))
            p.op("vector", lambda e: e.reduce_max(
                M4[:, 0:1], GP[:, 1, 0:8], axis=AXX, negate=True),
                reads=(bGPb,), writes=(bM4[0],))
            p.op("scalar", lambda e, gout=gout: e.activation(
                GP[:, 0, 0:8], gout, AF.Exp, bias=RMX[:, 0:1], scale=1.0),
                reads=(bP["PS2"], bRMX), writes=(bGPa,))
            p.op("vector", lambda e: e.tensor_sub(M4[:, 1:2], M4[:, 0:1], RMX[:, 0:1]),
                 reads=(bM4[0], bRMX), writes=(bM4[0],))
            p.op("scalar", lambda e: e.activation(
                M4[:, 2:3], M4[:, 1:2], AF.Exp, bias=0.0, scale=CON[:, 2:3]),
                reads=(bM4[0], bconst), writes=(bM4[0],))
            p.op("vector", lambda e: e.tensor_scalar_add(M4[:, 2:3], M4[:, 2:3], 1.0),
                 reads=(bM4[0],), writes=(bM4[0],))
            p.op("vector", lambda e: e.reciprocal(M4[:, 3:4], M4[:, 2:3]),
                 reads=(bM4[0],), writes=(bM4[0],))
            p.op("vector", lambda e, gout=gout: e.tensor_scalar(
                GP[:, 1, 0:8], gout, M4[:, 0:1], 0.0, op0=ALU.add, op1=ALU.is_ge),
                reads=(bP["PS2"], bM4[0]), writes=(bGPb,))
            p.op("vector", lambda e, qs=qs: e.scalar_tensor_tensor(
                CMB[:, qs, :], GP[:, 0, 0:8], M4[:, 3:4], GP[:, 1, 0:8],
                op0=ALU.mult, op1=ALU.mult),
                reads=(bGPa, bGPb, bM4[0]), writes=(bCMB,))
        if debug and u == 0:
            p.op("sync", lambda e: e.dma_start(out=taps["t_comb"].ap(), in_=CMB[:, :, :]),
                 reads=(bCMB,), dma_sem="DO0")
        for qs in range(4):
            p.op("vector", lambda e, qs=qs: e.tensor_copy(CMBB[:, :], CMB[:, qs, :]),
                 reads=(bCMB,), writes=(bCMBB,))
            p.op("tensor", lambda e, qs=qs: e.transpose(
                PS3b16[0:8, 128 * qs:128 * (qs + 1)], CMBB[:, :], IDENT[:, :]),
                reads=(bCMBB, bconst), writes=(bP["PS3"],))
        p.op("vector", lambda e: e.tensor_copy(SH8[0:8, 0:512], PS3b16[0:8, 0:512]),
             reads=(bP["PS3"],), writes=(bSH,))

        for ex in range(NE):
            p.op("sync", lambda e, ex=ex: e.dma_start(
                out=SH8[0:1, 512:1024], in_=SH8[ex:ex + 1, 0:512]),
                reads=(bSH,), writes=(bSH,), dma_sem="DK2")
            cbn = "PS0" if ex % 2 == 0 else "PS1"
            cbp = PS0 if ex % 2 == 0 else PS1
            p.op("tensor", lambda e, cbp=cbp: e.matmul(
                cbp[:, :], ONESB[0:1, :], SH8[0:1, 512:1024], start=True, stop=True),
                reads=(bSH, bconst), writes=(bP[cbn],))
            for fc in range(FC_MOE):
                if fc % 2 == 0:
                    fb = (fc // 2) % 2
                    dstw = EP[:, :, :] if fb == 0 else C_M[:, :, :]
                    wbufs = tuple(bE) if fb == 0 else (bC,)
                    p.op("gpsimd", lambda e, ex=ex, fc=fc, dstw=dstw: e.dma_start(
                        out=dstw,
                        in_=dr["wupC"].ap()[ex].rearrange("(c p) m -> p c m", p=128)[
                            :, :, 256 * fc:256 * (fc + 2)]),
                        writes=wbufs, dma_sem=f"DW{fb}")
                fb = (fc // 2) % 2
                wtm = (EP if fb == 0 else C_M).ap().rearrange(
                    "p a b -> p (a b)").rearrange("p (c n) -> p c n", n=512)
                wup = wtm[:, :, 256 * (fc % 2):256 * (fc % 2) + 256]
                wbufsu = tuple(bE) if fb == 0 else (bC,)
                pu = "PS45a" if fc % 2 == 0 else "PS67a"
                pg = "PS45b" if fc % 2 == 0 else "PS67b"
                pub = (PS45 if fc % 2 == 0 else PS67)[:, 0:512]
                pgb = (PS45 if fc % 2 == 0 else PS67)[:, 512:1024]
                for k in range(8):
                    last = k == 7
                    p.op("tensor", lambda e, last=last, k=k, wup=wup, pub=pub: e.matmul(
                        pub, wup[:, k, 0:128], B_H[:, k, :], start=(k == 0), stop=last),
                        reads=wbufsu + (bB,), writes=(bP[pu],),
                        pre_only=not last)
                for k in range(8):
                    last = k == 7
                    p.op("tensor", lambda e, last=last, k=k, wup=wup, pgb=pgb: e.matmul(
                        pgb, wup[:, k, 128:256], B_H[:, k, :], start=(k == 0), stop=last),
                        reads=wbufsu + (bB,), writes=(bP[pg],),
                        pre_only=not last)
                slb = GPb16[:, 0, 0:512]
                p.op("scalar", lambda e, pgb=pgb, slb=slb: e.activation(slb, pgb, AF.Silu),
                     reads=(bP[pg],), writes=(bGPa,))
                p.op("vector", lambda e, slb=slb, pub=pub: e.tensor_mul(
                    GP[:, 1, :], slb, pub),
                    reads=(bGPa, bP[pu]), writes=(bGPb,))
                hhdst = Yb16[:, fc % 8, 512 * (fc // 8):512 * (fc // 8 + 1)]
                p.op("vector", lambda e, hhdst=hhdst, cbp=cbp: e.tensor_mul(
                    hhdst, GP[:, 1, :], cbp[:, :]),
                    reads=(bGPb, bP[cbn]), writes=(bY,))
            for c in range(8):
                if c % 2 == 0:
                    cb2 = (c // 2) % 2
                    dstw = (EP if cb2 == 0 else C_M).ap().rearrange(
                        "p a b -> p (a b)").rearrange("p (f n) -> p f n", n=256)
                    wbufsd = tuple(bE) if cb2 == 0 else (bC,)
                    p.op("gpsimd", lambda e, ex=ex, c=c, dstw=dstw: e.dma_start(
                        out=dstw,
                        in_=dr["wdn"].ap()[ex].rearrange("(f p) m -> p f m", p=128)[
                            :, :, 128 * c:128 * (c + 2)]),
                        writes=wbufsd, dma_sem=f"DM{cb2}")
                cb2 = (c // 2) % 2
                wv = (EP if cb2 == 0 else C_M).ap().rearrange(
                    "p a b -> p (a b)").rearrange("p (f n) -> p f n", n=256)[
                    :, :, 128 * (c % 2):128 * (c % 2) + 128]
                wrd = tuple(bE) if cb2 == 0 else (bC,)
                psn = "PS2" if c % 2 == 0 else "PS3"
                psv = (PS2 if c % 2 == 0 else PS3)[:, :]
                for fc in range(FC_MOE):
                    last = fc == FC_MOE - 1
                    hhsrc = Yb16[:, fc % 8, 512 * (fc // 8):512 * (fc // 8 + 1)]
                    p.op("tensor", lambda e, last=last, fc=fc, wv=wv, psv=psv, hhsrc=hhsrc: e.matmul(
                        psv, wv[:, fc, :], hhsrc, start=(fc == 0), stop=last),
                        reads=wrd + (bY,), writes=(bP[psn],), pre_only=not last)
                p.op("vector", lambda e, c=c, psv=psv: e.scalar_tensor_tensor(
                    A_Z[:, c, :], psv, GAIN[:, 5, c:c + 1], A_Z[:, c, :],
                    op0=ALU.mult, op1=ALU.add),
                    reads=(bP[psn], bGAIN, bA), writes=(bA,))

        dsem = "DO0" if u % 2 == 0 else "DO1"
        p.op("sync", lambda e, u=u: e.dma_start(out=yout.ap()[u], in_=A_Z[:, :, :]),
             reads=(bA,), dma_sem=dsem)

    totals = [(n, p.cnt[n]) for n in ("DO0", "DO1") if p.cnt[n] > 0]
    def tail(e, totals=totals, hs={n: p.handles[n] for n, _ in totals}):
        for n, v in totals:
            e.wait_ge(hs[n], v)
    p.raw("sync", tail)

    with nc.Block() as block:
        for eng in ENGINES:
            def runner(e, lst=p.ops[eng]):
                for f in lst:
                    f(e)
            getattr(block, eng)(runner)
    p.close_sems()
    return nc


def _rope_tables():
    pos = np.arange(S, dtype=np.float64)
    kk = np.arange(64) // 2
    ang = pos[None, :] / np.power(10000.0, 2.0 * kk.astype(np.float64) / 64.0)[:, None]
    cos = np.cos(ang)
    sinr = np.sin(ang)
    CT = np.empty((64, S))
    ST = np.empty((64, S))
    for n in range(64):
        m = n % 32
        CT[n] = cos[2 * m]
        ST[n] = sinr[2 * m] * (-1.0 if n < 32 else 1.0)
    CT2 = np.concatenate([CT, CT], axis=0).astype(np.float32)
    ST2 = np.concatenate([ST, ST], axis=0).astype(np.float32)
    return np.ascontiguousarray(CT2), np.ascontiguousarray(ST2)


def _deinterleave_perm():
    perm = np.empty(D, dtype=np.int64)
    for h in range(H):
        base = 64 * h
        perm[base:base + 32] = base + 2 * np.arange(32)
        perm[base + 32:base + 64] = base + 2 * np.arange(32) + 1
    return perm


def _fm(v):
    return np.ascontiguousarray(v.reshape(8, 128).T.astype(np.float32))


def stage_inputs(inputs):
    x = np.asarray(inputs["L0_attn_x"], np.float32)
    perm = _deinterleave_perm()
    CT2, ST2 = _rope_tables()
    wq = np.asarray(inputs["L0_attn_wq"], np.float32)[perm, :]
    wk = np.asarray(inputs["L0_attn_wk"], np.float32)[perm, :]
    wqT = np.ascontiguousarray(wq.T)
    wkT = np.ascontiguousarray(wk.T)
    wvT = np.ascontiguousarray(np.asarray(inputs["L0_attn_wv"], np.float32).T)
    woT = np.ascontiguousarray(np.asarray(inputs["L0_attn_wo"], np.float32).T)
    w1T = np.asarray(inputs["L0_attn_w1"], np.float32).T
    w3T = np.asarray(inputs["L0_attn_w3"], np.float32).T
    w13 = np.empty((D, FC_FFN * 256), np.float32)
    for fc in range(FC_FFN):
        w13[:, 256 * fc:256 * fc + 128] = w1T[:, 128 * fc:128 * (fc + 1)]
        w13[:, 256 * fc + 128:256 * (fc + 1)] = w3T[:, 128 * fc:128 * (fc + 1)]
    w2T = np.ascontiguousarray(np.asarray(inputs["L0_attn_w2"], np.float32).T)
    wup = np.asarray(inputs["L0_ffn_Wup"], np.float32)
    wupC = np.empty((NE, D, FC_MOE * 256), np.float32)
    for fc in range(FC_MOE):
        wupC[:, :, 256 * fc:256 * fc + 128] = wup[:, :, 128 * fc:128 * (fc + 1)]
        wupC[:, :, 256 * fc + 128:256 * (fc + 1)] = wup[:, :, 2048 + 128 * fc:2048 + 128 * (fc + 1)]
    wdn = np.ascontiguousarray(np.asarray(inputs["L0_ffn_Wdown"], np.float32))
    wmod = np.ascontiguousarray(np.asarray(inputs["L0_wmod"], np.float32))
    wgT = np.ascontiguousarray(np.asarray(inputs["L0_ffn_Wg"], np.float32).T)
    bg_row = np.asarray(inputs["L0_ffn_bg"], np.float32).reshape(1, 8)
    n1w = _fm(np.asarray(inputs["L0_attn_norm1_w"], np.float32))
    n2w = _fm(np.asarray(inputs["L0_attn_norm2_w"], np.float32))
    consts = np.broadcast_to(np.array(
        [1e-5, 1.0 / 1024.0, -1.0, 0.125, 0.25, 1.0, -1e30, 0.0], np.float32), (128, 8)).copy()
    cr1 = np.array([[-1.0, 0, 0, 0, 0, 0, 0, 0]], np.float32)
    onesf = np.ones((128, 128), np.float32)
    identf = np.eye(128, dtype=np.float32)

    def maskq_for(par):
        m = np.ones((4, 128, 8, T), np.float32)
        for u in range(4):
            span = PSPAN[u]
            tspan = OWN[par][u] + 1
            nkc = 4 * span
            for j8 in range(8):
                kc = nkc - 8 + j8
                blk = kc // 4
                if blk > tspan - 1:
                    m[u, :, j8, :] = 0.0
                    continue
                if blk < tspan - 1:
                    continue
                off = (kc % 4) * 128
                kpos = off + np.arange(128)[:, None]
                qpos = np.arange(T)[None, :]
                m[u, :, j8, :] = (kpos <= qpos).astype(np.float32)
        return m

    maskq = [maskq_for(0), maskq_for(1)]

    in_maps = []
    for core in range(8):
        b = core // 2
        par = core % 2
        xT = np.ascontiguousarray(x[b].T)
        xs = np.concatenate([xT[:, T * t:T * (t + 1)] for t in OWN[par]], axis=1)
        rqC = np.concatenate([CT2[:, T * t:T * (t + 1)] for t in OWN[par]], axis=1)
        rqS = np.concatenate([ST2[:, T * t:T * (t + 1)] for t in OWN[par]], axis=1)
        in_maps.append({
            "x_full": xT, "x_slots": np.ascontiguousarray(xs),
            "ropeC": CT2, "ropeS": ST2,
            "ropeqC": np.ascontiguousarray(rqC), "ropeqS": np.ascontiguousarray(rqS),
            "wqT": wqT, "wkT": wkT, "wvT": wvT, "woT": woT,
            "w13T": w13, "w2T": w2T, "wupC": wupC, "wdn": wdn,
            "wmod": wmod, "wgT": wgT, "bg_row": bg_row,
            "c_fm": _fm(np.asarray(inputs["L0_c"], np.float32)[b]),
            "n1w": n1w, "n2w": n2w, "consts": consts, "cr1": cr1,
            "onesf": onesf, "identf": identf, "maskq": maskq[par],
        })
    return in_maps


def gather_output(results):
    out = np.empty((NB, S, D), np.float32)
    for core in range(8):
        b = core // 2
        par = core % 2
        y = results[core]["yout"]
        for u, t in enumerate(OWN[par]):
            out[b, T * t:T * (t + 1), :] = y[u].transpose(2, 1, 0).reshape(T, D)
    return out


def kernel(**inputs):
    in_maps = stage_inputs(inputs)
    nc = build_nc(debug=DEBUG)
    res = run_bass_kernel_spmd(nc, in_maps, core_ids=list(range(8)))
    return gather_output(res.results)


# version 07
# speedup vs baseline: 1.0882x; 1.0154x over previous
import math
import numpy as np
import ml_dtypes
import concourse.bass as bass
import concourse.mybir as mybir
from concourse.bass_utils import run_bass_kernel_spmd

F32 = mybir.dt.float32
BF16 = mybir.dt.bfloat16
AF = mybir.ActivationFunctionType
ALU = mybir.AluOpType
AXX = mybir.AxisListType.X

D = 1024
S = 4096
NB = 4
H = 16
T = 512
NT = S // T
OWN = [[0, 3, 4, 7], [1, 2, 5, 6]]
PSPAN = [2, 4, 6, 8]
NUNIT = sum(PSPAN)
NE = 8
FC_FFN = 32
FC_MOE = 16
ENGINES = ("tensor", "vector", "scalar", "sync", "gpsimd")

DEBUG = False


class Buf:
    __slots__ = ("name", "w", "r")

    def __init__(self, name):
        self.name = name
        self.w = None
        self.r = {}


class Prog:
    def __init__(self, nc):
        self.nc = nc
        self.ops = {e: [] for e in ENGINES}
        self.cnt = {}
        self.lastw = {}
        self._stack = []
        self.handles = {}
        self.prev_dma = {}

    def sem(self, name):
        g = self.nc.semaphore(name)
        h = g.__enter__()
        self._stack.append(g)
        self.cnt[name] = 0
        self.handles[name] = h
        return name

    def _waits(self, eng, toks):
        best = {}
        for t in toks:
            if t is None:
                continue
            s, v = t
            if best.get(s, -1) < v:
                best[s] = v
        out = []
        for s, v in best.items():
            if s == eng:
                pass
            if self.lastw.get((eng, s), 0) < v:
                self.lastw[(eng, s)] = v
                out.append((s, v))
        return out

    def op(self, eng, build, reads=(), writes=(), dma_sem=None, pre_only=False):
        if dma_sem is not None:
            reads = tuple(reads) + tuple(
                b for b in self.prev_dma.get(dma_sem, ()) if b not in writes)
            self.prev_dma[dma_sem] = tuple(writes)
            sink = self.prev_dma.setdefault(("sink", dma_sem), Buf("sink" + dma_sem))
            writes = tuple(writes) + (sink,)
        toks = []
        for b in reads:
            toks.append(b.w)
        for b in writes:
            toks.append(b.w)
            toks.extend(b.r.items())
        wl = self._waits(eng, toks)
        tok = None
        if not pre_only:
            sem = dma_sem if dma_sem is not None else eng
            self.cnt[sem] += 16 if dma_sem is not None else 1
            tok = (sem, self.cnt[sem])
        need = set(s for s, _ in wl)
        if tok is not None:
            need.add(tok[0])
        hs = {n: self.handles[n] for n in need}

        def closure(e, wl=wl, build=build, tok=tok, hs=hs, dma=dma_sem is not None):
            for s, v in wl:
                e.wait_ge(hs[s], v)
            ins = build(e)
            if tok is not None:
                ins.then_inc(hs[tok[0]], 16 if dma else 1)

        self.ops[eng].append(closure)
        if not pre_only:
            for b in reads:
                if b.r.get(tok[0], -1) < tok[1]:
                    b.r[tok[0]] = tok[1]
            for b in writes:
                b.w = tok
                b.r = {}
        return tok

    def raw(self, eng, fn):
        self.ops[eng].append(fn)

    def close_sems(self):
        for g in reversed(self._stack):
            g.__exit__(None, None, None)


def build_nc(debug=False):
    nc = bass.Bass("TRN2", target_bir_lowering=False)

    dr = {}

    def din(name, shape):
        dr[name] = nc.dram_tensor(name, list(shape), F32, kind="ExternalInput")

    din("x_full", (D, S))
    din("x_slots", (D, 4 * T))
    din("ropeC", (128, S))
    din("ropeS", (128, S))
    din("ropeqC", (128, 4 * T))
    din("ropeqS", (128, 4 * T))
    din("wqT", (D, D))
    din("wkT", (D, D))
    din("wvT", (D, D))
    din("woT", (D, D))
    din("w13T", (D, FC_FFN * 256))
    din("w2T", (4096, D))
    din("wupC", (NE, D, FC_MOE * 256))
    din("wdn", (NE, 2048, D))
    din("wmod", (D, 6 * D))
    din("wgT", (D, 8))
    din("bg_row", (1, 8))
    din("c_fm", (128, 8))
    din("n1w", (128, 8))
    din("n2w", (128, 8))
    din("consts", (128, 8))
    din("cr1", (1, 8))
    din("onesf", (128, 128))
    din("identf", (128, 128))
    din("maskq", (4, 128, 8, T))
    yout = nc.dram_tensor("yout", [4, 128, 8, T], F32, kind="ExternalOutput")

    taps = {}
    if debug:
        def tap(name, shape, dt=F32):
            taps[name] = nc.dram_tensor(name, list(shape), dt, kind="ExternalOutput")
        tap("t_K", [128, 8, S], BF16)
        tap("t_VH", [128, 32, 1040], BF16)
        tap("t_h", [128, 8, T], BF16)
        tap("t_h1", [128, 8, T], BF16)
        tap("t_qh", [128, 8, T], BF16)
        tap("t_ctx", [128, 8, T], BF16)
        tap("t_y", [128, 8, T], F32)
        tap("t_x1", [128, 8, T], F32)
        tap("t_g", [128, 8, T], BF16)
        tap("t_comb", [128, 4, 8], F32)
        tap("t_mod", [128, 48], F32)

    sb = nc.alloc_sbuf_tensor
    K = sb("K", [128, 8, S], BF16)
    VH = sb("VH", [128, 32, 1040], BF16)
    A_Z = sb("A_Z", [128, 8, T], F32)
    Y_B = sb("Y_B", [128, 8, T], F32)
    B_H = sb("B_H", [128, 8, T], BF16)
    C_M = sb("C_M", [128, 8, T], BF16)
    DRW = sb("DRW", [128, 2, T], F32)
    EP = sb("EP", [128, 8, T], BF16)
    GP = sb("GP", [128, 2, T], F32)
    GAIN = sb("GAIN", [128, 6, 8], F32)
    N1W = sb("N1W", [128, 8], F32)
    N2W = sb("N2W", [128, 8], F32)
    CON = sb("CON", [128, 8], F32)
    CR1 = sb("CR1", [1, 8], F32)
    ONESB = sb("ONESB", [128, 128], BF16)
    ONESF = sb("ONESF", [128, 128], F32)
    IDENT = sb("IDENT", [128, 128], BF16)
    SH8 = sb("SH8", [8, 1024], BF16)
    RL = sb("RL", [65, T], F32)
    M4 = sb("M4", [128, 4], F32)
    M4B = sb("M4B", [128, 4], F32)
    RMX = sb("RMX", [128, 1], F32)
    MB4 = sb("MB4", [128, 1], BF16)
    MSC = sb("MSC", [1, 4], BF16)
    MBIAS = sb("MBIAS", [128, 16], F32)
    CMB = sb("CMB", [128, 4, 8], F32)
    CMBB = sb("CMBB", [128, 8], BF16)
    BGR = sb("BGR", [1, 8], BF16)
    CF = sb("CF", [128, 8], BF16)
    CF0 = sb("CF0", [128, 8], F32)
    MODS = sb("MODS", [128, 48], F32)
    RPC = sb("RPC", [128, T], F32)
    RPS = sb("RPS", [128, T], F32)
    WGT = sb("WGT", [128, 8, 8], BF16)

    ap_ = nc.alloc_psum_tensor
    PS0 = ap_("PS0", [128, 512], F32)
    PS1 = ap_("PS1", [128, 512], F32)
    PS2 = ap_("PS2", [128, 512], F32)
    PS3 = ap_("PS3", [128, 512], F32)
    PS45 = ap_("PS45", [128, 1024], F32)
    PS67 = ap_("PS67", [128, 1024], F32)

    p = Prog(nc)
    for e in ENGINES:
        p.sem(e)
    for n in ("DX", "DRP", "DRS", "DK1", "DK2", "DO0", "DO1",
              "DW0", "DW1", "DW2", "DW3", "DW4", "DW5",
              "DM0", "DM1", "DCN", "DCG"):
        p.sem(n)

    bK = Buf("K"); bVH = Buf("VH")
    bD0a = Buf("D0a"); bD0b = Buf("D0b")
    bD1 = Buf("D1")
    bYs = [Buf(f"Ys{i}") for i in range(16)]
    bAs = [Buf(f"As{c}") for c in range(8)]
    bBs = [Buf(f"Bs{c}") for c in range(8)]
    bCs = [Buf(f"Cs{c}") for c in range(8)]
    YA, AA, BA, CA = tuple(bYs), tuple(bAs), tuple(bBs), tuple(bCs)

    def ych(c):
        return (bYs[c], bYs[c + 8])
    YCH = [ych(c) for c in range(8)]
    ACH = [(bAs[c],) for c in range(8)]
    BCH = [(bBs[c],) for c in range(8)]
    CCH = [(bCs[c],) for c in range(8)]
    bGPa = Buf("GPa"); bGPb = Buf("GPb")
    bE = [Buf(f"E{j}") for j in range(4)]
    bP = {n: Buf(n) for n in ("PS0", "PS1", "PS2", "PS3",
                              "PS45a", "PS45b", "PS67a", "PS67b")}
    bSH = Buf("SH8"); bRL = Buf("RL"); bM4 = [Buf("M4"), Buf("M4B")]; bRMX = Buf("RMX")
    bMB4 = Buf("MB4"); bCMB = Buf("CMB"); bCMBB = Buf("CMBB")
    bMSC = Buf("MSC"); bMBIAS = Buf("MBIAS")
    bMODS = Buf("MODS"); bGAIN = Buf("GAIN"); bCF = Buf("CF")
    bPT = [Buf("PT0"), Buf("PT1"), bD0a, bD0b]
    bRPC = Buf("RPC"); bRPS = Buf("RPS"); bWG = Buf("WGT")
    bconst = Buf("const")

    Yb16 = Y_B.ap().bitcast(BF16)
    GPb16 = GP.ap().bitcast(BF16)
    Bb16 = B_H
    PS45b16 = PS45.ap().bitcast(BF16)
    PS3b16 = PS3.ap().bitcast(BF16)
    qh3 = Yb16[:, :, 0:512]

    def ep_slot(j):
        return EP[:, 2 * j:2 * j + 2, :].rearrange("p a b -> p (a b)").rearrange(
            "p (c n) -> p c n", n=128)

    def ep_pair(j2):
        return EP[:, 4 * j2:4 * j2 + 4, :].rearrange("p a b -> p (a b)").rearrange(
            "p (c n) -> p c n", n=256)

    def b_half(j):
        return B_H[:, 4 * j:4 * j + 4, :].rearrange("p a b -> p (a b)").rearrange(
            "p (c n) -> p c n", n=128)

    CEPS = CON[:, 0:1]
    CK1024 = CON[:, 1:2]
    CQ8 = CON[:, 3:4]
    CONE = CON[:, 5:6]

    for nm, dst in (("consts", CON), ("cr1", CR1), ("n1w", N1W), ("n2w", N2W),
                    ("onesf", ONESF), ("c_fm", CF0)):
        p.op("sync", lambda e, nm=nm, dst=dst: e.dma_start(out=dst[:, :], in_=dr[nm].ap()),
             writes=(bconst,), dma_sem="DCN")
    for nm, dst in (("onesf", ONESB), ("identf", IDENT),
                    ("bg_row", BGR)):
        p.op("gpsimd", lambda e, nm=nm, dst=dst: e.dma_start(out=dst[:, :], in_=dr[nm].ap()),
             writes=(bconst,), dma_sem="DCG")
    p.op("gpsimd", lambda e: e.dma_start(
        out=WGT[:, :, :], in_=dr["wgT"].ap().rearrange("(c p) m -> p c m", p=128)),
        writes=(bWG,), dma_sem="DCG")

    vones = VH.ap().rearrange("p t (h x) -> p t h x", x=65)[:, :, :, 64:65]
    p.op("vector", lambda e: e.memset(vones, 1.0), writes=(bVH,))

    p.op("scalar", lambda e: e.activation(CF[:, :], CF0[:, :], AF.Silu),
         reads=(bconst,), writes=(bCF,))

    for j in range(48):
        sl = j % 2
        p.op("gpsimd", lambda e, j=j, sl=sl: e.dma_start(
            out=ep_slot(sl)[:, :, :], in_=dr["wmod"].ap().rearrange(
                "(c p) m -> p c m", p=128)[:, :, 128 * j:128 * (j + 1)]),
            writes=(bE[sl],), dma_sem=f"DW{sl}")
        for k in range(8):
            last = k == 7
            p.op("tensor", lambda e, last=last, j=j, k=k, sl=sl: e.matmul(
                PS2[:, j:j + 1], ep_slot(sl)[:, k, :], CF[:, k:k + 1],
                start=(k == 0), stop=last),
                reads=(bE[sl], bCF), writes=(bP["PS2"],), pre_only=not last)
    p.op("vector", lambda e: e.tensor_copy(MODS[:, :], PS2[:, 0:48]),
         reads=(bP["PS2"],), writes=(bMODS,))
    if debug:
        p.op("sync", lambda e: e.dma_start(out=taps["t_mod"].ap(), in_=MODS[:, :]),
             reads=(bMODS,), dma_sem="DO0")
    p.op("vector", lambda e: e.tensor_copy(GAIN[:, 0, :], MODS[:, 0:8]),
         reads=(bMODS,), writes=(bGAIN,))
    p.op("vector", lambda e: e.tensor_scalar_add(GAIN[:, 1, :], MODS[:, 8:16], 1.0),
         reads=(bMODS,), writes=(bGAIN,))
    p.op("vector", lambda e: e.tensor_copy(GAIN[:, 3, :], MODS[:, 24:32]),
         reads=(bMODS,), writes=(bGAIN,))
    p.op("vector", lambda e: e.tensor_scalar_add(GAIN[:, 4, :], MODS[:, 32:40], 1.0),
         reads=(bMODS,), writes=(bGAIN,))
    for src, dsti in ((16, 2), (40, 5)):
        p.op("scalar", lambda e, src=src: e.activation(
            GP[:, 0, 0:8], MODS[:, src:src + 8], AF.Abs),
            reads=(bMODS,), writes=(bGPa,))
        p.op("vector", lambda e: e.tensor_scalar_max(GP[:, 0, 0:8], GP[:, 0, 0:8], 0.25),
             reads=(bGPa,), writes=(bGPa,))
        p.op("scalar", lambda e, src=src: e.sign(GP[:, 1, 0:8], MODS[:, src:src + 8]),
             reads=(bMODS,), writes=(bGPb,))
        p.op("vector", lambda e, dsti=dsti: e.tensor_mul(
            GAIN[:, dsti, :], GP[:, 0, 0:8], GP[:, 1, 0:8]),
            reads=(bGPa, bGPb), writes=(bGAIN,))

    def ln_stats(srcbufs, src32, wantmean):
        if wantmean:
            for k in range(8):
                last = k == 7
                p.op("tensor", lambda e, last=last, k=k, src32=src32: e.matmul(
                    PS0[:, :], ONESF[:, :], src32[:, k, :], start=(k == 0), stop=last),
                    reads=(tuple({b for t in srcbufs for b in t}) + (bconst,)) if last
                    else (srcbufs[k] + (bconst,)),
                    writes=(bP["PS0"],), pre_only=not last)
        for c in range(8):
            p.op("scalar", lambda e, c=c, src32=src32: e.activation(
                C_M[:, c, :], src32[:, c, :], AF.Square),
                 reads=srcbufs[c], writes=(bCs[c],))
        for k in range(8):
            last = k == 7
            p.op("tensor", lambda e, last=last, k=k: e.matmul(
                PS1[:, :], ONESB[:, :], C_M[:, k, :], start=(k == 0), stop=last),
                reads=(CA + (bconst,)) if last else (bCs[k], bconst),
                writes=(bP["PS1"],), pre_only=not last)
        if wantmean:
            pass
            p.op("vector", lambda e: e.tensor_scalar_mul(DRW[:, 1, :], PS0[:, :], 1.0 / 1024.0),
                 reads=(bP["PS0"],), writes=(bD1,))
            p.op("vector", lambda e: e.tensor_mul(GP[:, 0, :], DRW[:, 1, :], DRW[:, 1, :]),
                 reads=(bD1,), writes=(bGPa,))
            p.op("vector", lambda e: e.scalar_tensor_tensor(
                GP[:, 0, :], PS1[:, :], 1.0 / 1024.0, GP[:, 0, :],
                op0=ALU.mult, op1=ALU.subtract),
                reads=(bP["PS1"], bGPa), writes=(bGPa,))
            p.op("scalar", lambda e: e.activation(GP[:, 0, :], GP[:, 0, :], AF.Sqrt,
                                                  bias=CEPS, scale=CONE),
                 reads=(bGPa, bconst), writes=(bGPa,))
            p.op("vector", lambda e: e.reciprocal(DRW[:, 0, :], GP[:, 0, :]),
                 reads=(bGPa,), writes=(bD0a, bD0b))
            p.op("vector", lambda e: e.tensor_mul(DRW[:, 1, :], DRW[:, 1, :], DRW[:, 0, :]),
                 reads=(bD0a, bD0b, bD1), writes=(bD1,))
        else:
            p.op("scalar", lambda e: e.activation(GP[:, 0, :], PS1[:, :], AF.Sqrt,
                                                  bias=CEPS, scale=CK1024),
                 reads=(bP["PS1"], bconst), writes=(bGPa,))
            p.op("vector", lambda e: e.reciprocal(DRW[:, 0, :], GP[:, 0, :]),
                 reads=(bGPa,), writes=(bD0a, bD0b))

    def ln_apply(srcbufs, src32, dstbufs, dst16, grow, srow):
        for c in range(8):
            p.op("vector", lambda e, c=c, src32=src32: e.tensor_mul(
                GP[:, 0, :], src32[:, c, :], DRW[:, 0, :]),
                reads=srcbufs[c] + (bD0a, bD0b), writes=(bGPa,))
            p.op("vector", lambda e: e.tensor_sub(GP[:, 0, :], GP[:, 0, :], DRW[:, 1, :]),
                 reads=(bGPa, bD1), writes=(bGPa,))
            p.op("scalar", lambda e, c=c, dst16=dst16, grow=grow, srow=srow: e.activation(
                dst16[:, c, :], GP[:, 0, :], AF.Identity,
                bias=GAIN[:, srow, c:c + 1], scale=GAIN[:, grow, c:c + 1]),
                reads=(bGPa, bGAIN), writes=dstbufs[c])

    def rms_apply(srcbufs, src, dstbufs, dst16, wten):
        for c in range(8):
            p.op("vector", lambda e, c=c, src=src: e.tensor_mul(
                GP[:, 0, :], src[:, c, :], DRW[:, 0, :]),
                reads=srcbufs[c] + (bD0a, bD0b), writes=(bGPa,))
            p.op("scalar", lambda e, c=c, dst16=dst16, wten=wten: e.activation(
                dst16[:, c, :], GP[:, 0, :], AF.Identity, bias=0.0,
                scale=wten[:, c:c + 1]),
                reads=(bGPa, bconst), writes=dstbufs[c])

    def wstream(slot, dsemi, wname, col0, ncol):
        p.op("gpsimd", lambda e, slot=slot, wname=wname, col0=col0, ncol=ncol: e.dma_start(
            out=ep_slot(slot)[:, :, 0:ncol] if ncol <= 128 else ep_pair(slot // 2)[:, :, 0:ncol],
            in_=dr[wname].ap().rearrange("(c p) m -> p c m", p=128)[:, :, col0:col0 + ncol]),
            writes=(bE[slot],) if ncol <= 128 else (bE[slot], bE[slot + 1]),
            dma_sem=f"DW{dsemi}")

    def proj_group(ps_ap, psbuf, lhs3, rhsbufs, rhs3, extra=()):
        allr = tuple(extra) + tuple({b for t in rhsbufs for b in t})
        for k in range(8):
            last = k == 7
            p.op("tensor", lambda e, last=last, k=k, ps_ap=ps_ap, lhs3=lhs3, rhs3=rhs3: e.matmul(
                ps_ap, lhs3[:, k, :], rhs3[:, k, :], start=(k == 0), stop=last),
                reads=allr if last else (tuple(extra) + rhsbufs[k]),
                writes=(psbuf,), pre_only=not last)

    def rope_pass(ps_ap, psbuf, dst_ap, dstbufs):
        p.op("vector", lambda e, ps_ap=ps_ap: e.tensor_mul(GP[:, 0, :], ps_ap, RPC[:, :]),
             reads=(psbuf, bRPC), writes=(bGPa,))
        for i, (o, s) in enumerate(((0, 32), (32, 0), (64, 96), (96, 64))):
            p.op("vector", lambda e, o=o, s=s, ps_ap=ps_ap: e.tensor_mul(
                GP[o:o + 32, 1, :], ps_ap[s:s + 32, :], RPS[o:o + 32, :]),
                reads=(psbuf, bRPS), writes=(bGPb,), pre_only=(i != 3))
        p.op("vector", lambda e, dst_ap=dst_ap: e.tensor_add(dst_ap, GP[:, 0, :], GP[:, 1, :]),
             reads=(bGPa, bGPb), writes=dstbufs)

    for j in range(NT):
        xsrc = dr["x_full"].ap().rearrange("(c p) s -> p c s", p=128)[:, :, T * j:T * (j + 1)]
        p.op("sync", lambda e, xsrc=xsrc: e.dma_start(out=Y_B[:, :, :], in_=xsrc),
             writes=YA, dma_sem="DX")
        p.op("sync", lambda e, j=j: e.dma_start(
            out=RPC[:, :], in_=dr["ropeC"].ap()[:, T * j:T * (j + 1)]),
            writes=(bRPC,), dma_sem="DRP")
        p.op("sync", lambda e, j=j: e.dma_start(
            out=RPS[:, :], in_=dr["ropeS"].ap()[:, T * j:T * (j + 1)]),
            writes=(bRPS,), dma_sem="DRS")
        ln_stats(YCH, Y_B[:, :, :], wantmean=True)
        ln_apply(YCH, Y_B[:, :, :], BCH, B_H[:, :, :], grow=1, srow=0)
        ln_stats(BCH, B_H[:, :, :], wantmean=False)
        rms_apply(BCH, B_H[:, :, :], CCH, C_M[:, :, :], N1W)
        for c2 in range(4):
            j2 = c2 % 2
            p.op("gpsimd", lambda e, c2=c2, j2=j2: e.dma_start(
                out=ep_pair(j2)[:, :, :],
                in_=dr["wkT"].ap().rearrange("(c p) m -> p c m", p=128)[
                    :, :, 256 * c2:256 * (c2 + 1)]),
                writes=(bE[2 * j2], bE[2 * j2 + 1]), dma_sem=f"DW{j2}")
            for ci in range(2):
                c = 2 * c2 + ci
                psn, psv = ("PS2", PS2[:, :]) if c % 2 == 0 else ("PS3", PS3[:, :])
                lhs = ep_pair(j2)[:, :, 128 * ci:128 * (ci + 1)]
                proj_group(psv, bP[psn], lhs, CCH, C_M[:, :, :],
                           extra=(bE[2 * j2], bE[2 * j2 + 1]))
                rope_pass(psv, bP[psn], K[:, c, T * j:T * (j + 1)], (bK,))
        for half in range(2):
            p.op("gpsimd", lambda e, half=half: e.dma_start(
                out=EP[:, :, :],
                in_=dr["wvT"].ap().rearrange("(c p) m -> p c m", p=128)[
                    :, :, 512 * half:512 * (half + 1)]),
                writes=tuple(bE), dma_sem="DM0")
            for s4 in range(4):
                psn = "PS45a" if s4 % 2 == 0 else "PS67a"
                psv = (PS45 if s4 % 2 == 0 else PS67)[:, 0:512]
                for k in range(8):
                    last = k == 7
                    p.op("tensor", lambda e, last=last, k=k, s4=s4, psv=psv: e.matmul(
                        psv, C_M[:, k, 128 * s4:128 * (s4 + 1)], EP[:, k, :],
                        start=(k == 0), stop=last),
                        reads=(CA if last else (bCs[k],)) + tuple(bE),
                        writes=(bP[psn],), pre_only=not last)
                vdst = VH[:, 4 * j + s4, :].rearrange("p (h x) -> p h x", x=65)[
                    :, 8 * half:8 * (half + 1), 0:64]
                p.op("vector", lambda e, psv=psv, vdst=vdst: e.tensor_copy(
                    vdst, psv.rearrange("p (h x) -> p h x", x=64)),
                     reads=(bP[psn],), writes=(bVH,))

    if debug:
        p.op("sync", lambda e: e.dma_start(out=taps["t_K"].ap(), in_=K[:, :, :]),
             reads=(bK,), dma_sem="DO0")
        p.op("sync", lambda e: e.dma_start(out=taps["t_VH"].ap(), in_=VH[:, :, :]),
             reads=(bVH,), dma_sem="DO0")

    unit_base = [sum(PSPAN[:u]) for u in range(4)]

    for u in range(4):
        span = PSPAN[u]
        nkc = 4 * span
        xsrc = dr["x_slots"].ap().rearrange("(c p) s -> p c s", p=128)[:, :, T * u:T * (u + 1)]
        p.op("sync", lambda e, xsrc=xsrc: e.dma_start(out=Y_B[:, :, :], in_=xsrc),
             writes=YA, dma_sem="DX")
        p.op("sync", lambda e, u=u: e.dma_start(
            out=RPC[:, :], in_=dr["ropeqC"].ap()[:, T * u:T * (u + 1)]),
            writes=(bRPC,), dma_sem="DRP")
        p.op("sync", lambda e, u=u: e.dma_start(
            out=RPS[:, :], in_=dr["ropeqS"].ap()[:, T * u:T * (u + 1)]),
            writes=(bRPS,), dma_sem="DRS")
        ln_stats(YCH, Y_B[:, :, :], wantmean=True)
        ln_apply(YCH, Y_B[:, :, :], BCH, B_H[:, :, :], grow=1, srow=0)
        for c in range(8):
            p.op("vector", lambda e, c=c: e.scalar_tensor_tensor(
                A_Z[:, c, :], B_H[:, c, :], GAIN[:, 2, c:c + 1], Y_B[:, c, :],
                op0=ALU.mult, op1=ALU.add),
                reads=(bBs[c], bGAIN) + ych(c), writes=(bAs[c],))
        if debug and u == 0:
            p.op("sync", lambda e: e.dma_start(out=taps["t_h"].ap(), in_=B_H[:, :, :]),
                 reads=BA, dma_sem="DO0")
        ln_stats(BCH, B_H[:, :, :], wantmean=False)
        rms_apply(BCH, B_H[:, :, :], CCH, C_M[:, :, :], N1W)
        if debug and u == 0:
            p.op("sync", lambda e: e.dma_start(out=taps["t_h1"].ap(), in_=C_M[:, :, :]),
                 reads=CA, dma_sem="DO0")
        for c2 in range(4):
            j2 = c2 % 2
            p.op("gpsimd", lambda e, c2=c2, j2=j2: e.dma_start(
                out=ep_pair(j2)[:, :, :],
                in_=dr["wqT"].ap().rearrange("(c p) m -> p c m", p=128)[
                    :, :, 256 * c2:256 * (c2 + 1)]),
                writes=(bE[2 * j2], bE[2 * j2 + 1]), dma_sem=f"DW{j2}")
            for ci in range(2):
                c = 2 * c2 + ci
                psn, psv = ("PS2", PS2[:, :]) if c % 2 == 0 else ("PS3", PS3[:, :])
                lhs = ep_pair(j2)[:, :, 128 * ci:128 * (ci + 1)]
                proj_group(psv, bP[psn], lhs, CCH, C_M[:, :, :],
                           extra=(bE[2 * j2], bE[2 * j2 + 1]))
                rope_pass(psv, bP[psn], qh3[:, c, :], (bYs[c],))
        if debug and u == 0:
            p.op("sync", lambda e: e.dma_start(out=taps["t_qh"].ap(), in_=qh3[:, :, :]),
                 reads=YA, dma_sem="DO0")
        p.op("gpsimd", lambda e, u=u: e.dma_start(
            out=EP[:, 0:4, :], in_=dr["maskq"].ap()[u, :, 0:4, :]),
            writes=(bE[0], bE[1]), dma_sem="DM0")
        p.op("gpsimd", lambda e, u=u: e.dma_start(
            out=EP[:, 4:8, :], in_=dr["maskq"].ap()[u, :, 4:8, :]),
            writes=(bE[2], bE[3]), dma_sem="DM1")

        PS0b16 = PS0.ap().bitcast(BF16)
        M4s = [M4, M4B]

        def emit_bias_chain(h):
            p.op("vector", lambda e: e.reduce_max(RMX[:, :], M4[:, :], axis=AXX),
                 reads=(bM4[0],), writes=(bRMX,))
            p.op("vector", lambda e: e.tensor_scalar(
                MB4[:, 0:1], RMX[:, 0:1], 0.0, 1.0, op0=ALU.max, op1=ALU.add),
                reads=(bRMX,), writes=(bMB4,))
            p.op("tensor", lambda e: e.transpose(
                PS0b16[0:1, 0:128], MB4[:, 0:1], IDENT[:, :]),
                reads=(bMB4, bconst), writes=(bP["PS0"],))
            p.op("vector", lambda e: e.reduce_max(MSC[0:1, 0:1], PS0b16[0:1, 0:128], axis=AXX),
                 reads=(bP["PS0"],), writes=(bMSC,))
            p.op("vector", lambda e: e.tensor_single_scalar(
                MSC[0:1, 1:2], MSC[0:1, 0:1], -0.125, op=ALU.mult),
                reads=(bMSC,), writes=(bMSC,))
            p.op("tensor", lambda e: e.matmul(
                PS1[:, 0:1], ONESB[0:1, 0:128], MSC[0:1, 1:2], start=True, stop=True),
                reads=(bMSC, bconst), writes=(bP["PS1"],))
            p.op("vector", lambda e: e.tensor_copy(MBIAS[:, 0:1], PS1[:, 0:1]),
                 reads=(bP["PS1"],), writes=(bMBIAS,))

        aunit = [0]
        for hp in range(H // 2):
            for qs in range(4):
                for pr in range(span // 2):
                    for par in range(2):
                        h = 2 * hp + par
                        hc = hp
                        r0 = par * 64
                        pspair = PS45 if aunit[0] % 2 == 0 else PS67
                        pn = ("PS45a", "PS45b") if aunit[0] % 2 == 0 else ("PS67a", "PS67b")
                        aunit[0] += 1
                        for hf in range(2):
                            blk = 2 * pr + hf
                            last = hf == 1
                            p.op("tensor", lambda e, last=last, qs=qs, blk=blk, hf=hf,
                                 hc=hc, r0=r0, pspair=pspair: e.matmul(
                                pspair[:, 512 * hf:512 * (hf + 1)],
                                qh3[r0:r0 + 64, hc, 128 * qs:128 * (qs + 1)],
                                K[r0:r0 + 64, hc, 512 * blk:512 * (blk + 1)],
                                start=True, stop=True, tile_position=(r0, 0)),
                                reads=(bYs[hc], bK), writes=(bP[pn[0]], bP[pn[1]]),
                                pre_only=not last)
                        p.op("vector", lambda e, pspair=pspair: e.reduce_max(
                            RMX[:, :], pspair[:, :], axis=AXX),
                             reads=(bP[pn[0]], bP[pn[1]]), writes=(bRMX,))
                        first = hp == 0 and pr == 0 and par == 0
                        if first:
                            p.op("vector", lambda e, qs=qs: e.tensor_copy(
                                M4[:, qs:qs + 1], RMX[:, :]),
                                 reads=(bRMX,), writes=(bM4[0],))
                        else:
                            p.op("vector", lambda e, qs=qs: e.tensor_max(
                                M4[:, qs:qs + 1], M4[:, qs:qs + 1], RMX[:, :]),
                                reads=(bRMX, bM4[0]), writes=(bM4[0],))
        emit_bias_chain(0)
        bring = [("PS67a", PS67[:, 0:512]), ("PS67b", PS67[:, 512:1024]),
                 ("PS45a", PS45[:, 0:512]), ("PS45b", PS45[:, 512:1024])]
        pring = [GPb16[:, 0, 0:512], GPb16[:, 0, 512:1024],
                 DRW.ap().bitcast(BF16)[:, 0, 0:512],
                 DRW.ap().bitcast(BF16)[:, 0, 512:1024]]

        def emit_B(h, kc):
            hc = h // 2
            r0 = (h % 2) * 64
            bn, bap = bring[(2 * kc + (h % 2)) % 4]
            p.op("tensor", lambda e, kc=kc, bap=bap, hc=hc, r0=r0: e.matmul(
                bap, K[r0:r0 + 64, hc, 128 * kc:128 * (kc + 1)],
                qh3[r0:r0 + 64, hc, :], start=True, stop=True,
                tile_position=(r0, 0)),
                reads=(bK, bYs[hc]), writes=(bP[bn],))
            pslot = (2 * kc + (h % 2)) % 4
            pT = pring[pslot]
            pbuf = bPT[pslot]
            p.op("scalar", lambda e, bap=bap, pT=pT, h=h: e.activation(
                pT, bap, AF.Exp, bias=MBIAS[:, 0:1], scale=CQ8),
                reads=(bP[bn], bconst, bMBIAS), writes=(pbuf,))
            j8 = kc - (nkc - 8)
            if j8 >= 0:
                p.op("vector", lambda e, pT=pT, j8=j8: e.tensor_mul(
                    pT, pT, EP[:, j8, :]),
                    reads=tuple(bE) + (pbuf,), writes=(pbuf,))

        def emit_PV(h, kc):
            ctxn = "PS2" if h % 2 == 0 else "PS3"
            ctxps = PS2 if h % 2 == 0 else PS3
            pslot = (2 * kc + (h % 2)) % 4
            pT = pring[pslot]
            pbuf = bPT[pslot]
            last = kc == nkc - 1
            p.op("tensor", lambda e, kc=kc, h=h, pT=pT, ctxps=ctxps, nkc=nkc: e.matmul(
                ctxps[0:65, :], VH[:, kc, 65 * h:65 * (h + 1)], pT,
                start=(kc == 0), stop=(kc == nkc - 1)),
                reads=(bVH, pbuf), writes=(bP[ctxn],), pre_only=not last)

        def emit_evac(h):
            hc = h // 2
            ctxn = "PS2" if h % 2 == 0 else "PS3"
            ctxps = PS2 if h % 2 == 0 else PS3
            p.op("scalar", lambda e, ctxps=ctxps: e.copy(RL[64:65, :], ctxps[64:65, :]),
                 reads=(bP[ctxn],), writes=(bRL,))
            rbn = "PS0" if h % 2 == 0 else "PS1"
            rbp = PS0 if h % 2 == 0 else PS1
            p.op("tensor", lambda e, rbp=rbp: e.matmul(
                rbp[0:64, :], ONESF[64:65, 0:64], RL[64:65, :], start=True, stop=True,
                tile_position=(64, 0)),
                reads=(bRL, bconst), writes=(bP[rbn],))
            p.op("vector", lambda e, rbp=rbp: e.reciprocal(DRW[0:64, 1, :], rbp[0:64, :]),
                 reads=(bP[rbn],), writes=(bD1,))
            if h % 2 == 0:
                p.op("vector", lambda e, ctxps=ctxps, hc=hc: e.tensor_mul(
                    C_M[0:64, hc, :], ctxps[0:64, :], DRW[0:64, 1, :]),
                    reads=(bP[ctxn], bD1), writes=(bCs[hc],))
            else:
                p.op("vector", lambda e, ctxps=ctxps: e.tensor_mul(
                    GPb16[0:64, 1, 0:512], ctxps[0:64, :], DRW[0:64, 1, :]),
                    reads=(bP[ctxn], bD1), writes=(bGPb,))
                p.op("sync", lambda e, hc=hc: e.dma_start(
                    out=C_M[64:128, hc, :], in_=GPb16[0:64, 1, 0:512]),
                    reads=(bGPb,), writes=(bCs[hc],), dma_sem="DK1")

        for hp in range(H // 2):
            h0, h1 = 2 * hp, 2 * hp + 1
            emit_B(h0, 0)
            emit_B(h1, 0)
            for kc in range(nkc):
                if kc == 0 and hp > 0:
                    emit_evac(2 * hp - 2)
                    emit_evac(2 * hp - 1)
                if kc + 1 < nkc:
                    emit_B(h0, kc + 1)
                    emit_B(h1, kc + 1)
                emit_PV(h0, kc)
                emit_PV(h1, kc)
        emit_evac(H - 2)
        emit_evac(H - 1)
        if debug and u == 0:
            p.op("sync", lambda e: e.dma_start(out=taps["t_ctx"].ap(), in_=C_M[:, :, :]),
                 reads=CA, dma_sem="DO0")

        for c2 in range(4):
            j2 = c2 % 2
            p.op("gpsimd", lambda e, c2=c2, j2=j2: e.dma_start(
                out=ep_pair(j2)[:, :, :],
                in_=dr["woT"].ap().rearrange("(c p) m -> p c m", p=128)[
                    :, :, 256 * c2:256 * (c2 + 1)]),
                writes=(bE[2 * j2], bE[2 * j2 + 1]), dma_sem=f"DW{j2}")
            for ci in range(2):
                c = 2 * c2 + ci
                psn = "PS45a" if c % 2 == 0 else "PS45b"
                psv = PS45[:, 0:512] if c % 2 == 0 else PS45[:, 512:1024]
                lhs = ep_pair(j2)[:, :, 128 * ci:128 * (ci + 1)]
                proj_group(psv, bP[psn], lhs, CCH, C_M[:, :, :],
                           extra=(bE[2 * j2], bE[2 * j2 + 1]))
                p.op("vector", lambda e, c=c, psv=psv: e.tensor_add(
                    Y_B[:, c, :], psv, B_H[:, c, :]),
                    reads=(bP[psn], bBs[c]), writes=ych(c))
                p.op("vector", lambda e, c=c, psv=psv: e.scalar_tensor_tensor(
                    A_Z[:, c, :], psv, GAIN[:, 2, c:c + 1], A_Z[:, c, :],
                    op0=ALU.mult, op1=ALU.add),
                    reads=(bP[psn], bGAIN, bAs[c]), writes=(bAs[c],))
        if debug and u == 0:
            p.op("sync", lambda e: e.dma_start(out=taps["t_y"].ap(), in_=Y_B[:, :, :]),
                 reads=YA, dma_sem="DO0")

        ln_stats(YCH, Y_B[:, :, :], wantmean=False)
        rms_apply(YCH, Y_B[:, :, :], CCH, C_M[:, :, :], N2W)
        for halfF in range(2):
            for fc16 in range(FC_MOE):
                fc = 16 * halfF + fc16
                j2 = fc % 2
                if fc16 % 2 == 0:
                    fb = (fc // 2) % 2
                    dst = EP[:, :, :] if fb == 0 else B_H[:, :, :]
                    wbufs = tuple(bE) if fb == 0 else BA
                    p.op("gpsimd", lambda e, fc=fc, dst=dst: e.dma_start(
                        out=dst,
                        in_=dr["w13T"].ap().rearrange("(c p) m -> p c m", p=128)[
                            :, :, 256 * fc:256 * (fc + 2)]),
                        writes=wbufs, dma_sem=f"DW{fb}")
                fb = (fc // 2) % 2
                wt = (EP if fb == 0 else B_H).ap().rearrange(
                    "p a b -> p (a b)").rearrange("p (c n) -> p c n", n=512)
                w13 = wt[:, :, 256 * (fc % 2):256 * (fc % 2) + 256]
                wbufs13 = tuple(bE) if fb == 0 else BA
                pu = "PS45a" if fc % 2 == 0 else "PS67a"
                pg = "PS45b" if fc % 2 == 0 else "PS67b"
                pub = (PS45 if fc % 2 == 0 else PS67)[:, 0:512]
                pgb = (PS45 if fc % 2 == 0 else PS67)[:, 512:1024]
                for k in range(8):
                    last = k == 7
                    p.op("tensor", lambda e, last=last, k=k, w13=w13, pub=pub: e.matmul(
                        pub, w13[:, k, 0:128], C_M[:, k, :], start=(k == 0), stop=last),
                        reads=wbufs13 + (CA if last else (bCs[k],)), writes=(bP[pu],),
                        pre_only=not last)
                for k in range(8):
                    last = k == 7
                    p.op("tensor", lambda e, last=last, k=k, w13=w13, pgb=pgb: e.matmul(
                        pgb, w13[:, k, 128:256], C_M[:, k, :], start=(k == 0), stop=last),
                        reads=wbufs13 + (CA if last else (bCs[k],)), writes=(bP[pg],),
                        pre_only=not last)
                slb = GPb16[:, 0, 0:512]
                p.op("scalar", lambda e, pub=pub, slb=slb: e.activation(slb, pub, AF.Silu),
                     reads=(bP[pu],), writes=(bGPa,))
                ffdst = Yb16[:, fc16 % 8, 512 * (fc16 // 8):512 * (fc16 // 8 + 1)]
                p.op("vector", lambda e, ffdst=ffdst, slb=slb, pgb=pgb: e.tensor_mul(
                    ffdst, slb, pgb),
                    reads=(bGPa, bP[pg]), writes=(bYs[fc16],))
            for c in range(8):
                if c % 2 == 0:
                    cb2 = (c // 2) % 2
                    dstw = (EP if cb2 == 0 else B_H).ap().rearrange(
                        "p a b -> p (a b)").rearrange("p (f n) -> p f n", n=256)[:, 0:16, :]
                    wbuf2 = tuple(bE) if cb2 == 0 else BA
                    p.op("gpsimd", lambda e, c=c, halfF=halfF, dstw=dstw: e.dma_start(
                        out=dstw,
                        in_=dr["w2T"].ap().rearrange("(f p) m -> p f m", p=128)[
                            :, 16 * halfF:16 * (halfF + 1), 128 * c:128 * (c + 2)]),
                        writes=wbuf2, dma_sem=f"DW{cb2}")
                cb2 = (c // 2) % 2
                wv = (EP if cb2 == 0 else B_H).ap().rearrange(
                    "p a b -> p (a b)").rearrange("p (f n) -> p f n", n=256)[
                    :, 0:16, 128 * (c % 2):128 * (c % 2) + 128]
                wrd = tuple(bE) if cb2 == 0 else BA
                psn = "PS2" if c % 2 == 0 else "PS3"
                psv = (PS2 if c % 2 == 0 else PS3)[:, :]
                for fc16 in range(FC_MOE):
                    last = fc16 == FC_MOE - 1
                    ffsrc = Yb16[:, fc16 % 8, 512 * (fc16 // 8):512 * (fc16 // 8 + 1)]
                    p.op("tensor", lambda e, last=last, fc16=fc16, wv=wv, psv=psv, ffsrc=ffsrc: e.matmul(
                        psv, wv[:, fc16, :], ffsrc, start=(fc16 == 0), stop=last),
                        reads=wrd + (YA if last else (bYs[fc16],)),
                        writes=(bP[psn],), pre_only=not last)
                p.op("vector", lambda e, c=c, psv=psv: e.scalar_tensor_tensor(
                    A_Z[:, c, :], psv, GAIN[:, 2, c:c + 1], A_Z[:, c, :],
                    op0=ALU.mult, op1=ALU.add),
                    reads=(bP[psn], bGAIN, bAs[c]), writes=(bAs[c],))
        if debug and u == 0:
            p.op("sync", lambda e: e.dma_start(out=taps["t_x1"].ap(), in_=A_Z[:, :, :]),
                 reads=AA, dma_sem="DO0")

        ln_stats(ACH, A_Z[:, :, :], wantmean=True)
        ln_apply(ACH, A_Z[:, :, :], BCH, B_H[:, :, :], grow=4, srow=3)
        if debug and u == 0:
            p.op("sync", lambda e: e.dma_start(out=taps["t_g"].ap(), in_=B_H[:, :, :]),
                 reads=BA, dma_sem="DO0")
        for qs in range(4):
            gout = PS2[:, 8 * qs:8 * (qs + 1)]
            for k in range(8):
                p.op("tensor", lambda e, k=k, qs=qs, gout=gout: e.matmul(
                    gout, B_H[:, k, 128 * qs:128 * (qs + 1)], WGT[:, k, :],
                    start=(k == 0), stop=False),
                    reads=(bBs[k], bWG), writes=(bP["PS2"],), pre_only=True)
            p.op("tensor", lambda e, qs=qs, gout=gout: e.matmul(
                gout, ONESB[0:1, 0:128], BGR[0:1, :], start=False, stop=True),
                reads=tuple(bBs) + (bWG, bconst), writes=(bP["PS2"],))
            p.op("vector", lambda e, gout=gout: e.reduce_max(
                RMX[:, :], gout, axis=AXX, negate=True),
                reads=(bP["PS2"],), writes=(bRMX,))
            p.op("vector", lambda e, gout=gout: e.tensor_scalar(
                GP[:, 0, 0:8], gout, RMX[:, 0:1], 0.0, op0=ALU.add, op1=ALU.is_ge),
                reads=(bP["PS2"], bRMX), writes=(bGPa,))
            p.op("vector", lambda e, gout=gout: e.scalar_tensor_tensor(
                GP[:, 1, 0:8], GP[:, 0, 0:8], -1e30, gout,
                op0=ALU.mult, op1=ALU.add),
                reads=(bGPa, bP["PS2"]), writes=(bGPb,))
            p.op("vector", lambda e: e.reduce_max(
                M4[:, 0:1], GP[:, 1, 0:8], axis=AXX, negate=True),
                reads=(bGPb,), writes=(bM4[0],))
            p.op("scalar", lambda e, gout=gout: e.activation(
                GP[:, 0, 0:8], gout, AF.Exp, bias=RMX[:, 0:1], scale=1.0),
                reads=(bP["PS2"], bRMX), writes=(bGPa,))
            p.op("vector", lambda e: e.tensor_sub(M4[:, 1:2], M4[:, 0:1], RMX[:, 0:1]),
                 reads=(bM4[0], bRMX), writes=(bM4[0],))
            p.op("scalar", lambda e: e.activation(
                M4[:, 2:3], M4[:, 1:2], AF.Exp, bias=0.0, scale=CON[:, 2:3]),
                reads=(bM4[0], bconst), writes=(bM4[0],))
            p.op("vector", lambda e: e.tensor_scalar_add(M4[:, 2:3], M4[:, 2:3], 1.0),
                 reads=(bM4[0],), writes=(bM4[0],))
            p.op("vector", lambda e: e.reciprocal(M4[:, 3:4], M4[:, 2:3]),
                 reads=(bM4[0],), writes=(bM4[0],))
            p.op("vector", lambda e, gout=gout: e.tensor_scalar(
                GP[:, 1, 0:8], gout, M4[:, 0:1], 0.0, op0=ALU.add, op1=ALU.is_ge),
                reads=(bP["PS2"], bM4[0]), writes=(bGPb,))
            p.op("vector", lambda e, qs=qs: e.scalar_tensor_tensor(
                CMB[:, qs, :], GP[:, 0, 0:8], M4[:, 3:4], GP[:, 1, 0:8],
                op0=ALU.mult, op1=ALU.mult),
                reads=(bGPa, bGPb, bM4[0]), writes=(bCMB,))
        if debug and u == 0:
            p.op("sync", lambda e: e.dma_start(out=taps["t_comb"].ap(), in_=CMB[:, :, :]),
                 reads=(bCMB,), dma_sem="DO0")
        for qs in range(4):
            p.op("vector", lambda e, qs=qs: e.tensor_copy(CMBB[:, :], CMB[:, qs, :]),
                 reads=(bCMB,), writes=(bCMBB,))
            p.op("tensor", lambda e, qs=qs: e.transpose(
                PS3b16[0:8, 128 * qs:128 * (qs + 1)], CMBB[:, :], IDENT[:, :]),
                reads=(bCMBB, bconst), writes=(bP["PS3"],))
        p.op("vector", lambda e: e.tensor_copy(SH8[0:8, 0:512], PS3b16[0:8, 0:512]),
             reads=(bP["PS3"],), writes=(bSH,))

        for ex in range(NE):
            p.op("sync", lambda e, ex=ex: e.dma_start(
                out=SH8[0:1, 512:1024], in_=SH8[ex:ex + 1, 0:512]),
                reads=(bSH,), writes=(bSH,), dma_sem="DK2")
            cbn = "PS0" if ex % 2 == 0 else "PS1"
            cbp = PS0 if ex % 2 == 0 else PS1
            p.op("tensor", lambda e, cbp=cbp: e.matmul(
                cbp[:, :], ONESB[0:1, :], SH8[0:1, 512:1024], start=True, stop=True),
                reads=(bSH, bconst), writes=(bP[cbn],))
            for fc in range(FC_MOE):
                if fc % 2 == 0:
                    fb = (fc // 2) % 2
                    dstw = EP[:, :, :] if fb == 0 else C_M[:, :, :]
                    wbufs = tuple(bE) if fb == 0 else CA
                    p.op("gpsimd", lambda e, ex=ex, fc=fc, dstw=dstw: e.dma_start(
                        out=dstw,
                        in_=dr["wupC"].ap()[ex].rearrange("(c p) m -> p c m", p=128)[
                            :, :, 256 * fc:256 * (fc + 2)]),
                        writes=wbufs, dma_sem=f"DW{fb}")
                fb = (fc // 2) % 2
                wtm = (EP if fb == 0 else C_M).ap().rearrange(
                    "p a b -> p (a b)").rearrange("p (c n) -> p c n", n=512)
                wup = wtm[:, :, 256 * (fc % 2):256 * (fc % 2) + 256]
                wbufsu = tuple(bE) if fb == 0 else CA
                pu = "PS45a" if fc % 2 == 0 else "PS67a"
                pg = "PS45b" if fc % 2 == 0 else "PS67b"
                pub = (PS45 if fc % 2 == 0 else PS67)[:, 0:512]
                pgb = (PS45 if fc % 2 == 0 else PS67)[:, 512:1024]
                for k in range(8):
                    last = k == 7
                    p.op("tensor", lambda e, last=last, k=k, wup=wup, pub=pub: e.matmul(
                        pub, wup[:, k, 0:128], B_H[:, k, :], start=(k == 0), stop=last),
                        reads=wbufsu + (BA if last else (bBs[k],)), writes=(bP[pu],),
                        pre_only=not last)
                for k in range(8):
                    last = k == 7
                    p.op("tensor", lambda e, last=last, k=k, wup=wup, pgb=pgb: e.matmul(
                        pgb, wup[:, k, 128:256], B_H[:, k, :], start=(k == 0), stop=last),
                        reads=wbufsu + (BA if last else (bBs[k],)), writes=(bP[pg],),
                        pre_only=not last)
                slb = GPb16[:, 0, 0:512]
                p.op("scalar", lambda e, pgb=pgb, slb=slb: e.activation(slb, pgb, AF.Silu),
                     reads=(bP[pg],), writes=(bGPa,))
                p.op("vector", lambda e, slb=slb, pub=pub: e.tensor_mul(
                    GP[:, 1, :], slb, pub),
                    reads=(bGPa, bP[pu]), writes=(bGPb,))
                hhdst = Yb16[:, fc % 8, 512 * (fc // 8):512 * (fc // 8 + 1)]
                p.op("vector", lambda e, hhdst=hhdst, cbp=cbp: e.tensor_mul(
                    hhdst, GP[:, 1, :], cbp[:, :]),
                    reads=(bGPb, bP[cbn]), writes=(bYs[fc],))
            for c in range(8):
                if c % 2 == 0:
                    cb2 = (c // 2) % 2
                    dstw = (EP if cb2 == 0 else C_M).ap().rearrange(
                        "p a b -> p (a b)").rearrange("p (f n) -> p f n", n=256)
                    wbufsd = tuple(bE) if cb2 == 0 else CA
                    p.op("gpsimd", lambda e, ex=ex, c=c, dstw=dstw: e.dma_start(
                        out=dstw,
                        in_=dr["wdn"].ap()[ex].rearrange("(f p) m -> p f m", p=128)[
                            :, :, 128 * c:128 * (c + 2)]),
                        writes=wbufsd, dma_sem=f"DM{cb2}")
                cb2 = (c // 2) % 2
                wv = (EP if cb2 == 0 else C_M).ap().rearrange(
                    "p a b -> p (a b)").rearrange("p (f n) -> p f n", n=256)[
                    :, :, 128 * (c % 2):128 * (c % 2) + 128]
                wrd = tuple(bE) if cb2 == 0 else CA
                psn = "PS2" if c % 2 == 0 else "PS3"
                psv = (PS2 if c % 2 == 0 else PS3)[:, :]
                for fc in range(FC_MOE):
                    last = fc == FC_MOE - 1
                    hhsrc = Yb16[:, fc % 8, 512 * (fc // 8):512 * (fc // 8 + 1)]
                    p.op("tensor", lambda e, last=last, fc=fc, wv=wv, psv=psv, hhsrc=hhsrc: e.matmul(
                        psv, wv[:, fc, :], hhsrc, start=(fc == 0), stop=last),
                        reads=wrd + (YA if last else (bYs[fc],)),
                        writes=(bP[psn],), pre_only=not last)
                p.op("vector", lambda e, c=c, psv=psv: e.scalar_tensor_tensor(
                    A_Z[:, c, :], psv, GAIN[:, 5, c:c + 1], A_Z[:, c, :],
                    op0=ALU.mult, op1=ALU.add),
                    reads=(bP[psn], bGAIN, bAs[c]), writes=(bAs[c],))

        dsem = "DO0" if u % 2 == 0 else "DO1"
        p.op("sync", lambda e, u=u: e.dma_start(out=yout.ap()[u], in_=A_Z[:, :, :]),
             reads=AA, dma_sem=dsem)

    totals = [(n, p.cnt[n]) for n in ("DO0", "DO1") if p.cnt[n] > 0]
    def tail(e, totals=totals, hs={n: p.handles[n] for n, _ in totals}):
        for n, v in totals:
            e.wait_ge(hs[n], v)
    p.raw("sync", tail)

    with nc.Block() as block:
        for eng in ENGINES:
            def runner(e, lst=p.ops[eng]):
                for f in lst:
                    f(e)
            getattr(block, eng)(runner)
    p.close_sems()
    return nc


def _rope_tables():
    pos = np.arange(S, dtype=np.float64)
    kk = np.arange(64) // 2
    ang = pos[None, :] / np.power(10000.0, 2.0 * kk.astype(np.float64) / 64.0)[:, None]
    cos = np.cos(ang)
    sinr = np.sin(ang)
    CT = np.empty((64, S))
    ST = np.empty((64, S))
    for n in range(64):
        m = n % 32
        CT[n] = cos[2 * m]
        ST[n] = sinr[2 * m] * (-1.0 if n < 32 else 1.0)
    CT2 = np.concatenate([CT, CT], axis=0).astype(np.float32)
    ST2 = np.concatenate([ST, ST], axis=0).astype(np.float32)
    return np.ascontiguousarray(CT2), np.ascontiguousarray(ST2)


def _deinterleave_perm():
    perm = np.empty(D, dtype=np.int64)
    for h in range(H):
        base = 64 * h
        perm[base:base + 32] = base + 2 * np.arange(32)
        perm[base + 32:base + 64] = base + 2 * np.arange(32) + 1
    return perm


def _fm(v):
    return np.ascontiguousarray(v.reshape(8, 128).T.astype(np.float32))


def stage_inputs(inputs):
    x = np.asarray(inputs["L0_attn_x"], np.float32)
    perm = _deinterleave_perm()
    CT2, ST2 = _rope_tables()
    wq = np.asarray(inputs["L0_attn_wq"], np.float32)[perm, :]
    wk = np.asarray(inputs["L0_attn_wk"], np.float32)[perm, :]
    wqT = np.ascontiguousarray(wq.T)
    wkT = np.ascontiguousarray(wk.T)
    wvT = np.ascontiguousarray(np.asarray(inputs["L0_attn_wv"], np.float32).T)
    woT = np.ascontiguousarray(np.asarray(inputs["L0_attn_wo"], np.float32).T)
    w1T = np.asarray(inputs["L0_attn_w1"], np.float32).T
    w3T = np.asarray(inputs["L0_attn_w3"], np.float32).T
    w13 = np.empty((D, FC_FFN * 256), np.float32)
    for fc in range(FC_FFN):
        w13[:, 256 * fc:256 * fc + 128] = w1T[:, 128 * fc:128 * (fc + 1)]
        w13[:, 256 * fc + 128:256 * (fc + 1)] = w3T[:, 128 * fc:128 * (fc + 1)]
    w2T = np.ascontiguousarray(np.asarray(inputs["L0_attn_w2"], np.float32).T)
    wup = np.asarray(inputs["L0_ffn_Wup"], np.float32)
    wupC = np.empty((NE, D, FC_MOE * 256), np.float32)
    for fc in range(FC_MOE):
        wupC[:, :, 256 * fc:256 * fc + 128] = wup[:, :, 128 * fc:128 * (fc + 1)]
        wupC[:, :, 256 * fc + 128:256 * (fc + 1)] = wup[:, :, 2048 + 128 * fc:2048 + 128 * (fc + 1)]
    wdn = np.ascontiguousarray(np.asarray(inputs["L0_ffn_Wdown"], np.float32))
    wmod = np.ascontiguousarray(np.asarray(inputs["L0_wmod"], np.float32))
    wgT = np.ascontiguousarray(np.asarray(inputs["L0_ffn_Wg"], np.float32).T)
    bg_row = np.asarray(inputs["L0_ffn_bg"], np.float32).reshape(1, 8)
    n1w = _fm(np.asarray(inputs["L0_attn_norm1_w"], np.float32))
    n2w = _fm(np.asarray(inputs["L0_attn_norm2_w"], np.float32))
    consts = np.broadcast_to(np.array(
        [1e-5, 1.0 / 1024.0, -1.0, 0.125, 0.25, 1.0, -1e30, 0.0], np.float32), (128, 8)).copy()
    cr1 = np.array([[-1.0, 0, 0, 0, 0, 0, 0, 0]], np.float32)
    onesf = np.ones((128, 128), np.float32)
    identf = np.eye(128, dtype=np.float32)

    def maskq_for(par):
        m = np.ones((4, 128, 8, T), np.float32)
        for u in range(4):
            span = PSPAN[u]
            tspan = OWN[par][u] + 1
            nkc = 4 * span
            for j8 in range(8):
                kc = nkc - 8 + j8
                blk = kc // 4
                if blk > tspan - 1:
                    m[u, :, j8, :] = 0.0
                    continue
                if blk < tspan - 1:
                    continue
                off = (kc % 4) * 128
                kpos = off + np.arange(128)[:, None]
                qpos = np.arange(T)[None, :]
                m[u, :, j8, :] = (kpos <= qpos).astype(np.float32)
        return m

    maskq = [maskq_for(0), maskq_for(1)]

    in_maps = []
    for core in range(8):
        b = core // 2
        par = core % 2
        xT = np.ascontiguousarray(x[b].T)
        xs = np.concatenate([xT[:, T * t:T * (t + 1)] for t in OWN[par]], axis=1)
        rqC = np.concatenate([CT2[:, T * t:T * (t + 1)] for t in OWN[par]], axis=1)
        rqS = np.concatenate([ST2[:, T * t:T * (t + 1)] for t in OWN[par]], axis=1)
        in_maps.append({
            "x_full": xT, "x_slots": np.ascontiguousarray(xs),
            "ropeC": CT2, "ropeS": ST2,
            "ropeqC": np.ascontiguousarray(rqC), "ropeqS": np.ascontiguousarray(rqS),
            "wqT": wqT, "wkT": wkT, "wvT": wvT, "woT": woT,
            "w13T": w13, "w2T": w2T, "wupC": wupC, "wdn": wdn,
            "wmod": wmod, "wgT": wgT, "bg_row": bg_row,
            "c_fm": _fm(np.asarray(inputs["L0_c"], np.float32)[b]),
            "n1w": n1w, "n2w": n2w, "consts": consts, "cr1": cr1,
            "onesf": onesf, "identf": identf, "maskq": maskq[par],
        })
    return in_maps


def gather_output(results):
    out = np.empty((NB, S, D), np.float32)
    for core in range(8):
        b = core // 2
        par = core % 2
        y = results[core]["yout"]
        for u, t in enumerate(OWN[par]):
            out[b, T * t:T * (t + 1), :] = y[u].transpose(2, 1, 0).reshape(T, D)
    return out


def kernel(**inputs):
    in_maps = stage_inputs(inputs)
    nc = build_nc(debug=DEBUG)
    res = run_bass_kernel_spmd(nc, in_maps, core_ids=list(range(8)))
    return gather_output(res.results)


# version 09
# speedup vs baseline: 1.1175x; 1.0269x over previous
import math
import numpy as np
import ml_dtypes
import concourse.bass as bass
import concourse.mybir as mybir
from concourse.bass_utils import run_bass_kernel_spmd

F32 = mybir.dt.float32
BF16 = mybir.dt.bfloat16
AF = mybir.ActivationFunctionType
ALU = mybir.AluOpType
AXX = mybir.AxisListType.X

D = 1024
S = 4096
NB = 4
H = 16
T = 512
NT = S // T
OWN = [[0, 3, 4, 7], [1, 2, 5, 6]]
PSPAN = [2, 4, 6, 8]
NUNIT = sum(PSPAN)
NE = 8
FC_FFN = 32
FC_MOE = 16
ENGINES = ("tensor", "vector", "scalar", "sync", "gpsimd")

DEBUG = False


class Buf:
    __slots__ = ("name", "w", "r")

    def __init__(self, name):
        self.name = name
        self.w = None
        self.r = {}


class Prog:
    def __init__(self, nc):
        self.nc = nc
        self.ops = {e: [] for e in ENGINES}
        self.cnt = {}
        self.lastw = {}
        self._stack = []
        self.handles = {}
        self.prev_dma = {}

    def sem(self, name):
        g = self.nc.semaphore(name)
        h = g.__enter__()
        self._stack.append(g)
        self.cnt[name] = 0
        self.handles[name] = h
        return name

    def _waits(self, eng, toks):
        best = {}
        for t in toks:
            if t is None:
                continue
            s, v = t
            if best.get(s, -1) < v:
                best[s] = v
        out = []
        for s, v in best.items():
            if s == eng:
                pass
            if self.lastw.get((eng, s), 0) < v:
                self.lastw[(eng, s)] = v
                out.append((s, v))
        return out

    def op(self, eng, build, reads=(), writes=(), dma_sem=None, pre_only=False):
        if dma_sem is not None:
            reads = tuple(reads) + tuple(
                b for b in self.prev_dma.get(dma_sem, ()) if b not in writes)
            self.prev_dma[dma_sem] = tuple(writes)
            sink = self.prev_dma.setdefault(("sink", dma_sem), Buf("sink" + dma_sem))
            writes = tuple(writes) + (sink,)
        toks = []
        for b in reads:
            toks.append(b.w)
        for b in writes:
            toks.append(b.w)
            toks.extend(b.r.items())
        wl = self._waits(eng, toks)
        tok = None
        if not pre_only:
            sem = dma_sem if dma_sem is not None else eng
            self.cnt[sem] += 16 if dma_sem is not None else 1
            tok = (sem, self.cnt[sem])
        need = set(s for s, _ in wl)
        if tok is not None:
            need.add(tok[0])
        hs = {n: self.handles[n] for n in need}

        def closure(e, wl=wl, build=build, tok=tok, hs=hs, dma=dma_sem is not None):
            for s, v in wl:
                e.wait_ge(hs[s], v)
            ins = build(e)
            if tok is not None:
                ins.then_inc(hs[tok[0]], 16 if dma else 1)

        self.ops[eng].append(closure)
        if not pre_only:
            for b in reads:
                if b.r.get(tok[0], -1) < tok[1]:
                    b.r[tok[0]] = tok[1]
            for b in writes:
                b.w = tok
                b.r = {}
        return tok

    def raw(self, eng, fn):
        self.ops[eng].append(fn)

    def close_sems(self):
        for g in reversed(self._stack):
            g.__exit__(None, None, None)


def build_nc(debug=False):
    nc = bass.Bass("TRN2", target_bir_lowering=False)

    dr = {}

    def din(name, shape):
        dr[name] = nc.dram_tensor(name, list(shape), F32, kind="ExternalInput")

    din("x_full", (D, S))
    din("x_slots", (D, 4 * T))
    din("ropeC", (128, S))
    din("ropeS", (128, S))
    din("ropeqC", (128, 4 * T))
    din("ropeqS", (128, 4 * T))
    din("ropeCl", (128, S))
    din("ropeSl", (128, S))
    din("ropeqCl", (128, 4 * T))
    din("ropeqSl", (128, 4 * T))
    din("wqT", (D, D))
    din("wkT", (D, D))
    din("wvT", (D, D))
    din("woT", (D, D))
    din("w13T", (D, FC_FFN * 256))
    din("w2T", (4096, D))
    din("wupC", (NE, D, FC_MOE * 256))
    din("wdn", (NE, 2048, D))
    din("wmod", (D, 6 * D))
    din("wgT", (D, 8))
    din("bg_row", (1, 8))
    din("c_fm", (128, 8))
    din("n1w", (128, 8))
    din("n2w", (128, 8))
    din("consts", (128, 8))
    din("cr1", (1, 8))
    din("onesf", (128, 128))
    din("identf", (128, 128))
    din("maskq", (4, 128, 8, T))
    yout = nc.dram_tensor("yout", [4, 128, 8, T], F32, kind="ExternalOutput")

    taps = {}
    if debug:
        def tap(name, shape, dt=F32):
            taps[name] = nc.dram_tensor(name, list(shape), dt, kind="ExternalOutput")
        tap("t_K", [128, 8, S], BF16)
        tap("t_VH", [128, 32, 1040], BF16)
        tap("t_h", [128, 8, T], BF16)
        tap("t_h1", [128, 8, T], BF16)
        tap("t_qh", [128, 8, T], BF16)
        tap("t_ctx", [128, 8, T], BF16)
        tap("t_y", [128, 8, T], BF16)
        tap("t_x1", [128, 8, T], F32)
        tap("t_g", [128, 8, T], BF16)
        tap("t_comb", [128, 4, 8], F32)
        tap("t_mod", [128, 48], F32)

    _cursor = [0]
    _rem = [nc.sbuf_bytes_remaining]

    def sb(name, shape, dt):
        t = nc.alloc_sbuf_tensor(name, shape, dt)
        rem = nc.sbuf_bytes_remaining
        off = _cursor[0]
        _cursor[0] += _rem[0] - rem
        _rem[0] = rem
        sb.off[name] = off
        return t
    sb.off = {}
    K = sb("K", [128, 8, S], BF16)
    VH = sb("VH", [128, 32, 1040], BF16)
    A_Z = sb("A_Z", [128, 8, T], F32)
    YB = sb("YB", [128, 8, 2 * T], BF16)
    B_H = sb("B_H", [128, 8, T], BF16)
    C_M = sb("C_M", [128, 8, T], BF16)
    DRW = sb("DRW", [128, 2, T], F32)
    EP = sb("EP", [128, 8, T], BF16)
    GP = sb("GP", [128, 2, T], F32)
    PB = sb("PB", [128, 4, T], BF16)
    GAIN = sb("GAIN", [128, 6, 8], F32)
    N1W = sb("N1W", [128, 8], F32)
    N2W = sb("N2W", [128, 8], F32)
    CON = sb("CON", [128, 8], F32)
    CR1 = sb("CR1", [1, 8], F32)
    ONESB = sb("ONESB", [128, 128], BF16)
    ONESF = sb("ONESF", [128, 128], F32)
    IDENTF = sb("IDENTF", [128, 128], F32)
    SH8 = sb("SH8", [8, 1024], BF16)
    RL = sb("RL", [65, T], F32)
    M4 = sb("M4", [128, 4], F32)
    M4B = sb("M4B", [128, 4], F32)
    RMX = sb("RMX", [128, 1], F32)
    MB4 = sb("MB4", [128, 1], F32)
    MSC = sb("MSC", [1, 4], F32)
    MBIAS = sb("MBIAS", [128, 16], F32)
    CMB = sb("CMB", [128, 4, 8], F32)
    CMBB = sb("CMBB", [128, 8], BF16)
    BGR = sb("BGR", [1, 8], BF16)
    CF = sb("CF", [128, 8], BF16)
    CF0 = sb("CF0", [128, 8], F32)
    MODS = sb("MODS", [128, 48], F32)
    RPC = sb("RPC", [128, T], F32)
    RPS = sb("RPS", [128, T], F32)
    WGT = sb("WGT", [128, 8, 8], BF16)

    ap_ = nc.alloc_psum_tensor
    PS0 = ap_("PS0", [128, 512], F32)
    PS1 = ap_("PS1", [128, 512], F32)
    PS2 = ap_("PS2", [128, 512], F32)
    PS3 = ap_("PS3", [128, 512], F32)
    PS45 = ap_("PS45", [128, 1024], F32)
    PS67 = ap_("PS67", [128, 1024], F32)

    p = Prog(nc)
    for e in ENGINES:
        p.sem(e)
    for n in ("DX", "DRP", "DRS", "DK1", "DK2", "DO0", "DO1",
              "DW0", "DW1", "DW2", "DW3", "DW4", "DW5",
              "DM0", "DM1", "DCN", "DCG"):
        p.sem(n)

    bK = Buf("K"); bVH = Buf("VH")
    bD0a = Buf("D0a"); bD0b = Buf("D0b")
    bD1 = Buf("D1")
    bYs = [Buf(f"Ys{i}") for i in range(16)]
    bAs = [Buf(f"As{c}") for c in range(8)]
    bBs = [Buf(f"Bs{c}") for c in range(8)]
    bCs = [Buf(f"Cs{c}") for c in range(8)]
    YA, AA, BA, CA = tuple(bYs), tuple(bAs), tuple(bBs), tuple(bCs)

    YCH = [(bYs[c],) for c in range(8)]
    ACH = [(bAs[c],) for c in range(8)]
    BCH = [(bBs[c],) for c in range(8)]
    CCH = [(bCs[c],) for c in range(8)]
    bGPa = Buf("GPa"); bGPb = Buf("GPb")
    bE = [Buf(f"E{j}") for j in range(4)]
    bP = {n: Buf(n) for n in ("PS0", "PS1", "PS2", "PS3",
                              "PS45a", "PS45b", "PS67a", "PS67b")}
    bSH = Buf("SH8"); bRL = Buf("RL"); bM4 = [Buf("M4"), Buf("M4B")]; bRMX = Buf("RMX")
    bMB4 = Buf("MB4"); bCMB = Buf("CMB"); bCMBB = Buf("CMBB")
    bMSC = Buf("MSC"); bMBIAS = Buf("MBIAS")
    bMODS = Buf("MODS"); bGAIN = Buf("GAIN"); bCF = Buf("CF")
    bPB = [Buf(f"PB{i}") for i in range(4)]
    bRPC = Buf("RPC"); bRPS = Buf("RPS"); bWG = Buf("WGT")
    bconst = Buf("const")

    Yb16 = YB[:, :, :]
    qh3 = YB[:, :, 0:512]
    Ylow = YB[:, :, 0:512]

    def ep_slot(j):
        return EP[:, 2 * j:2 * j + 2, :].rearrange("p a b -> p (a b)").rearrange(
            "p (c n) -> p c n", n=128)

    def ep_pair(j2):
        return EP[:, 4 * j2:4 * j2 + 4, :].rearrange("p a b -> p (a b)").rearrange(
            "p (c n) -> p c n", n=256)

    def b_half(j):
        return B_H[:, 4 * j:4 * j + 4, :].rearrange("p a b -> p (a b)").rearrange(
            "p (c n) -> p c n", n=128)

    CEPS = CON[:, 0:1]
    CK1024 = CON[:, 1:2]
    CQ8 = CON[:, 3:4]
    CONE = CON[:, 5:6]

    for nm, dst in (("consts", CON), ("cr1", CR1), ("n1w", N1W), ("n2w", N2W),
                    ("onesf", ONESF), ("identf", IDENTF), ("c_fm", CF0)):
        p.op("sync", lambda e, nm=nm, dst=dst: e.dma_start(out=dst[:, :], in_=dr[nm].ap()),
             writes=(bconst,), dma_sem="DCN")
    for nm, dst in (("onesf", ONESB),
                    ("bg_row", BGR)):
        p.op("gpsimd", lambda e, nm=nm, dst=dst: e.dma_start(out=dst[:, :], in_=dr[nm].ap()),
             writes=(bconst,), dma_sem="DCG")
    p.op("gpsimd", lambda e: e.dma_start(
        out=WGT[:, :, :], in_=dr["wgT"].ap().rearrange("(c p) m -> p c m", p=128)),
        writes=(bWG,), dma_sem="DCG")

    vones = VH.ap().rearrange("p t (h x) -> p t h x", x=65)[:, :, :, 64:65]
    p.op("vector", lambda e: e.memset(vones, 1.0), writes=(bVH,))

    p.op("scalar", lambda e: e.activation(CF[:, :], CF0[:, :], AF.Silu),
         reads=(bconst,), writes=(bCF,))

    for j in range(48):
        sl = j % 2
        p.op("gpsimd", lambda e, j=j, sl=sl: e.dma_start(
            out=ep_slot(sl)[:, :, :], in_=dr["wmod"].ap().rearrange(
                "(c p) m -> p c m", p=128)[:, :, 128 * j:128 * (j + 1)]),
            writes=(bE[sl],), dma_sem=f"DW{sl}")
        for k in range(8):
            last = k == 7
            p.op("tensor", lambda e, last=last, j=j, k=k, sl=sl: e.matmul(
                PS2[:, j:j + 1], ep_slot(sl)[:, k, :], CF[:, k:k + 1],
                start=(k == 0), stop=last),
                reads=(bE[sl], bCF), writes=(bP["PS2"],), pre_only=not last)
    p.op("vector", lambda e: e.tensor_copy(MODS[:, :], PS2[:, 0:48]),
         reads=(bP["PS2"],), writes=(bMODS,))
    if debug:
        p.op("sync", lambda e: e.dma_start(out=taps["t_mod"].ap(), in_=MODS[:, :]),
             reads=(bMODS,), dma_sem="DO0")
    p.op("vector", lambda e: e.tensor_copy(GAIN[:, 0, :], MODS[:, 0:8]),
         reads=(bMODS,), writes=(bGAIN,))
    p.op("vector", lambda e: e.tensor_scalar_add(GAIN[:, 1, :], MODS[:, 8:16], 1.0),
         reads=(bMODS,), writes=(bGAIN,))
    p.op("vector", lambda e: e.tensor_copy(GAIN[:, 3, :], MODS[:, 24:32]),
         reads=(bMODS,), writes=(bGAIN,))
    p.op("vector", lambda e: e.tensor_scalar_add(GAIN[:, 4, :], MODS[:, 32:40], 1.0),
         reads=(bMODS,), writes=(bGAIN,))
    for src, dsti in ((16, 2), (40, 5)):
        p.op("scalar", lambda e, src=src: e.activation(
            GP[:, 0, 0:8], MODS[:, src:src + 8], AF.Abs),
            reads=(bMODS,), writes=(bGPa,))
        p.op("vector", lambda e: e.tensor_scalar_max(GP[:, 0, 0:8], GP[:, 0, 0:8], 0.25),
             reads=(bGPa,), writes=(bGPa,))
        p.op("scalar", lambda e, src=src: e.sign(GP[:, 1, 0:8], MODS[:, src:src + 8]),
             reads=(bMODS,), writes=(bGPb,))
        p.op("vector", lambda e, dsti=dsti: e.tensor_mul(
            GAIN[:, dsti, :], GP[:, 0, 0:8], GP[:, 1, 0:8]),
            reads=(bGPa, bGPb), writes=(bGAIN,))

    def ln_stats(srcbufs, src32, wantmean, ones_sum=None):
        if wantmean:
            osum = ones_sum if ones_sum is not None else ONESF
            for k in range(8):
                last = k == 7
                p.op("tensor", lambda e, last=last, k=k, src32=src32, osum=osum: e.matmul(
                    PS0[:, :], osum[:, :], src32[:, k, :], start=(k == 0), stop=last),
                    reads=(tuple({b for t in srcbufs for b in t}) + (bconst,)) if last
                    else (srcbufs[k] + (bconst,)),
                    writes=(bP["PS0"],), pre_only=not last)
        for c in range(8):
            p.op("scalar", lambda e, c=c, src32=src32: e.activation(
                C_M[:, c, :], src32[:, c, :], AF.Square),
                 reads=srcbufs[c], writes=(bCs[c],))
        for k in range(8):
            last = k == 7
            p.op("tensor", lambda e, last=last, k=k: e.matmul(
                PS1[:, :], ONESB[:, :], C_M[:, k, :], start=(k == 0), stop=last),
                reads=(CA + (bconst,)) if last else (bCs[k], bconst),
                writes=(bP["PS1"],), pre_only=not last)
        if wantmean:
            pass
            p.op("vector", lambda e: e.tensor_scalar_mul(DRW[:, 1, :], PS0[:, :], 1.0 / 1024.0),
                 reads=(bP["PS0"],), writes=(bD1,))
            p.op("vector", lambda e: e.tensor_mul(GP[:, 0, :], DRW[:, 1, :], DRW[:, 1, :]),
                 reads=(bD1,), writes=(bGPa,))
            p.op("vector", lambda e: e.scalar_tensor_tensor(
                GP[:, 0, :], PS1[:, :], 1.0 / 1024.0, GP[:, 0, :],
                op0=ALU.mult, op1=ALU.subtract),
                reads=(bP["PS1"], bGPa), writes=(bGPa,))
            p.op("scalar", lambda e: e.activation(GP[:, 0, :], GP[:, 0, :], AF.Sqrt,
                                                  bias=CEPS, scale=CONE),
                 reads=(bGPa, bconst), writes=(bGPa,))
            p.op("vector", lambda e: e.reciprocal(DRW[:, 0, :], GP[:, 0, :]),
                 reads=(bGPa,), writes=(bD0a, bD0b))
            p.op("vector", lambda e: e.tensor_mul(DRW[:, 1, :], DRW[:, 1, :], DRW[:, 0, :]),
                 reads=(bD0a, bD0b, bD1), writes=(bD1,))
        else:
            p.op("scalar", lambda e: e.activation(GP[:, 0, :], PS1[:, :], AF.Sqrt,
                                                  bias=CEPS, scale=CK1024),
                 reads=(bP["PS1"], bconst), writes=(bGPa,))
            p.op("vector", lambda e: e.reciprocal(DRW[:, 0, :], GP[:, 0, :]),
                 reads=(bGPa,), writes=(bD0a, bD0b))

    def ln_apply(srcbufs, src32, dstbufs, dst16, grow, srow):
        for c in range(8):
            p.op("vector", lambda e, c=c, src32=src32: e.tensor_mul(
                GP[:, 0, :], src32[:, c, :], DRW[:, 0, :]),
                reads=srcbufs[c] + (bD0a, bD0b), writes=(bGPa,))
            p.op("vector", lambda e: e.tensor_sub(GP[:, 0, :], GP[:, 0, :], DRW[:, 1, :]),
                 reads=(bGPa, bD1), writes=(bGPa,))
            p.op("scalar", lambda e, c=c, dst16=dst16, grow=grow, srow=srow: e.activation(
                dst16[:, c, :], GP[:, 0, :], AF.Identity,
                bias=GAIN[:, srow, c:c + 1], scale=GAIN[:, grow, c:c + 1]),
                reads=(bGPa, bGAIN), writes=dstbufs[c])

    def rms_apply(srcbufs, src, dstbufs, dst16, wten):
        for c in range(8):
            p.op("vector", lambda e, c=c, src=src: e.tensor_mul(
                GP[:, 0, :], src[:, c, :], DRW[:, 0, :]),
                reads=srcbufs[c] + (bD0a, bD0b), writes=(bGPa,))
            p.op("scalar", lambda e, c=c, dst16=dst16, wten=wten: e.activation(
                dst16[:, c, :], GP[:, 0, :], AF.Identity, bias=0.0,
                scale=wten[:, c:c + 1]),
                reads=(bGPa, bconst), writes=dstbufs[c])

    def wstream(slot, dsemi, wname, col0, ncol):
        p.op("gpsimd", lambda e, slot=slot, wname=wname, col0=col0, ncol=ncol: e.dma_start(
            out=ep_slot(slot)[:, :, 0:ncol] if ncol <= 128 else ep_pair(slot // 2)[:, :, 0:ncol],
            in_=dr[wname].ap().rearrange("(c p) m -> p c m", p=128)[:, :, col0:col0 + ncol]),
            writes=(bE[slot],) if ncol <= 128 else (bE[slot], bE[slot + 1]),
            dma_sem=f"DW{dsemi}")

    def proj_group(ps_ap, psbuf, lhs3, rhsbufs, rhs3, extra=()):
        allr = tuple(extra) + tuple({b for t in rhsbufs for b in t})
        for k in range(8):
            last = k == 7
            p.op("tensor", lambda e, last=last, k=k, ps_ap=ps_ap, lhs3=lhs3, rhs3=rhs3: e.matmul(
                ps_ap, lhs3[:, k, :], rhs3[:, k, :], start=(k == 0), stop=last),
                reads=allr if last else (tuple(extra) + rhsbufs[k]),
                writes=(psbuf,), pre_only=not last)

    def rope_pass(ps_ap, psbuf, dst_ap, dstbufs):
        p.op("vector", lambda e, ps_ap=ps_ap: e.tensor_mul(GP[:, 0, :], ps_ap, RPC[:, :]),
             reads=(psbuf, bRPC), writes=(bGPa,))
        for i, (o, s) in enumerate(((0, 32), (32, 0), (64, 96), (96, 64))):
            p.op("vector", lambda e, o=o, s=s, ps_ap=ps_ap: e.tensor_mul(
                GP[o:o + 32, 1, :], ps_ap[s:s + 32, :], RPS[o:o + 32, :]),
                reads=(psbuf, bRPS), writes=(bGPb,), pre_only=(i != 3))
        p.op("vector", lambda e, dst_ap=dst_ap: e.tensor_add(dst_ap, GP[:, 0, :], GP[:, 1, :]),
             reads=(bGPa, bGPb), writes=dstbufs)

    def rope_load(cn, sn, cln, sln, lo, hi):
        p.op("sync", lambda e, cn=cn, lo=lo, hi=hi: e.dma_start(
            out=RPC[:, :], in_=dr[cn].ap()[:, lo:hi]),
            writes=(bRPC,), dma_sem="DRP")
        p.op("sync", lambda e, sn=sn, lo=lo, hi=hi: e.dma_start(
            out=RPS[:, :], in_=dr[sn].ap()[:, lo:hi]),
            writes=(bRPS,), dma_sem="DRS")
        p.op("sync", lambda e, cln=cln, lo=lo, hi=hi: e.dma_start(
            out=GP[:, 0, :], in_=dr[cln].ap()[:, lo:hi]),
            writes=(bGPa,), dma_sem="DRP")
        p.op("sync", lambda e, sln=sln, lo=lo, hi=hi: e.dma_start(
            out=GP[:, 1, :], in_=dr[sln].ap()[:, lo:hi]),
            writes=(bGPb,), dma_sem="DRS")
        p.op("vector", lambda e: e.tensor_add(RPC[:, :], RPC[:, :], GP[:, 0, :]),
             reads=(bGPa, bRPC), writes=(bRPC,))
        p.op("vector", lambda e: e.tensor_add(RPS[:, :], RPS[:, :], GP[:, 1, :]),
             reads=(bGPb, bRPS), writes=(bRPS,))

    for j in range(NT):
        xsrc = dr["x_full"].ap().rearrange("(c p) s -> p c s", p=128)[:, :, T * j:T * (j + 1)]
        p.op("gpsimd", lambda e, xsrc=xsrc: e.dma_start(out=Ylow[:, :, :], in_=xsrc),
             writes=tuple(bYs[0:8]), dma_sem="DX")
        rope_load("ropeC", "ropeS", "ropeCl", "ropeSl", T * j, T * (j + 1))
        ln_stats(YCH, Ylow[:, :, :], wantmean=True, ones_sum=ONESB)
        ln_apply(YCH, Ylow[:, :, :], BCH, B_H[:, :, :], grow=1, srow=0)
        ln_stats(BCH, B_H[:, :, :], wantmean=False)
        rms_apply(BCH, B_H[:, :, :], CCH, C_M[:, :, :], N1W)
        for c2 in range(4):
            j2 = c2 % 2
            p.op("gpsimd", lambda e, c2=c2, j2=j2: e.dma_start(
                out=ep_pair(j2)[:, :, :],
                in_=dr["wkT"].ap().rearrange("(c p) m -> p c m", p=128)[
                    :, :, 256 * c2:256 * (c2 + 1)]),
                writes=(bE[2 * j2], bE[2 * j2 + 1]), dma_sem=f"DW{j2}")
            for ci in range(2):
                c = 2 * c2 + ci
                psn, psv = ("PS2", PS2[:, :]) if c % 2 == 0 else ("PS3", PS3[:, :])
                lhs = ep_pair(j2)[:, :, 128 * ci:128 * (ci + 1)]
                proj_group(psv, bP[psn], lhs, CCH, C_M[:, :, :],
                           extra=(bE[2 * j2], bE[2 * j2 + 1]))
                rope_pass(psv, bP[psn], K[:, c, T * j:T * (j + 1)], (bK,))
        for half in range(2):
            p.op("gpsimd", lambda e, half=half: e.dma_start(
                out=EP[:, :, :],
                in_=dr["wvT"].ap().rearrange("(c p) m -> p c m", p=128)[
                    :, :, 512 * half:512 * (half + 1)]),
                writes=tuple(bE), dma_sem="DM0")
            for s4 in range(4):
                psn = "PS45a" if s4 % 2 == 0 else "PS67a"
                psv = (PS45 if s4 % 2 == 0 else PS67)[:, 0:512]
                for k in range(8):
                    last = k == 7
                    p.op("tensor", lambda e, last=last, k=k, s4=s4, psv=psv: e.matmul(
                        psv, C_M[:, k, 128 * s4:128 * (s4 + 1)], EP[:, k, :],
                        start=(k == 0), stop=last),
                        reads=(CA if last else (bCs[k],)) + tuple(bE),
                        writes=(bP[psn],), pre_only=not last)
                vdst = VH[:, 4 * j + s4, :].rearrange("p (h x) -> p h x", x=65)[
                    :, 8 * half:8 * (half + 1), 0:64]
                p.op("vector", lambda e, psv=psv, vdst=vdst: e.tensor_copy(
                    vdst, psv.rearrange("p (h x) -> p h x", x=64)),
                     reads=(bP[psn],), writes=(bVH,))

    if debug:
        p.op("sync", lambda e: e.dma_start(out=taps["t_K"].ap(), in_=K[:, :, :]),
             reads=(bK,), dma_sem="DO0")
        p.op("sync", lambda e: e.dma_start(out=taps["t_VH"].ap(), in_=VH[:, :, :]),
             reads=(bVH,), dma_sem="DO0")

    unit_base = [sum(PSPAN[:u]) for u in range(4)]

    for u in range(4):
        span = PSPAN[u]
        nkc = 4 * span
        xsrc = dr["x_slots"].ap().rearrange("(c p) s -> p c s", p=128)[:, :, T * u:T * (u + 1)]
        p.op("gpsimd", lambda e, xsrc=xsrc: e.dma_start(out=Ylow[:, :, :], in_=xsrc),
             writes=tuple(bYs[0:8]), dma_sem="DX")
        p.op("sync", lambda e, xsrc=xsrc: e.dma_start(out=A_Z[:, :, :], in_=xsrc),
             writes=AA, dma_sem="DK2")
        rope_load("ropeqC", "ropeqS", "ropeqCl", "ropeqSl", T * u, T * (u + 1))
        ln_stats(YCH, Ylow[:, :, :], wantmean=True, ones_sum=ONESB)
        ln_apply(YCH, Ylow[:, :, :], BCH, B_H[:, :, :], grow=1, srow=0)
        for c in range(8):
            p.op("vector", lambda e, c=c: e.scalar_tensor_tensor(
                A_Z[:, c, :], B_H[:, c, :], GAIN[:, 2, c:c + 1], A_Z[:, c, :],
                op0=ALU.mult, op1=ALU.add),
                reads=(bBs[c], bGAIN, bAs[c]), writes=(bAs[c],))
        if debug and u == 0:
            p.op("sync", lambda e: e.dma_start(out=taps["t_h"].ap(), in_=B_H[:, :, :]),
                 reads=BA, dma_sem="DO0")
        ln_stats(BCH, B_H[:, :, :], wantmean=False)
        rms_apply(BCH, B_H[:, :, :], CCH, C_M[:, :, :], N1W)
        if debug and u == 0:
            p.op("sync", lambda e: e.dma_start(out=taps["t_h1"].ap(), in_=C_M[:, :, :]),
                 reads=CA, dma_sem="DO0")
        for c2 in range(4):
            j2 = c2 % 2
            p.op("gpsimd", lambda e, c2=c2, j2=j2: e.dma_start(
                out=ep_pair(j2)[:, :, :],
                in_=dr["wqT"].ap().rearrange("(c p) m -> p c m", p=128)[
                    :, :, 256 * c2:256 * (c2 + 1)]),
                writes=(bE[2 * j2], bE[2 * j2 + 1]), dma_sem=f"DW{j2}")
            for ci in range(2):
                c = 2 * c2 + ci
                psn, psv = ("PS2", PS2[:, :]) if c % 2 == 0 else ("PS3", PS3[:, :])
                lhs = ep_pair(j2)[:, :, 128 * ci:128 * (ci + 1)]
                proj_group(psv, bP[psn], lhs, CCH, C_M[:, :, :],
                           extra=(bE[2 * j2], bE[2 * j2 + 1]))
                rope_pass(psv, bP[psn], qh3[:, c, :], (bYs[c],))
        if debug and u == 0:
            p.op("sync", lambda e: e.dma_start(out=taps["t_qh"].ap(), in_=qh3[:, :, :]),
                 reads=YA, dma_sem="DO0")
        p.op("gpsimd", lambda e, u=u: e.dma_start(
            out=EP[:, 0:4, :], in_=dr["maskq"].ap()[u, :, 0:4, :]),
            writes=(bE[0], bE[1]), dma_sem="DM0")
        p.op("gpsimd", lambda e, u=u: e.dma_start(
            out=EP[:, 4:8, :], in_=dr["maskq"].ap()[u, :, 4:8, :]),
            writes=(bE[2], bE[3]), dma_sem="DM1")

        M4s = [M4, M4B]

        def emit_bias_chain(h):
            p.op("vector", lambda e: e.reduce_max(RMX[:, :], M4[:, :], axis=AXX),
                 reads=(bM4[0],), writes=(bRMX,))
            p.op("vector", lambda e: e.tensor_scalar(
                MB4[:, 0:1], RMX[:, 0:1], 0.0, 1.0, op0=ALU.max, op1=ALU.add),
                reads=(bRMX,), writes=(bMB4,))
            p.op("tensor", lambda e: e.transpose(
                PS0[0:1, 0:128], MB4[:, 0:1], IDENTF[:, :]),
                reads=(bMB4, bconst), writes=(bP["PS0"],))
            p.op("vector", lambda e: e.reduce_max(MSC[0:1, 0:1], PS0[0:1, 0:128], axis=AXX),
                 reads=(bP["PS0"],), writes=(bMSC,))
            p.op("vector", lambda e: e.tensor_single_scalar(
                MSC[0:1, 1:2], MSC[0:1, 0:1], -0.125, op=ALU.mult),
                reads=(bMSC,), writes=(bMSC,))
            p.op("tensor", lambda e: e.matmul(
                PS1[:, 0:1], ONESF[0:1, 0:128], MSC[0:1, 1:2], start=True, stop=True),
                reads=(bMSC, bconst), writes=(bP["PS1"],))
            p.op("vector", lambda e: e.tensor_copy(MBIAS[:, 0:1], PS1[:, 0:1]),
                 reads=(bP["PS1"],), writes=(bMBIAS,))

        aunit = [0]
        for hp in range(H // 2):
            for qs in range(4):
                for pr in range(span // 2):
                    for par in range(2):
                        h = 2 * hp + par
                        hc = hp
                        r0 = par * 64
                        pspair = PS45 if aunit[0] % 2 == 0 else PS67
                        pn = ("PS45a", "PS45b") if aunit[0] % 2 == 0 else ("PS67a", "PS67b")
                        aunit[0] += 1
                        for hf in range(2):
                            blk = 2 * pr + hf
                            last = hf == 1
                            p.op("tensor", lambda e, last=last, qs=qs, blk=blk, hf=hf,
                                 hc=hc, r0=r0, pspair=pspair: e.matmul(
                                pspair[:, 512 * hf:512 * (hf + 1)],
                                qh3[r0:r0 + 64, hc, 128 * qs:128 * (qs + 1)],
                                K[r0:r0 + 64, hc, 512 * blk:512 * (blk + 1)],
                                start=True, stop=True, tile_position=(r0, 0)),
                                reads=(bYs[hc], bK), writes=(bP[pn[0]], bP[pn[1]]),
                                pre_only=not last)
                        p.op("vector", lambda e, pspair=pspair: e.reduce_max(
                            RMX[:, :], pspair[:, :], axis=AXX),
                             reads=(bP[pn[0]], bP[pn[1]]), writes=(bRMX,))
                        first = hp == 0 and pr == 0 and par == 0
                        if first:
                            p.op("vector", lambda e, qs=qs: e.tensor_copy(
                                M4[:, qs:qs + 1], RMX[:, :]),
                                 reads=(bRMX,), writes=(bM4[0],))
                        else:
                            p.op("vector", lambda e, qs=qs: e.tensor_max(
                                M4[:, qs:qs + 1], M4[:, qs:qs + 1], RMX[:, :]),
                                reads=(bRMX, bM4[0]), writes=(bM4[0],))
        emit_bias_chain(0)
        bring = [("PS67a", PS67[:, 0:512]), ("PS67b", PS67[:, 512:1024]),
                 ("PS45a", PS45[:, 0:512]), ("PS45b", PS45[:, 512:1024])]
        pring = [PB[:, 0, :], PB[:, 1, :], PB[:, 2, :], PB[:, 3, :]]

        def emit_B(h, kc):
            hc = h // 2
            r0 = (h % 2) * 64
            bn, bap = bring[(2 * kc + (h % 2)) % 4]
            p.op("tensor", lambda e, kc=kc, bap=bap, hc=hc, r0=r0: e.matmul(
                bap, K[r0:r0 + 64, hc, 128 * kc:128 * (kc + 1)],
                qh3[r0:r0 + 64, hc, :], start=True, stop=True,
                tile_position=(r0, 0)),
                reads=(bK, bYs[hc]), writes=(bP[bn],))
            pslot = (2 * kc + (h % 2)) % 4
            pT = pring[pslot]
            pbuf = bPB[pslot]
            p.op("scalar", lambda e, bap=bap, pT=pT, h=h: e.activation(
                pT, bap, AF.Exp, bias=MBIAS[:, 0:1], scale=CQ8),
                reads=(bP[bn], bconst, bMBIAS), writes=(pbuf,))
            j8 = kc - (nkc - 8)
            if j8 >= 0:
                p.op("vector", lambda e, pT=pT, j8=j8: e.tensor_mul(
                    pT, pT, EP[:, j8, :]),
                    reads=tuple(bE) + (pbuf,), writes=(pbuf,))

        def emit_PV(h, kc):
            ctxn = "PS2" if h % 2 == 0 else "PS3"
            ctxps = PS2 if h % 2 == 0 else PS3
            pslot = (2 * kc + (h % 2)) % 4
            pT = pring[pslot]
            pbuf = bPB[pslot]
            last = kc == nkc - 1
            p.op("tensor", lambda e, kc=kc, h=h, pT=pT, ctxps=ctxps, nkc=nkc: e.matmul(
                ctxps[0:65, :], VH[:, kc, 65 * h:65 * (h + 1)], pT,
                start=(kc == 0), stop=(kc == nkc - 1)),
                reads=(bVH, pbuf), writes=(bP[ctxn],), pre_only=not last)

        def emit_evac(h):
            hc = h // 2
            ctxn = "PS2" if h % 2 == 0 else "PS3"
            ctxps = PS2 if h % 2 == 0 else PS3
            p.op("scalar", lambda e, ctxps=ctxps: e.copy(RL[64:65, :], ctxps[64:65, :]),
                 reads=(bP[ctxn],), writes=(bRL,))
            rbn = "PS0" if h % 2 == 0 else "PS1"
            rbp = PS0 if h % 2 == 0 else PS1
            p.op("tensor", lambda e, rbp=rbp: e.matmul(
                rbp[0:64, :], ONESF[64:65, 0:64], RL[64:65, :], start=True, stop=True,
                tile_position=(64, 0)),
                reads=(bRL, bconst), writes=(bP[rbn],))
            p.op("vector", lambda e, rbp=rbp: e.reciprocal(DRW[0:64, 1, :], rbp[0:64, :]),
                 reads=(bP[rbn],), writes=(bD1,))
            if h % 2 == 0:
                p.op("vector", lambda e, ctxps=ctxps, hc=hc: e.tensor_mul(
                    C_M[0:64, hc, :], ctxps[0:64, :], DRW[0:64, 1, :]),
                    reads=(bP[ctxn], bD1), writes=(bCs[hc],))
            else:
                p.op("vector", lambda e, ctxps=ctxps: e.tensor_mul(
                    PB[0:64, 3, :], ctxps[0:64, :], DRW[0:64, 1, :]),
                    reads=(bP[ctxn], bD1), writes=(bPB[3],))
                p.op("sync", lambda e, hc=hc: e.dma_start(
                    out=C_M[64:128, hc, :], in_=PB[0:64, 3, :]),
                    reads=(bPB[3],), writes=(bCs[hc],), dma_sem="DK1")

        for hp in range(H // 2):
            h0, h1 = 2 * hp, 2 * hp + 1
            emit_B(h0, 0)
            emit_B(h1, 0)
            for kc in range(nkc):
                if kc == 0 and hp > 0:
                    emit_evac(2 * hp - 2)
                    emit_evac(2 * hp - 1)
                if kc + 1 < nkc:
                    emit_B(h0, kc + 1)
                    emit_B(h1, kc + 1)
                emit_PV(h0, kc)
                emit_PV(h1, kc)
        emit_evac(H - 2)
        emit_evac(H - 1)
        if debug and u == 0:
            p.op("sync", lambda e: e.dma_start(out=taps["t_ctx"].ap(), in_=C_M[:, :, :]),
                 reads=CA, dma_sem="DO0")

        for c2 in range(4):
            j2 = c2 % 2
            p.op("gpsimd", lambda e, c2=c2, j2=j2: e.dma_start(
                out=ep_pair(j2)[:, :, :],
                in_=dr["woT"].ap().rearrange("(c p) m -> p c m", p=128)[
                    :, :, 256 * c2:256 * (c2 + 1)]),
                writes=(bE[2 * j2], bE[2 * j2 + 1]), dma_sem=f"DW{j2}")
            for ci in range(2):
                c = 2 * c2 + ci
                psn = "PS45a" if c % 2 == 0 else "PS45b"
                psv = PS45[:, 0:512] if c % 2 == 0 else PS45[:, 512:1024]
                lhs = ep_pair(j2)[:, :, 128 * ci:128 * (ci + 1)]
                proj_group(psv, bP[psn], lhs, CCH, C_M[:, :, :],
                           extra=(bE[2 * j2], bE[2 * j2 + 1]))
                p.op("vector", lambda e, c=c, psv=psv: e.tensor_add(
                    Ylow[:, c, :], psv, B_H[:, c, :]),
                    reads=(bP[psn], bBs[c]), writes=(bYs[c],))
                p.op("vector", lambda e, c=c, psv=psv: e.scalar_tensor_tensor(
                    A_Z[:, c, :], psv, GAIN[:, 2, c:c + 1], A_Z[:, c, :],
                    op0=ALU.mult, op1=ALU.add),
                    reads=(bP[psn], bGAIN, bAs[c]), writes=(bAs[c],))
        if debug and u == 0:
            p.op("sync", lambda e: e.dma_start(out=taps["t_y"].ap(), in_=Ylow[:, :, :]),
                 reads=YA, dma_sem="DO0")

        ln_stats(YCH, Ylow[:, :, :], wantmean=False)
        rms_apply(YCH, Ylow[:, :, :], CCH, C_M[:, :, :], N2W)
        for halfF in range(2):
            for fc16 in range(FC_MOE):
                fc = 16 * halfF + fc16
                j2 = fc % 2
                if fc16 % 2 == 0:
                    fb = (fc // 2) % 2
                    dst = EP[:, :, :] if fb == 0 else B_H[:, :, :]
                    wbufs = tuple(bE) if fb == 0 else BA
                    p.op("gpsimd", lambda e, fc=fc, dst=dst: e.dma_start(
                        out=dst,
                        in_=dr["w13T"].ap().rearrange("(c p) m -> p c m", p=128)[
                            :, :, 256 * fc:256 * (fc + 2)]),
                        writes=wbufs, dma_sem=f"DW{fb}")
                fb = (fc // 2) % 2
                wt = (EP if fb == 0 else B_H).ap().rearrange(
                    "p a b -> p (a b)").rearrange("p (c n) -> p c n", n=512)
                w13 = wt[:, :, 256 * (fc % 2):256 * (fc % 2) + 256]
                wbufs13 = tuple(bE) if fb == 0 else BA
                pu = "PS45a" if fc % 2 == 0 else "PS67a"
                pg = "PS45b" if fc % 2 == 0 else "PS67b"
                pub = (PS45 if fc % 2 == 0 else PS67)[:, 0:512]
                pgb = (PS45 if fc % 2 == 0 else PS67)[:, 512:1024]
                for k in range(8):
                    last = k == 7
                    p.op("tensor", lambda e, last=last, k=k, w13=w13, pub=pub: e.matmul(
                        pub, w13[:, k, 0:128], C_M[:, k, :], start=(k == 0), stop=last),
                        reads=wbufs13 + (CA if last else (bCs[k],)), writes=(bP[pu],),
                        pre_only=not last)
                for k in range(8):
                    last = k == 7
                    p.op("tensor", lambda e, last=last, k=k, w13=w13, pgb=pgb: e.matmul(
                        pgb, w13[:, k, 128:256], C_M[:, k, :], start=(k == 0), stop=last),
                        reads=wbufs13 + (CA if last else (bCs[k],)), writes=(bP[pg],),
                        pre_only=not last)
                slb = PB[:, 0, :]
                p.op("scalar", lambda e, pub=pub, slb=slb: e.activation(slb, pub, AF.Silu),
                     reads=(bP[pu],), writes=(bPB[0],))
                ffdst = YB[:, fc16 % 8, 512 * (fc16 // 8):512 * (fc16 // 8 + 1)]
                p.op("vector", lambda e, ffdst=ffdst, slb=slb, pgb=pgb: e.tensor_mul(
                    ffdst, slb, pgb),
                    reads=(bPB[0], bP[pg]), writes=(bYs[fc16],))
            for c in range(8):
                if c % 2 == 0:
                    cb2 = (c // 2) % 2
                    dstw = (EP if cb2 == 0 else B_H).ap().rearrange(
                        "p a b -> p (a b)").rearrange("p (f n) -> p f n", n=256)[:, 0:16, :]
                    wbuf2 = tuple(bE) if cb2 == 0 else BA
                    p.op("gpsimd", lambda e, c=c, halfF=halfF, dstw=dstw: e.dma_start(
                        out=dstw,
                        in_=dr["w2T"].ap().rearrange("(f p) m -> p f m", p=128)[
                            :, 16 * halfF:16 * (halfF + 1), 128 * c:128 * (c + 2)]),
                        writes=wbuf2, dma_sem=f"DW{cb2}")
                cb2 = (c // 2) % 2
                wv = (EP if cb2 == 0 else B_H).ap().rearrange(
                    "p a b -> p (a b)").rearrange("p (f n) -> p f n", n=256)[
                    :, 0:16, 128 * (c % 2):128 * (c % 2) + 128]
                wrd = tuple(bE) if cb2 == 0 else BA
                psn = "PS2" if c % 2 == 0 else "PS3"
                psv = (PS2 if c % 2 == 0 else PS3)[:, :]
                for fc16 in range(FC_MOE):
                    last = fc16 == FC_MOE - 1
                    ffsrc = YB[:, fc16 % 8, 512 * (fc16 // 8):512 * (fc16 // 8 + 1)]
                    p.op("tensor", lambda e, last=last, fc16=fc16, wv=wv, psv=psv, ffsrc=ffsrc: e.matmul(
                        psv, wv[:, fc16, :], ffsrc, start=(fc16 == 0), stop=last),
                        reads=wrd + (YA if last else (bYs[fc16],)),
                        writes=(bP[psn],), pre_only=not last)
                p.op("vector", lambda e, c=c, psv=psv: e.scalar_tensor_tensor(
                    A_Z[:, c, :], psv, GAIN[:, 2, c:c + 1], A_Z[:, c, :],
                    op0=ALU.mult, op1=ALU.add),
                    reads=(bP[psn], bGAIN, bAs[c]), writes=(bAs[c],))
        if debug and u == 0:
            p.op("sync", lambda e: e.dma_start(out=taps["t_x1"].ap(), in_=A_Z[:, :, :]),
                 reads=AA, dma_sem="DO0")

        ln_stats(ACH, A_Z[:, :, :], wantmean=True)
        ln_apply(ACH, A_Z[:, :, :], BCH, B_H[:, :, :], grow=4, srow=3)
        if debug and u == 0:
            p.op("sync", lambda e: e.dma_start(out=taps["t_g"].ap(), in_=B_H[:, :, :]),
                 reads=BA, dma_sem="DO0")
        for qs in range(4):
            gout = PS2[:, 8 * qs:8 * (qs + 1)]
            for k in range(8):
                p.op("tensor", lambda e, k=k, qs=qs, gout=gout: e.matmul(
                    gout, B_H[:, k, 128 * qs:128 * (qs + 1)], WGT[:, k, :],
                    start=(k == 0), stop=False),
                    reads=(bBs[k], bWG), writes=(bP["PS2"],), pre_only=True)
            p.op("tensor", lambda e, qs=qs, gout=gout: e.matmul(
                gout, ONESB[0:1, 0:128], BGR[0:1, :], start=False, stop=True),
                reads=tuple(bBs) + (bWG, bconst), writes=(bP["PS2"],))
            p.op("vector", lambda e, gout=gout: e.reduce_max(
                RMX[:, :], gout, axis=AXX, negate=True),
                reads=(bP["PS2"],), writes=(bRMX,))
            p.op("vector", lambda e, gout=gout: e.tensor_scalar(
                GP[:, 0, 0:8], gout, RMX[:, 0:1], 0.0, op0=ALU.add, op1=ALU.is_ge),
                reads=(bP["PS2"], bRMX), writes=(bGPa,))
            p.op("vector", lambda e, gout=gout: e.scalar_tensor_tensor(
                GP[:, 1, 0:8], GP[:, 0, 0:8], -1e30, gout,
                op0=ALU.mult, op1=ALU.add),
                reads=(bGPa, bP["PS2"]), writes=(bGPb,))
            p.op("vector", lambda e: e.reduce_max(
                M4[:, 0:1], GP[:, 1, 0:8], axis=AXX, negate=True),
                reads=(bGPb,), writes=(bM4[0],))
            p.op("scalar", lambda e, gout=gout: e.activation(
                GP[:, 0, 0:8], gout, AF.Exp, bias=RMX[:, 0:1], scale=1.0),
                reads=(bP["PS2"], bRMX), writes=(bGPa,))
            p.op("vector", lambda e: e.tensor_sub(M4[:, 1:2], M4[:, 0:1], RMX[:, 0:1]),
                 reads=(bM4[0], bRMX), writes=(bM4[0],))
            p.op("scalar", lambda e: e.activation(
                M4[:, 2:3], M4[:, 1:2], AF.Exp, bias=0.0, scale=CON[:, 2:3]),
                reads=(bM4[0], bconst), writes=(bM4[0],))
            p.op("vector", lambda e: e.tensor_scalar_add(M4[:, 2:3], M4[:, 2:3], 1.0),
                 reads=(bM4[0],), writes=(bM4[0],))
            p.op("vector", lambda e: e.reciprocal(M4[:, 3:4], M4[:, 2:3]),
                 reads=(bM4[0],), writes=(bM4[0],))
            p.op("vector", lambda e, gout=gout: e.tensor_scalar(
                GP[:, 1, 0:8], gout, M4[:, 0:1], 0.0, op0=ALU.add, op1=ALU.is_ge),
                reads=(bP["PS2"], bM4[0]), writes=(bGPb,))
            p.op("vector", lambda e, qs=qs: e.scalar_tensor_tensor(
                CMB[:, qs, :], GP[:, 0, 0:8], M4[:, 3:4], GP[:, 1, 0:8],
                op0=ALU.mult, op1=ALU.mult),
                reads=(bGPa, bGPb, bM4[0]), writes=(bCMB,))
        if debug and u == 0:
            p.op("sync", lambda e: e.dma_start(out=taps["t_comb"].ap(), in_=CMB[:, :, :]),
                 reads=(bCMB,), dma_sem="DO0")
        for qs in range(4):
            p.op("tensor", lambda e, qs=qs: e.transpose(
                PS3[0:8, 128 * qs:128 * (qs + 1)], CMB[:, qs, :], IDENTF[:, :]),
                reads=(bCMB, bconst), writes=(bP["PS3"],))
        p.op("vector", lambda e: e.tensor_copy(SH8[0:8, 0:512], PS3[0:8, 0:512]),
             reads=(bP["PS3"],), writes=(bSH,))

        for ex in range(NE):
            p.op("sync", lambda e, ex=ex: e.dma_start(
                out=SH8[0:1, 512:1024], in_=SH8[ex:ex + 1, 0:512]),
                reads=(bSH,), writes=(bSH,), dma_sem="DK2")
            cbn = "PS0" if ex % 2 == 0 else "PS1"
            cbp = PS0 if ex % 2 == 0 else PS1
            p.op("tensor", lambda e, cbp=cbp: e.matmul(
                cbp[:, :], ONESB[0:1, :], SH8[0:1, 512:1024], start=True, stop=True),
                reads=(bSH, bconst), writes=(bP[cbn],))
            for fc in range(FC_MOE):
                if fc % 2 == 0:
                    fb = (fc // 2) % 2
                    dstw = EP[:, :, :] if fb == 0 else C_M[:, :, :]
                    wbufs = tuple(bE) if fb == 0 else CA
                    p.op("gpsimd", lambda e, ex=ex, fc=fc, dstw=dstw: e.dma_start(
                        out=dstw,
                        in_=dr["wupC"].ap()[ex].rearrange("(c p) m -> p c m", p=128)[
                            :, :, 256 * fc:256 * (fc + 2)]),
                        writes=wbufs, dma_sem=f"DW{fb}")
                fb = (fc // 2) % 2
                wtm = (EP if fb == 0 else C_M).ap().rearrange(
                    "p a b -> p (a b)").rearrange("p (c n) -> p c n", n=512)
                wup = wtm[:, :, 256 * (fc % 2):256 * (fc % 2) + 256]
                wbufsu = tuple(bE) if fb == 0 else CA
                pu = "PS45a" if fc % 2 == 0 else "PS67a"
                pg = "PS45b" if fc % 2 == 0 else "PS67b"
                pub = (PS45 if fc % 2 == 0 else PS67)[:, 0:512]
                pgb = (PS45 if fc % 2 == 0 else PS67)[:, 512:1024]
                for k in range(8):
                    last = k == 7
                    p.op("tensor", lambda e, last=last, k=k, wup=wup, pub=pub: e.matmul(
                        pub, wup[:, k, 0:128], B_H[:, k, :], start=(k == 0), stop=last),
                        reads=wbufsu + (BA if last else (bBs[k],)), writes=(bP[pu],),
                        pre_only=not last)
                for k in range(8):
                    last = k == 7
                    p.op("tensor", lambda e, last=last, k=k, wup=wup, pgb=pgb: e.matmul(
                        pgb, wup[:, k, 128:256], B_H[:, k, :], start=(k == 0), stop=last),
                        reads=wbufsu + (BA if last else (bBs[k],)), writes=(bP[pg],),
                        pre_only=not last)
                slb = PB[:, 0, :]
                p.op("scalar", lambda e, pgb=pgb, slb=slb: e.activation(slb, pgb, AF.Silu),
                     reads=(bP[pg],), writes=(bPB[0],))
                p.op("vector", lambda e, slb=slb, pub=pub: e.tensor_mul(
                    GP[:, 1, :], slb, pub),
                    reads=(bPB[0], bP[pu]), writes=(bGPb,))
                hhdst = YB[:, fc % 8, 512 * (fc // 8):512 * (fc // 8 + 1)]
                p.op("vector", lambda e, hhdst=hhdst, cbp=cbp: e.tensor_mul(
                    hhdst, GP[:, 1, :], cbp[:, :]),
                    reads=(bGPb, bP[cbn]), writes=(bYs[fc],))
            for c in range(8):
                if c % 2 == 0:
                    cb2 = (c // 2) % 2
                    dstw = (EP if cb2 == 0 else C_M).ap().rearrange(
                        "p a b -> p (a b)").rearrange("p (f n) -> p f n", n=256)
                    wbufsd = tuple(bE) if cb2 == 0 else CA
                    p.op("gpsimd", lambda e, ex=ex, c=c, dstw=dstw: e.dma_start(
                        out=dstw,
                        in_=dr["wdn"].ap()[ex].rearrange("(f p) m -> p f m", p=128)[
                            :, :, 128 * c:128 * (c + 2)]),
                        writes=wbufsd, dma_sem=f"DM{cb2}")
                cb2 = (c // 2) % 2
                wv = (EP if cb2 == 0 else C_M).ap().rearrange(
                    "p a b -> p (a b)").rearrange("p (f n) -> p f n", n=256)[
                    :, :, 128 * (c % 2):128 * (c % 2) + 128]
                wrd = tuple(bE) if cb2 == 0 else CA
                psn = "PS2" if c % 2 == 0 else "PS3"
                psv = (PS2 if c % 2 == 0 else PS3)[:, :]
                for fc in range(FC_MOE):
                    last = fc == FC_MOE - 1
                    hhsrc = YB[:, fc % 8, 512 * (fc // 8):512 * (fc // 8 + 1)]
                    p.op("tensor", lambda e, last=last, fc=fc, wv=wv, psv=psv, hhsrc=hhsrc: e.matmul(
                        psv, wv[:, fc, :], hhsrc, start=(fc == 0), stop=last),
                        reads=wrd + (YA if last else (bYs[fc],)),
                        writes=(bP[psn],), pre_only=not last)
                p.op("vector", lambda e, c=c, psv=psv: e.scalar_tensor_tensor(
                    A_Z[:, c, :], psv, GAIN[:, 5, c:c + 1], A_Z[:, c, :],
                    op0=ALU.mult, op1=ALU.add),
                    reads=(bP[psn], bGAIN, bAs[c]), writes=(bAs[c],))

        dsem = "DO0" if u % 2 == 0 else "DO1"
        p.op("sync", lambda e, u=u: e.dma_start(out=yout.ap()[u], in_=A_Z[:, :, :]),
             reads=AA, dma_sem=dsem)

    totals = [(n, p.cnt[n]) for n in ("DO0", "DO1") if p.cnt[n] > 0]
    def tail(e, totals=totals, hs={n: p.handles[n] for n, _ in totals}):
        for n, v in totals:
            e.wait_ge(hs[n], v)
    p.raw("sync", tail)

    with nc.Block() as block:
        for eng in ENGINES:
            def runner(e, lst=p.ops[eng]):
                for f in lst:
                    f(e)
            getattr(block, eng)(runner)
    p.close_sems()
    return nc


def _rope_tables():
    pos = np.arange(S, dtype=np.float64)
    kk = np.arange(64) // 2
    ang = pos[None, :] / np.power(10000.0, 2.0 * kk.astype(np.float64) / 64.0)[:, None]
    cos = np.cos(ang)
    sinr = np.sin(ang)
    CT = np.empty((64, S))
    ST = np.empty((64, S))
    for n in range(64):
        m = n % 32
        CT[n] = cos[2 * m]
        ST[n] = sinr[2 * m] * (-1.0 if n < 32 else 1.0)
    CT2d = np.concatenate([CT, CT], axis=0)
    ST2d = np.concatenate([ST, ST], axis=0)
    CT2 = CT2d.astype(np.float32)
    ST2 = ST2d.astype(np.float32)
    CT2l = (CT2d - CT2.astype(np.float64)).astype(np.float32)
    ST2l = (ST2d - ST2.astype(np.float64)).astype(np.float32)
    return (np.ascontiguousarray(CT2), np.ascontiguousarray(ST2),
            np.ascontiguousarray(CT2l), np.ascontiguousarray(ST2l))


def _deinterleave_perm():
    perm = np.empty(D, dtype=np.int64)
    for h in range(H):
        base = 64 * h
        perm[base:base + 32] = base + 2 * np.arange(32)
        perm[base + 32:base + 64] = base + 2 * np.arange(32) + 1
    return perm


def _fm(v):
    return np.ascontiguousarray(v.reshape(8, 128).T.astype(np.float32))


def stage_inputs(inputs):
    x = np.asarray(inputs["L0_attn_x"], np.float32)
    perm = _deinterleave_perm()
    CT2, ST2, CT2l, ST2l = _rope_tables()
    wq = np.asarray(inputs["L0_attn_wq"], np.float32)[perm, :]
    wk = np.asarray(inputs["L0_attn_wk"], np.float32)[perm, :]
    wqT = np.ascontiguousarray(wq.T)
    wkT = np.ascontiguousarray(wk.T)
    wvT = np.ascontiguousarray(np.asarray(inputs["L0_attn_wv"], np.float32).T)
    woT = np.ascontiguousarray(np.asarray(inputs["L0_attn_wo"], np.float32).T)
    w1T = np.asarray(inputs["L0_attn_w1"], np.float32).T
    w3T = np.asarray(inputs["L0_attn_w3"], np.float32).T
    w13 = np.empty((D, FC_FFN * 256), np.float32)
    for fc in range(FC_FFN):
        w13[:, 256 * fc:256 * fc + 128] = w1T[:, 128 * fc:128 * (fc + 1)]
        w13[:, 256 * fc + 128:256 * (fc + 1)] = w3T[:, 128 * fc:128 * (fc + 1)]
    w2T = np.ascontiguousarray(np.asarray(inputs["L0_attn_w2"], np.float32).T)
    wup = np.asarray(inputs["L0_ffn_Wup"], np.float32)
    wupC = np.empty((NE, D, FC_MOE * 256), np.float32)
    for fc in range(FC_MOE):
        wupC[:, :, 256 * fc:256 * fc + 128] = wup[:, :, 128 * fc:128 * (fc + 1)]
        wupC[:, :, 256 * fc + 128:256 * (fc + 1)] = wup[:, :, 2048 + 128 * fc:2048 + 128 * (fc + 1)]
    wdn = np.ascontiguousarray(np.asarray(inputs["L0_ffn_Wdown"], np.float32))
    wmod = np.ascontiguousarray(np.asarray(inputs["L0_wmod"], np.float32))
    wgT = np.ascontiguousarray(np.asarray(inputs["L0_ffn_Wg"], np.float32).T)
    bg_row = np.asarray(inputs["L0_ffn_bg"], np.float32).reshape(1, 8)
    n1w = _fm(np.asarray(inputs["L0_attn_norm1_w"], np.float32))
    n2w = _fm(np.asarray(inputs["L0_attn_norm2_w"], np.float32))
    consts = np.broadcast_to(np.array(
        [1e-5, 1.0 / 1024.0, -1.0, 0.125, 0.25, 1.0, -1e30, 0.0], np.float32), (128, 8)).copy()
    cr1 = np.array([[-1.0, 0, 0, 0, 0, 0, 0, 0]], np.float32)
    onesf = np.ones((128, 128), np.float32)
    identf = np.eye(128, dtype=np.float32)

    def maskq_for(par):
        m = np.ones((4, 128, 8, T), np.float32)
        for u in range(4):
            span = PSPAN[u]
            tspan = OWN[par][u] + 1
            nkc = 4 * span
            for j8 in range(8):
                kc = nkc - 8 + j8
                blk = kc // 4
                if blk > tspan - 1:
                    m[u, :, j8, :] = 0.0
                    continue
                if blk < tspan - 1:
                    continue
                off = (kc % 4) * 128
                kpos = off + np.arange(128)[:, None]
                qpos = np.arange(T)[None, :]
                m[u, :, j8, :] = (kpos <= qpos).astype(np.float32)
        return m

    maskq = [maskq_for(0), maskq_for(1)]

    in_maps = []
    for core in range(8):
        b = core // 2
        par = core % 2
        xT = np.ascontiguousarray(x[b].T)
        xs = np.concatenate([xT[:, T * t:T * (t + 1)] for t in OWN[par]], axis=1)
        rqC = np.concatenate([CT2[:, T * t:T * (t + 1)] for t in OWN[par]], axis=1)
        rqS = np.concatenate([ST2[:, T * t:T * (t + 1)] for t in OWN[par]], axis=1)
        rqCl = np.concatenate([CT2l[:, T * t:T * (t + 1)] for t in OWN[par]], axis=1)
        rqSl = np.concatenate([ST2l[:, T * t:T * (t + 1)] for t in OWN[par]], axis=1)
        in_maps.append({
            "x_full": xT, "x_slots": np.ascontiguousarray(xs),
            "ropeC": CT2, "ropeS": ST2, "ropeCl": CT2l, "ropeSl": ST2l,
            "ropeqC": np.ascontiguousarray(rqC), "ropeqS": np.ascontiguousarray(rqS),
            "ropeqCl": np.ascontiguousarray(rqCl), "ropeqSl": np.ascontiguousarray(rqSl),
            "wqT": wqT, "wkT": wkT, "wvT": wvT, "woT": woT,
            "w13T": w13, "w2T": w2T, "wupC": wupC, "wdn": wdn,
            "wmod": wmod, "wgT": wgT, "bg_row": bg_row,
            "c_fm": _fm(np.asarray(inputs["L0_c"], np.float32)[b]),
            "n1w": n1w, "n2w": n2w, "consts": consts, "cr1": cr1,
            "onesf": onesf, "identf": identf, "maskq": maskq[par],
        })
    return in_maps


def gather_output(results):
    out = np.empty((NB, S, D), np.float32)
    for core in range(8):
        b = core // 2
        par = core % 2
        y = results[core]["yout"]
        for u, t in enumerate(OWN[par]):
            out[b, T * t:T * (t + 1), :] = y[u].transpose(2, 1, 0).reshape(T, D)
    return out


def kernel(**inputs):
    in_maps = stage_inputs(inputs)
    nc = build_nc(debug=DEBUG)
    res = run_bass_kernel_spmd(nc, in_maps, core_ids=list(range(8)))
    return gather_output(res.results)
